# version 07
# baseline (speedup 1.0000x reference)
import numpy as np
import ml_dtypes
import concourse.bass as bass
import concourse.mybir as mybir
from concourse.bass_utils import run_bass_kernel_spmd

F32 = mybir.dt.float32
BF16 = mybir.dt.bfloat16
AF = mybir.ActivationFunctionType
ALU = mybir.AluOpType
AX = mybir.AxisListType

DEBUG = False
STAGES = (1, 1, 1, 1)

T = 2048
D = 4096
NH, HP, SN, NG = 64, 64, 128, 8
I_SSM = NH * HP
GN = NG * SN
CONV = I_SSM + 2 * GN
PROJ = I_SSM + CONV + NH
Q = 128
NCH = T // Q
INT, LAT, NE = 8192, 2048, 8
EPS = 1e-5

ENGINES = ("pe", "act", "dve", "pool", "sp")


class Em:

    def __init__(self):
        self.ops = []
        self.tok = {}
        self.fence_ids = set()
        self.dma_counts = {"qsp": 0, "qact": 0}

    def op(self, eng, fn, reads=(), writes=(), dma_q=None, tag="", force=False):
        oid = len(self.ops)
        deps = set()
        for t in reads:
            st = self.tok.get(t)
            if st and st["w"] is not None:
                deps.add(st["w"])
        for t in writes:
            st = self.tok.get(t)
            if st:
                if st["w"] is not None:
                    deps.add(st["w"])
                deps.update(st["r"])
        deps.update(self.fence_ids)
        deps.discard(oid)
        if deps:
            best = {}
            for d in deps:
                p = self.ops[d]
                key = ("q", p["dma_q"]) if p["dma_q"] is not None else ("e", p["eng"])
                if key not in best or d > best[key]:
                    best[key] = d
            deps = set(best.values())
        dma_idx = None
        if dma_q is not None:
            self.dma_counts[dma_q] += 1
            dma_idx = self.dma_counts[dma_q]
        self.ops.append({"id": oid, "eng": eng, "fn": fn, "deps": deps,
                         "dma_q": dma_q, "dma_idx": dma_idx, "tag": tag,
                         "force": force})
        for t in reads:
            self.tok.setdefault(t, {"w": None, "r": []})["r"].append(oid)
        for t in writes:
            self.tok[t] = {"w": oid, "r": []}
        return oid

    def dma(self, dst, src, q="qsp", reads=(), writes=(), tag="", **kw):
        eng = "sp" if q == "qsp" else "act"
        return self.op(eng, lambda e, d=dst, s=src: e.dma_start(d, s, **kw),
                       reads=reads, writes=writes, dma_q=q, tag=tag)

    def fence(self):
        last_eng = {}
        last_dma = {}
        for o in self.ops:
            if o["dma_q"] is not None:
                last_dma[o["dma_q"]] = o["id"]
            else:
                last_eng[o["eng"]] = o["id"]
        self.fence_ids = set(last_eng.values()) | set(last_dma.values())

    @staticmethod
    def _elide(p, c):
        if c["dma_q"] is not None or p.get("force"):
            return False
        return p["dma_q"] is None and p["eng"] == c["eng"]

    def emit(self, nc):
        ops = self.ops
        needed = set()
        for o in ops:
            for d in o["deps"]:
                if not self._elide(ops[d], o):
                    needed.add(d)
        ms = {e: 0 for e in ENGINES}
        for o in ops:
            if o["dma_q"] is not None:
                o["ms"] = (o["dma_q"], 16 * o["dma_idx"])
            elif o["id"] in needed:
                ms[o["eng"]] += 1
                o["ms"] = (o["eng"], ms[o["eng"]])
            else:
                o["ms"] = None
        clocks = {e: {} for e in ENGINES}
        per_eng = {e: [] for e in ENGINES}
        for o in ops:
            c = clocks[o["eng"]]
            waits = []
            for d in sorted(o["deps"]):
                p = ops[d]
                if self._elide(p, o):
                    continue
                sem, val = p["ms"]
                if c.get(sem, 0) < val:
                    waits.append((sem, val))
                    c[sem] = val
            o["waits"] = waits
            per_eng[o["eng"]].append(o)

        sems = {}
        hs = []
        for name in ("pe", "act", "dve", "pool", "qsp", "qact"):
            h = nc.semaphore("sem_" + name)
            sems[name] = h.__enter__()
            hs.append(h)

        def run(engine_obj, eng_name):
            for o in per_eng[eng_name]:
                for (sem, val) in o["waits"]:
                    engine_obj.wait_ge(sems[sem], val)
                inst = o["fn"](engine_obj)
                if o["dma_q"] is not None:
                    inst.then_inc(sems[o["dma_q"]], 16)
                elif o["id"] in needed:
                    inst.then_inc(sems[o["eng"]], 1)

        with nc.Block() as blk:
            @blk.tensor
            def _(e):
                run(e, "pe")

            @blk.scalar
            def _(e):
                run(e, "act")

            @blk.vector
            def _(e):
                run(e, "dve")

            @blk.gpsimd
            def _(e):
                run(e, "pool")

            @blk.sync
            def _(e):
                run(e, "sp")
        for h in reversed(hs):
            h.__exit__(None, None, None)


def strided(base_ap, pattern):
    return bass.AP(base_ap.tensor, base_ap.offset, pattern)


def build(nc):
    em = Em()

    def din(name, shape, dt=F32):
        return nc.dram_tensor(name, shape, dt, kind="ExternalInput").ap()

    def dscr(name, shape, dt=F32):
        kind = "ExternalOutput" if DEBUG else "Internal"
        return nc.dram_tensor(name, shape, dt, kind=kind).ap()

    xT_in = din("xT", [D, T])
    x_in = din("x", [T, D])
    winT = din("winT", [D, PROJ])
    woutT = din("woutT", [I_SSM, D])
    w_up = din("w_up", [D, INT])
    w_down = din("w_down", [INT, D])
    w_fc1 = din("w_fc1", [D, LAT])
    w_fc2 = din("w_fc2", [LAT, D])
    w_exp = din("w_exp", [LAT, LAT])
    w_gate = din("w_gate", [D, NE])
    convw = din("convw", [128, 48, 4])
    convb = din("convb", [128, 48])
    dtbias = din("dtbias", [NH, 1])
    alog = din("alog", [NH, 1])
    dvec = din("dvec", [NH])
    nw_in = din("nw", [128, 32])
    gam_in = din("gam", [D])
    bet_in = din("bet", [D])
    out = nc.dram_tensor("out", [T, D], F32, kind="ExternalOutput").ap()

    gate_scr = dscr("gate_scr", [32, 128, T], BF16)
    xbc_scr = dscr("xbc_scr", [48, 128, T], BF16)
    ynT_scr = dscr("ynT_scr", [32, 128, T], BF16)
    attn_scr = dscr("attn_scr", [NCH, 128, D])
    z_scr = dscr("z_scr", [NCH, 128, D])
    zT_scr = dscr("zT_scr", [32, 128, T], BF16)
    s_scr = dscr("s_scr", [NH, Q])
    r_scr = dscr("r_scr", [1, T])
    xlat_scr = dscr("xlat_scr", [16, 128, T], BF16)
    h1_scr = dscr("h1_scr", [64, 128, T], BF16)
    wdb_scr = dscr("wdb_scr", [64, 128, D], BF16)
    routed_scr = dscr("routed_scr", [NCH, 128, D])
    acc_scr = dscr("acc_scr", [NCH, 128, D])
    if DEBUG:
        dt_dbg = dscr("dt_dbg", [NH, T])
        dbg_dtsT = dscr("dbg_dtsT", [128, 128])
        dbg_lam = dscr("dbg_lam", [128, NH * Q], BF16)
        dbg_xnat = dscr("dbg_xnat", [128, I_SSM], BF16)
        dbg_xdt = dscr("dbg_xdt", [128, 512], BF16)
        dbg_gam = dscr("dbg_gam", [128, 8 * Q], BF16)
        dbg_yg = dscr("dbg_yg", [128, 512])
        dbg_y1 = dscr("dbg_y1", [128, 512])
        dbg_sb = dscr("dbg_sb", [128, I_SSM], BF16)
        dbg_yn16 = dscr("dbg_yn16", [128, 512], BF16)
        dbg_gcol = dscr("dbg_gcol", [128, 4])
        dbg_ynTb = dscr("dbg_ynTb", [128, 32 * 256], BF16)

    persist = []

    def sbp(name, shape, dt=F32):
        h = nc.sbuf_tensor(name, shape, dt)
        t = h.__enter__()
        persist.append(h)
        return t

    scope_hs = []

    def sbs(name, shape, dt=F32):
        h = nc.sbuf_tensor(name, shape, dt)
        t = h.__enter__()
        scope_hs.append(h)
        return t

    def pss(name, shape, dt=F32):
        h = nc.psum_tensor(name, shape, dt)
        t = h.__enter__()
        scope_hs.append(h)
        return t

    def close_scope():
        em.fence()
        for h in reversed(scope_hs):
            h.__exit__(None, None, None)
        scope_hs.clear()

    dt_sb = sbp("dt_sb", [NH, T])
    ident = sbp("ident", [128, 128], BF16)
    identf = sbp("identf", [128, 128], F32)
    convw_sb = sbp("convw_sb", [128, 48 * 4])
    convb_sb = sbp("convb_sb", [128, 48])
    small = sbp("small", [NH, 8])
    dbcp = sbp("dbcp", [128, NH])
    cst = sbp("cst", [128, 4])

    em.op("dve", lambda e: e.memset(cst[:, 0:1], EPS), writes=["cst"])
    em.op("dve", lambda e: e.memset(cst[:, 1:2], 1.0 / 512), writes=["cst"])
    em.op("dve", lambda e: e.memset(cst[:, 2:3], -0.5), writes=["cst"])
    em.op("pool", lambda e: e.memset(ident[:, :], 1.0), writes=["ident"])
    em.op("pool", lambda e: e.affine_select(ident[:, :], ident[:, :], [[1, 128]],
                                            ALU.is_equal, 0.0, channel_multiplier=-1),
          reads=["ident"], writes=["ident"])
    em.op("pool", lambda e: e.memset(identf[:, :], 1.0), writes=["identf"])
    em.op("pool", lambda e: e.affine_select(identf[:, :], identf[:, :], [[1, 128]],
                                            ALU.is_equal, 0.0, channel_multiplier=-1),
          reads=["identf"], writes=["identf"])
    em.dma(convw_sb[:, :], convw.rearrange("p a k -> p (a k)"), writes=["convw"])
    em.dma(convb_sb[:, :], convb, writes=["convb"])
    em.dma(small[:, 0:1], dtbias, writes=["small0"])
    em.dma(small[:, 1:2], alog, writes=["small1"])
    em.dma(dbcp[:, :], strided(dvec, [[0, 128], [1, NH]]), writes=["dbcp"])
    em.op("act", lambda e: e.activation(small[:, 2:3], small[:, 1:2], AF.Exp),
          reads=["small1"], writes=["smallA"])
    em.op("dve", lambda e: e.tensor_scalar_mul(small[:, 2:3], small[:, 2:3], -1.0),
          reads=["smallA"], writes=["smallA"])

    if STAGES[0]:
        xTb = sbs("xTb", [128, 32 * T], BF16)
        wstg = [sbs("wstg0", [128, 32 * 128])]
        wb16 = [sbs(f"wb16{i}", [128, 32 * 128], BF16) for i in range(2)]
        hpad = sbs("hpad", [128, 3 + T])
        accA = sbs("accA", [128, T])
        accB = sbs("accB", [128, T])
        xstg = [accA, accB]
        xout = [sbs("xout0", [128, T], BF16)]
        pbank = [pss(f"pbA{i}", [128, 512]) for i in range(8)]

        em.op("dve", lambda e: e.memset(hpad[:, 0:3], 0.0), writes=["hpad"])

        XSTOK = ["accA", "accB"]
        for dt_i in range(32):
            b = dt_i % 2
            em.dma(xstg[b][:, :], xT_in[dt_i * 128:(dt_i + 1) * 128, :],
                   writes=[XSTOK[b]], tag=f"ld_xT{dt_i}")
            em.op("act", lambda e, b=b, dt_i=dt_i: e.copy(
                xTb[:, dt_i * T:(dt_i + 1) * T], xstg[b][:, :]),
                reads=[XSTOK[b]], writes=[f"xTb{dt_i}"], tag=f"cv_xT{dt_i}")

        CT_ORDER = list(range(32, 81)) + list(range(32))

        def rec_w(cj):
            ctj = CT_ORDER[cj]
            cwj = 64 if ctj == 80 else 128
            bj = cj % 2
            em.dma(wstg[0][:, 0:32 * cwj].rearrange("p (a c) -> p a c", a=32),
                   winT[:, ctj * 128:ctj * 128 + cwj].rearrange("(a p) c -> p a c", p=128),
                   writes=["wstg0"], tag=f"ld_w{ctj}")
            em.op("act", lambda e, bj=bj, cwj=cwj: e.copy(
                wb16[bj][:, 0:32 * cwj], wstg[0][:, 0:32 * cwj]),
                reads=["wstg0"], writes=[f"wb16{bj}"], tag=f"cv_w{ctj}")

        for ci, ct in enumerate(CT_ORDER):
            cw = 64 if ct == 80 else 128
            b = ci % 2
            bank0 = 4 * (ci % 2)
            if ci == 0:
                rec_w(0)
            if ci + 1 < len(CT_ORDER):
                rec_w(ci + 1)
            for t in range(4):
                bank = pbank[bank0 + t]
                btok = f"pbA{bank0 + t}"
                for d in range(32):
                    first, last = (d == 0), (d == 31)
                    oap = bank[:, :] if cw == 128 else bank[0:64, :]
                    em.op("pe", lambda e, b=b, cw=cw, t=t, d=d, oap=oap:
                          e.matmul(oap, wb16[b][:, d * cw:(d + 1) * cw],
                                   xTb[:, d * T + t * 512:d * T + (t + 1) * 512],
                                   start=(d == 0), stop=(d == 31)),
                          reads=[f"wb16{b}", f"xTb{d}"],
                          writes=[btok] if (first or last) else [],
                          tag=f"mmA{ct}.{t}.{d}")
            if ct < 32:
                xo = xout[0]
                for t in range(4):
                    btok = f"pbA{bank0 + t}"
                    em.op("act", lambda e, t=t, bank=pbank[bank0 + t], xo=xo:
                          e.activation(xo[:, t * 512:(t + 1) * 512], bank[:, :], AF.Silu),
                          reads=[btok], writes=["xout0", btok], tag=f"siluG{ct}.{t}")
                em.dma(gate_scr[ct], xo[:, :], q="qact",
                       reads=["xout0"], tag=f"spG{ct}")
            elif ct < 80:
                j = ct - 32
                for t in range(4):
                    btok = f"pbA{bank0 + t}"
                    em.op("dve", lambda e, t=t, bank=pbank[bank0 + t]:
                          e.tensor_copy(hpad[:, 3 + t * 512:3 + (t + 1) * 512], bank[:, :]),
                          reads=[btok], writes=["hpad", btok], tag=f"evH{ct}.{t}")
                em.op("dve", lambda e, j=j: e.tensor_scalar(
                    accA[:, :], hpad[:, 3:3 + T], convw_sb[:, j * 4 + 3:j * 4 + 4],
                    None, ALU.mult),
                    reads=["hpad", "convw"], writes=["accA"], tag=f"cv3.{j}")
                em.op("dve", lambda e, j=j: e.scalar_tensor_tensor(
                    accB[:, :], hpad[:, 2:2 + T], convw_sb[:, j * 4 + 2:j * 4 + 3],
                    accA[:, :], ALU.mult, ALU.add),
                    reads=["hpad", "convw", "accA"], writes=["accB"], tag=f"cv2.{j}")
                em.op("dve", lambda e, j=j: e.scalar_tensor_tensor(
                    accA[:, :], hpad[:, 1:1 + T], convw_sb[:, j * 4 + 1:j * 4 + 2],
                    accB[:, :], ALU.mult, ALU.add),
                    reads=["hpad", "convw", "accB"], writes=["accA"], tag=f"cv1.{j}")
                em.op("dve", lambda e, j=j: e.scalar_tensor_tensor(
                    accB[:, :], hpad[:, 0:0 + T], convw_sb[:, j * 4 + 0:j * 4 + 1],
                    accA[:, :], ALU.mult, ALU.add),
                    reads=["hpad", "convw", "accA"], writes=["accB"], tag=f"cv0.{j}")
                xo = xout[0]
                em.op("act", lambda e, j=j, xo=xo: e.activation(
                    xo[:, :], accB[:, :], AF.Silu, bias=convb_sb[:, j:j + 1]),
                    reads=["accB", "convb"], writes=["xout0"], tag=f"siluX{j}")
                em.dma(xbc_scr[j], xo[:, :], q="qact",
                       reads=["xout0"], tag=f"spX{j}")
            else:
                for t in range(4):
                    btok = f"pbA{bank0 + t}"
                    em.op("dve", lambda e, t=t, bank=pbank[bank0 + t]:
                          e.tensor_copy(dt_sb[:, t * 512:(t + 1) * 512], bank[0:64, :]),
                          reads=[btok], writes=["dt_sb", btok], tag=f"evDT{t}")
        if DEBUG:
            em.dma(dt_dbg, dt_sb[:, :], q="qact", reads=["dt_sb"])
        close_scope()

    rbc = sbp("rbc", [128, T])
    route_cols = sbp("route_cols", [128, NCH])

    if STAGES[1]:
        xbcC = [sbs("xbcC0", [128, 48 * 256], BF16)]
        gateC = [sbs("gateC0", [128, 32 * 256], BF16)]
        S_f32 = sbs("S_f32", [128, I_SSM])
        S_b16 = sbs("S_b16", [128, I_SSM], BF16)
        bcast = sbs("bcast", [128, NH * Q])
        tb = bcast[0:64, 0:T]
        tb2 = bcast[0:64, T:2 * T]
        dtp = bcast[0:64, 2 * T:3 * T]
        lamT = sbs("lamT", [128, NH * Q], BF16)
        LcumF = sbs("LcumF", [128, 128])
        Abc = sbs("Abc", [128, NH])
        onesF = sbs("onesF", [1, 128])
        dtpT_all = sbs("dtpT_all", [128, NCH * NH])
        aT_all = sbs("aT_all", [128, NCH * NH])
        sT_all = sbs("sT_all", [128, NCH * NH])
        s_q = sbs("s_q", [NH, 512])
        slast_row = sbs("slast_row", [1, NCH * NH])
        edecSarg = sbs("edecSarg", [128, NCH * NH])
        edecS_all = sbs("edecS_all", [128, NCH * NH])
        expS_all = aT_all
        X_nat = sbs("X_nat", [128, I_SSM], BF16)
        xdt_g = [sbs(f"xdt_g{i}", [128, 512], BF16) for i in range(2)]
        xd2_g = [sbs(f"xd2_g{i}", [128, 512], BF16) for i in range(2)]
        gam_g = [sbs(f"gam_g{i}", [128, 8 * Q], BF16) for i in range(2)]
        bn_g = [sbs(f"bn_g{i}", [128, 128], BF16) for i in range(2)]
        ytmp = sbs("ytmp", [128, 512])
        xdterm = [sbs(f"xdterm{i}", [128, 512]) for i in range(2)]
        yg = [sbs(f"yg{i}", [128, 512]) for i in range(2)]
        gcol = sbs("gcol", [128, 4])
        yn16 = [sbs(f"yn16{i}", [128, 512], BF16) for i in range(2)]
        stmp = [sbs(f"stmpB{i}", [128, 512]) for i in range(2)]
        ynTb = sbs("ynTb", [128, 32 * 256], BF16)
        trashB = sbs("trashB", [128, 512])
        p_cb = [pss("p_cb0", [128, 512])]
        p_y1 = [pss(f"p_y1{i}", [128, 512]) for i in range(2)]
        p_y2 = [pss(f"p_y2{i}", [128, 512]) for i in range(2)]
        p_ds = pss("p_ds", [128, 512])
        p_tr = [pss(f"p_tr{i}", [128, 1024], BF16) for i in range(2)]
        p_st = p_y1[0]

        a_dram = dscr("a_dram", [NH, 1])
        s_scr_all = dscr("s_scr_all", [NCH, NH, Q])

        em.op("pool", lambda e: e.memset(LcumF[:, :], 1.0), writes=["LcumF"])
        em.op("pool", lambda e: e.affine_select(LcumF[:, :], LcumF[:, :], [[1, 128]],
                                                ALU.is_ge, 0.0, channel_multiplier=-1),
              reads=["LcumF"], writes=["LcumF"])
        em.op("dve", lambda e: e.memset(onesF[:, :], 1.0), writes=["onesF"])

        em.op("dve", lambda e: e.tensor_scalar(tb, dt_sb[:, :],
                                               small[:, 0:1], None, ALU.add),
              reads=["dt_sb", "small0", "bcast"], writes=["tbB", "bcast"], tag="sp_add")
        em.op("act", lambda e: e.activation(tb2, tb, AF.Abs),
              reads=["tbB", "bcast"], writes=["tb2B", "bcast"], tag="sp_abs")
        em.op("act", lambda e: e.activation(tb2, tb2, AF.Exp, scale=-1.0),
              reads=["tb2B"], writes=["tb2B"], tag="sp_exp")
        em.op("act", lambda e: e.activation(tb2, tb2, AF.Ln, bias=1.0),
              reads=["tb2B"], writes=["tb2B"], tag="sp_ln")
        em.op("dve", lambda e: e.tensor_relu(dtp, tb),
              reads=["tbB"], writes=["dtpB", "bcast"], tag="sp_relu")
        em.op("dve", lambda e: e.tensor_add(dtp, dtp, tb2),
              reads=["dtpB", "tb2B"], writes=["dtpB", "bcast"], tag="sp_sum")

        em.dma(a_dram, small[:, 2:3], reads=["smallA"], writes=["a_dram"], tag="sp_aw")
        em.dma(Abc[:, :], strided(a_dram, [[0, 128], [1, NH]]),
               reads=["a_dram"], writes=["Abc"], tag="sp_ar")

        for half in range(2):
            for j in range(8):
                c = half * 8 + j
                em.op("pe", lambda e, c=c, j=j: e.matmul(
                    p_st[:, j * 64:(j + 1) * 64], dtp[:, c * Q:(c + 1) * Q],
                    identf[0:64, 0:64], start=True, stop=True),
                    reads=["dtpB", "identf", "p_y10"],
                    writes=["p_y10"] if j in (0, 7) else [], tag=f"trdtp{c}")
            em.op("dve", lambda e, half=half: e.tensor_copy(
                dtpT_all[:, half * 512:(half + 1) * 512], p_st[:, :]),
                reads=["p_y10"], writes=["dtpT", "p_y10"], tag=f"evdtp{half}")
        em.op("dve", lambda e: e.tensor_tensor(
            aT_all[:, :].rearrange("p (c h) -> p c h", c=NCH),
            dtpT_all[:, :].rearrange("p (c h) -> p c h", c=NCH),
            strided(Abc[:, :], [[NH, 128], [0, NCH], [1, NH]]), ALU.mult),
            reads=["dtpT", "Abc"], writes=["aT_all"], tag="aTmul")
        for half in range(2):
            for j in range(8):
                c = half * 8 + j
                em.op("pe", lambda e, c=c, j=j: e.matmul(
                    p_st[:, j * 64:(j + 1) * 64], LcumF[:, :],
                    aT_all[:, c * NH:(c + 1) * NH], start=True, stop=True),
                    reads=["LcumF", "aT_all", "p_y10"],
                    writes=["p_y10"] if j in (0, 7) else [], tag=f"sT{c}")
            em.op("dve", lambda e, half=half: e.tensor_copy(
                sT_all[:, half * 512:(half + 1) * 512], p_st[:, :]),
                reads=["p_y10"], writes=["sT_all", "p_y10"], tag=f"evsT{half}")
        for qtr in range(4):
            for j in range(4):
                c = qtr * 4 + j
                em.op("pe", lambda e, c=c, j=j: e.matmul(
                    p_st[0:64, j * 128:(j + 1) * 128],
                    aT_all[:, c * NH:(c + 1) * NH], LcumF[:, :],
                    start=True, stop=True),
                    reads=["aT_all", "LcumF", "p_y10"],
                    writes=["p_y10"] if j in (0, 3) else [], tag=f"sN{c}")
            em.op("dve", lambda e, qtr=qtr: e.tensor_copy(
                s_q[:, :], p_st[0:64, :]),
                reads=["p_y10", "s_qT"], writes=["s_q", "p_y10"], tag=f"evsN{qtr}")
            em.dma(bass.AP(s_scr_all.tensor, s_scr_all.offset + qtr * 4 * NH * Q,
                           [[Q, NH], [NH * Q, 4], [1, Q]]),
                   s_q[:, :].rearrange("h (c t) -> h c t", c=4),
                   reads=["s_q"], writes=["s_qT", "s_scr"], tag=f"sp_s{qtr}")
        em.dma(slast_row[0:1, :], sT_all[127:128, :],
               reads=["sT_all"], writes=["slast"], tag="slast")
        for j in range(2):
            em.op("pe", lambda e, j=j: e.matmul(
                p_st[:, :], onesF[0:1, :], slast_row[0:1, j * 512:(j + 1) * 512],
                start=True, stop=True),
                reads=["onesF", "slast", "p_y10"], writes=["p_y10"], tag=f"r1s{j}")
            em.op("dve", lambda e, j=j: e.tensor_copy(
                edecSarg[:, j * 512:(j + 1) * 512], p_st[:, :]),
                reads=["p_y10"], writes=["edecSarg", "p_y10"], tag=f"evr1{j}")
        em.op("act", lambda e: e.activation(edecS_all[:, :], edecSarg[:, :], AF.Exp),
              reads=["edecSarg"], writes=["edecS_all"], tag="edecS")
        em.op("act", lambda e: e.activation(expS_all[:, :], sT_all[:, :], AF.Exp),
              reads=["sT_all"], writes=["expS_all", "aT_all"], tag="expS")
        em.op("dve", lambda e: e.tensor_sub(edecSarg[:, :], edecSarg[:, :], sT_all[:, :]),
              reads=["edecSarg", "sT_all"], writes=["edecSarg"], tag="edTa")
        em.op("act", lambda e: e.activation(edecSarg[:, :], edecSarg[:, :], AF.Exp),
              reads=["edecSarg"], writes=["edT_all"], tag="edTb")
        edT_all = edecSarg

        S_GRP = [f"Sf_{g}" for g in range(NG)]
        SB_GRP = [f"Sb_{g}" for g in range(NG)]
        em.op("dve", lambda e: e.memset(S_f32[:, :], 0.0), writes=S_GRP, tag="S0")
        em.op("dve", lambda e: e.memset(S_b16[:, :], 0.0), writes=SB_GRP, tag="Sb0")

        def rep8(buf, c, g, pstep=NCH * NH):
            base = buf[:, :]
            return bass.AP(base.tensor, base.offset + c * NH + g * 8,
                           [[pstep, 128], [1, 8], [0, HP]])

        def prefix(c):
            em.dma(bcast[:, :],
                   strided(bass.AP(s_scr_all.tensor, s_scr_all.offset + c * NH * Q,
                                   [[0, 128], [1, NH * Q]]), [[0, 128], [1, NH * Q]]),
                   reads=["s_scr"], writes=["bcast"], tag=f"bc{c}")

        def prefix2(c):
            em.op("dve", lambda e, c=c: e.tensor_tensor(
                bcast[:, :].rearrange("p (h t) -> p h t", h=NH),
                bcast[:, :].rearrange("p (h t) -> p h t", h=NH),
                bass.AP(sT_all[:, :].tensor, sT_all[:, :].offset + c * NH,
                        [[NCH * NH, 128], [1, NH], [0, Q]]),
                ALU.subtract),
                reads=["bcast", "sT_all"], writes=["bcast"], tag=f"arg{c}")

        def prefix3(c):
            em.op("pool", lambda e: e.affine_select(
                bcast[:, :].rearrange("p (h t) -> p h t", h=NH),
                bcast[:, :].rearrange("p (h t) -> p h t", h=NH),
                [[0, NH], [1, Q]], ALU.is_ge, -1e9, channel_multiplier=-1),
                reads=["bcast"], writes=["bcast"], tag=f"mask{c}")

        def prefix4(c):
            em.op("act", lambda e: e.activation(lamT[:, :], bcast[:, :], AF.Exp),
                  reads=["bcast"], writes=["lamT"], tag=f"lam{c}")

        pend_yn = []

        def flush_yn():
            while pend_yn:
                g, gb, toff, c = pend_yn.pop()
                for j in range(4):
                    em.op("pe", lambda e, j=j, gb=gb: e.transpose(
                        p_tr[1][:, j * 128:(j + 1) * 128],
                        yn16[gb][:, j * 128:(j + 1) * 128], ident[:, :]),
                        reads=[f"yn16{gb}", "ident", "p_tr1"],
                        writes=["p_tr1"] if j in (0, 3) else [], tag=f"trY{g}.{j}.{c}")
                em.op("dve", lambda e, g=g, toff=toff: e.tensor_tensor(
                    bass.AP(ynTb[:, :].tensor, ynTb[:, :].offset + 4 * g * 256 + toff,
                            [[32 * 256, 128], [256, 4], [1, 128]]),
                    p_tr[1][:, 0:512].rearrange("p (a t) -> p a t", a=4),
                    bass.AP(gateC[0][:, :].tensor,
                            gateC[0][:, :].offset + 4 * g * 256 + toff,
                            [[32 * 256, 128], [256, 4], [1, 128]]),
                    ALU.mult),
                    reads=["p_tr1", "gateC0", f"ynTb{g}"], writes=[f"ynTb{g}", "p_tr1"],
                    tag=f"evY{g}.{c}")

        prefix(0)
        prefix2(0)
        prefix3(0)
        prefix4(0)
        for c in range(NCH):
            toff = (c % 2) * 128
            if c % 2 == 0:
                flush_yn()
                em.dma(xbcC[0][:, :].rearrange("p (a t) -> p a t", a=48),
                       xbc_scr[:, :, c * Q:(c + 2) * Q].rearrange("a p t -> p a t"),
                       writes=["xbcC0"], tag=f"ldxbc{c}")
                em.dma(gateC[0][:, :].rearrange("p (a t) -> p a t", a=32),
                       gate_scr[:, :, c * Q:(c + 2) * Q].rearrange("a p t -> p a t"),
                       writes=["gateC0"], tag=f"ldg{c}")
            xbtok, gttok = "xbcC0", "gateC0"

            for ct4 in range(8):
                ptb = ct4 % 2
                for j in range(4):
                    ct = ct4 * 4 + j
                    em.op("pe", lambda e, ptb=ptb, j=j, ct=ct, toff=toff:
                          e.transpose(p_tr[ptb][:, j * 128:(j + 1) * 128],
                                      xbcC[0][:, ct * 256 + toff:ct * 256 + toff + 128],
                                      ident[:, :]),
                          reads=[xbtok, "ident", f"p_tr{ptb}"],
                          writes=[f"p_tr{ptb}"] if j in (0, 3) else [],
                          tag=f"trX{ct}.{c}")
                em.op("act", lambda e, ptb=ptb, ct4=ct4: e.copy(
                    X_nat[:, ct4 * 512:(ct4 + 1) * 512], p_tr[ptb][:, 0:512]),
                    reads=[f"p_tr{ptb}"], writes=["X_nat", f"p_tr{ptb}"],
                    tag=f"evX{ct4}.{c}")

            for g in range(NG):
                gb = g % 2
                cbb = 0
                cbt = "p_cb0"
                bslice = lambda g=g, toff=toff: xbcC[0][
                    :, (32 + g) * 256 + toff:(32 + g) * 256 + toff + 128]
                cslice = lambda g=g, toff=toff: xbcC[0][
                    :, (40 + g) * 256 + toff:(40 + g) * 256 + toff + 128]
                em.op("pe", lambda e, bs=bslice, cs=cslice, cbb=cbb: e.matmul(
                    p_cb[cbb][:, 0:128], bs(), cs(), start=True, stop=True),
                    reads=[xbtok, cbt], writes=[cbt], tag=f"cb{g}.{c}")
                em.op("pe", lambda e, bs=bslice: e.transpose(
                    p_tr[0][:, 0:128], bs(), ident[:, :]),
                    reads=[xbtok, "ident", "p_tr0"], writes=["p_tr0"], tag=f"trB{g}.{c}")
                em.op("act", lambda e, gb=gb: e.copy(bn_g[gb][:, :], p_tr[0][:, 0:128]),
                      reads=["p_tr0"], writes=[f"bn_g{gb}", "p_tr0"], tag=f"evB{g}.{c}")
                em.op("pool", lambda e, g=g, gb=gb, c=c: e.tensor_tensor(
                    xdt_g[gb][:, :].rearrange("p (h f) -> p h f", h=8),
                    X_nat[:, g * 512:(g + 1) * 512].rearrange("p (h f) -> p h f", h=8),
                    rep8(dtpT_all, c, g), ALU.mult),
                    reads=["X_nat", "dtpT"], writes=[f"xdt_g{gb}"], tag=f"xdt{g}.{c}")
                em.op("pool", lambda e, g=g, gb=gb, c=c: e.tensor_tensor(
                    xd2_g[gb][:, :].rearrange("p (h f) -> p h f", h=8),
                    xdt_g[gb][:, :].rearrange("p (h f) -> p h f", h=8),
                    rep8(edT_all, c, g), ALU.mult),
                    reads=[f"xdt_g{gb}", "edT_all"], writes=[f"xd2_g{gb}"], tag=f"xd2{g}.{c}")
                em.op("dve", lambda e, g=g, gb=gb, cbb=cbb: e.tensor_tensor(
                    gam_g[gb][:, :].rearrange("p (h t) -> p h t", h=8),
                    lamT[:, g * 8 * Q:(g + 1) * 8 * Q].rearrange("p (h t) -> p h t", h=8),
                    strided(p_cb[cbb][:, 0:128], [[512, 128], [0, 8], [1, 128]]),
                    ALU.mult),
                    reads=["lamT", cbt], writes=[f"gam_g{gb}", cbt], tag=f"gam{g}.{c}")
                for hh in range(8):
                    em.op("pe", lambda e, hh=hh, gb=gb: e.matmul(
                        p_y1[gb][:, hh * 64:(hh + 1) * 64],
                        gam_g[gb][:, hh * Q:(hh + 1) * Q],
                        xdt_g[gb][:, hh * 64:(hh + 1) * 64],
                        start=True, stop=True),
                        reads=[f"gam_g{gb}", f"xdt_g{gb}", f"p_y1{gb}"],
                        writes=[f"p_y1{gb}"] if hh in (0, 7) else [], tag=f"y1.{g}.{hh}.{c}")
                em.op("pe", lambda e, cs=cslice, g=g, gb=gb: e.matmul(
                    p_y2[gb][:, :], cs(), S_b16[:, g * 512:(g + 1) * 512],
                    start=True, stop=True),
                    reads=[xbtok, SB_GRP[g], f"p_y2{gb}"], writes=[f"p_y2{gb}"], tag=f"y2.{g}.{c}")
                em.op("pe", lambda e, gb=gb: e.matmul(
                    p_ds[:, :], bn_g[gb][:, :], xd2_g[gb][:, :], start=True, stop=True),
                    reads=[f"bn_g{gb}", f"xd2_g{gb}", "p_ds"], writes=["p_ds"],
                    tag=f"ds{g}.{c}")
                em.op("dve", lambda e, g=g, c=c, gb=gb: e.tensor_tensor(
                    ytmp[:, :].rearrange("p (h f) -> p h f", h=8),
                    p_y2[gb][:, :].rearrange("p (h f) -> p h f", h=8),
                    rep8(expS_all, c, g), ALU.mult),
                    reads=[f"p_y2{gb}", "expS_all"], writes=["ytmp", f"p_y2{gb}"],
                    tag=f"ytmp{g}.{c}")
                em.op("dve", lambda e, gb=gb: e.tensor_add(
                    ytmp[:, :], p_y1[gb][:, :], ytmp[:, :]),
                    reads=[f"p_y1{gb}", "ytmp"], writes=["ytmp", f"p_y1{gb}"],
                    tag=f"ycmb{g}.{c}")
                em.op("pool", lambda e, g=g, gb=gb: e.tensor_tensor(
                    xdterm[gb][:, :].rearrange("p (h f) -> p h f", h=8),
                    X_nat[:, g * 512:(g + 1) * 512].rearrange("p (h f) -> p h f", h=8),
                    strided(dbcp[:, g * 8:g * 8 + 8], [[NH, 128], [1, 8], [0, HP]]),
                    ALU.mult),
                    reads=["X_nat", "dbcp"], writes=[f"xdterm{gb}"], tag=f"xD{g}.{c}")
                em.op("dve", lambda e, gb=gb: e.tensor_add(
                    yg[gb][:, :], ytmp[:, :], xdterm[gb][:, :]),
                    reads=["ytmp", f"xdterm{gb}"], writes=[f"yg{gb}"], tag=f"ygv{g}.{c}")
                em.op("act", lambda e, gb=gb: e.activation(
                    trashB[:, :], yg[gb][:, :], AF.Square, accum_out=gcol[:, 0:1]),
                    reads=[f"yg{gb}"], writes=["trashB", "gcol"],
                    tag=f"gn1.{g}.{c}", force=True)
                em.op("act", lambda e: e.activation(
                    gcol[:, 1:2], gcol[:, 0:1], AF.Ln, bias=cst[:, 0:1],
                    scale=cst[:, 1:2]),
                    reads=["gcol", "cst"], writes=["gcol1"], tag=f"gn2.{g}.{c}", force=True)
                em.op("act", lambda e: e.activation(
                    gcol[:, 2:3], gcol[:, 1:2], AF.Exp, scale=cst[:, 2:3]),
                    reads=["gcol1", "cst"], writes=["gcol2"], tag=f"gn3.{g}.{c}", force=True)
                em.op("act", lambda e, gb=gb: e.activation(
                    yn16[gb][:, :], yg[gb][:, :], AF.Copy, scale=gcol[:, 2:3]),
                    reads=[f"yg{gb}", "gcol2"], writes=[f"yn16{gb}"], tag=f"yn{g}.{c}")
                flush_yn()
                pend_yn.append((g, gb, toff, c))
                em.op("pool", lambda e, g=g, gb=gb, c=c: e.tensor_tensor(
                    stmp[gb][:, :].rearrange("p (h f) -> p h f", h=8),
                    S_f32[:, g * 512:(g + 1) * 512].rearrange("p (h f) -> p h f", h=8),
                    rep8(edecS_all, c, g), ALU.mult),
                    reads=[S_GRP[g], "edecS_all"], writes=[f"stmpB{gb}"], tag=f"sdec{g}.{c}")
                em.op("dve", lambda e, g=g, gb=gb: e.tensor_add(
                    S_f32[:, g * 512:(g + 1) * 512], stmp[gb][:, :], p_ds[:, :]),
                    reads=[f"stmpB{gb}", "p_ds"], writes=[S_GRP[g], "p_ds"],
                    tag=f"supd{g}.{c}")
                em.op("act", lambda e, g=g: e.copy(
                    S_b16[:, g * 512:(g + 1) * 512], S_f32[:, g * 512:(g + 1) * 512]),
                    reads=[S_GRP[g]], writes=[SB_GRP[g]], tag=f"sb16{g}.{c}")
                if c + 1 < NCH:
                    if g == 0:
                        prefix(c + 1)
                    elif g == 2:
                        prefix2(c + 1)
                    elif g == 3:
                        prefix3(c + 1)
                    elif g == 4:
                        prefix4(c + 1)
            if c % 2 == 1:
                flush_yn()
                em.dma(ynT_scr[:, :, (c - 1) * Q:(c + 1) * Q].rearrange("a p t -> p a t"),
                       ynTb[:, :].rearrange("p (a t) -> p a t", a=32), q="qact",
                       reads=[f"ynTb{g2}" for g2 in range(NG)],
                       writes=[f"ynTb{g2}" for g2 in range(NG)],
                       tag=f"spyn{c}")
        close_scope()

    if STAGES[2]:
        ynr = [sbs(f"ynr{i}", [128, 32 * 512], BF16) for i in range(2)]
        wcs = [sbs("wcs0", [128, 32 * 256])]
        wcb = [sbs(f"wcb{i}", [128, 32 * 512], BF16) for i in range(2)]
        nwT = sbs("nwT", [128, 32])
        attb = [sbs(f"attb{i}", [128, 512]) for i in range(2)]
        p_c = [pss(f"p_c{i}", [128, 512]) for i in range(2)]

        em.dma(nwT[:, :], nw_in, writes=["nwT"], tag="ldnw")

        def rec_wo(dpj):
            wbj = dpj % 2
            for hq in range(2):
                em.dma(wcs[0][:, :].rearrange("p (a c) -> p a c", a=32),
                       woutT[:, dpj * 512 + hq * 256:dpj * 512 + (hq + 1) * 256]
                       .rearrange("(a p) c -> p a c", p=128),
                       reads=["wcs0"], writes=["wcs0w"], tag=f"ldWo{dpj}.{hq}")
                for ih2 in range(32):
                    if ih2 % 2 == 0:
                        em.op("act", lambda e, wbj=wbj, hq=hq, ih2=ih2: e.activation(
                            wcb[wbj][:, ih2 * 512 + hq * 256:ih2 * 512 + (hq + 1) * 256],
                            wcs[0][:, ih2 * 256:(ih2 + 1) * 256],
                            AF.Copy, scale=nwT[:, ih2:ih2 + 1]),
                            reads=["wcs0w", "nwT"],
                            writes=[f"wcb{wbj}"] if ih2 in (0, 30) else [],
                            tag=f"cvWo{dpj}.{hq}.{ih2}")
                    else:
                        em.op("dve", lambda e, wbj=wbj, hq=hq, ih2=ih2: e.tensor_scalar(
                            wcb[wbj][:, ih2 * 512 + hq * 256:ih2 * 512 + (hq + 1) * 256],
                            wcs[0][:, ih2 * 256:(ih2 + 1) * 256],
                            nwT[:, ih2:ih2 + 1], None, ALU.mult),
                            reads=["wcs0w", "nwT"],
                            writes=[f"wcb{wbj}"] if ih2 == 31 else [],
                            tag=f"cvWo{dpj}.{hq}.{ih2}")

        for dp in range(8):
            wb = dp % 2
            if dp == 0:
                rec_wo(0)
            for tc in range(NCH):
                if tc == 5 and dp + 1 < 8:
                    rec_wo(dp + 1)
                yb = (tc // 4) % 2
                if tc % 4 == 0:
                    em.dma(ynr[yb][:, :].rearrange("p (a t) -> p a t", a=32),
                           ynT_scr[:, :, tc * Q:(tc + 4) * Q].rearrange("a p t -> p a t"),
                           q=("qsp" if (tc // 4) % 2 == 0 else "qact"),
                           reads=[f"ynr{yb}"], writes=[f"ynr{yb}w"], tag=f"ldyn{dp}.{tc}")
                pc = tc % 2
                for i in range(32):
                    em.op("pe", lambda e, yb=yb, wb=wb, i=i, tc=tc, pc=pc: e.matmul(
                        p_c[pc][:, :],
                        ynr[yb][:, i * 512 + (tc % 4) * 128:i * 512 + (tc % 4) * 128 + 128],
                        wcb[wb][:, i * 512:(i + 1) * 512],
                        start=(i == 0), stop=(i == 31)),
                        reads=[f"ynr{yb}w", f"wcb{wb}", f"p_c{pc}"],
                        writes=[f"p_c{pc}"] if i in (0, 31) else [],
                        tag=f"mmC{dp}.{tc}.{i}")
                em.op("dve", lambda e, pc=pc: e.tensor_copy(attb[pc][:, :], p_c[pc][:, :]),
                      reads=[f"p_c{pc}"], writes=[f"attb{pc}", f"p_c{pc}"],
                      tag=f"evC{dp}.{tc}")
                em.dma(attn_scr[tc][:, dp * 512:(dp + 1) * 512], attb[pc][:, :],
                       q="qact", reads=[f"attb{pc}"], tag=f"spC{dp}.{tc}")
                if tc == NCH - 1:
                    pass
        close_scope()

        attc = sbs("attc", [128, D])
        xc = sbs("xc", [128, D])
        z0 = sbs("z0", [128, D])
        trash4 = sbs("trash4", [128, D])
        zb = sbs("zb", [128, D])
        zb16 = sbs("zb16", [128, D], BF16)
        zTp = sbs("zTp", [128, 32 * 256], BF16)
        lncol = sbs("lncol", [128, 8])
        p_tc = [pss(f"p_tc{i}", [128, 1024], BF16) for i in range(2)]

        for tc in range(NCH):
            em.dma(attc[:, :], attn_scr[tc], reads=["attc"], writes=["attcw"],
                   tag=f"ldat{tc}")
            em.dma(xc[:, :], x_in[tc * Q:(tc + 1) * Q, :], reads=["xc"],
                   writes=["xcw"], tag=f"ldx{tc}")
            em.op("dve", lambda e: e.scalar_tensor_tensor(
                z0[:, :], attc[:, :], 1.0, xc[:, :], ALU.mult, ALU.add,
                accum_out=lncol[:, 0:1]),
                reads=["attcw", "xcw"], writes=["z0", "lncol0", "attc", "xc"],
                tag=f"z0.{tc}", force=True)
            em.op("act", lambda e: e.activation(
                trash4[:, :], z0[:, :], AF.Square, accum_out=lncol[:, 1:2]),
                reads=["z0"], writes=["trash4", "lncol1"], tag=f"sq.{tc}", force=True)
            em.op("dve", lambda e: e.tensor_scalar_mul(lncol[:, 2:3], lncol[:, 0:1], 1.0 / D),
                  reads=["lncol0"], writes=["lncol2"], tag=f"mean.{tc}", force=True)
            em.op("dve", lambda e: e.tensor_tensor(lncol[:, 3:4], lncol[:, 2:3],
                                                   lncol[:, 2:3], ALU.mult),
                  reads=["lncol2"], writes=["lncol3"], tag=f"m2.{tc}", force=True)
            em.op("dve", lambda e: e.tensor_scalar(lncol[:, 4:5], lncol[:, 1:2],
                                                   1.0 / D, lncol[:, 3:4], ALU.mult,
                                                   ALU.subtract),
                  reads=["lncol1", "lncol3"], writes=["lncol4"], tag=f"var.{tc}", force=True)
            em.op("act", lambda e: e.activation(lncol[:, 5:6], lncol[:, 4:5], AF.Ln,
                                                bias=cst[:, 0:1]),
                  reads=["lncol4", "cst"], writes=["lncol5"], tag=f"lnv.{tc}", force=True)
            em.op("act", lambda e: e.activation(lncol[:, 6:7], lncol[:, 5:6], AF.Exp,
                                                scale=cst[:, 2:3]),
                  reads=["lncol5", "cst"], writes=["lncol6"], tag=f"rstd.{tc}", force=True)
            em.op("dve", lambda e: e.tensor_scalar(
                zb[:, :], z0[:, :], lncol[:, 2:3], lncol[:, 6:7],
                ALU.subtract, ALU.mult),
                reads=["z0", "lncol2", "lncol6"], writes=["zb"], tag=f"zn.{tc}")
            em.dma(z_scr[tc], zb[:, :], q="qact", reads=["zb"], tag=f"spz{tc}")
            em.op("act", lambda e: e.copy(zb16[:, :], zb[:, :]),
                  reads=["zb"], writes=["zb16"], tag=f"zb16.{tc}")
            for j4 in range(8):
                pb = j4 % 2
                for j in range(4):
                    jj = j4 * 4 + j
                    em.op("pe", lambda e, pb=pb, j=j, jj=jj: e.transpose(
                        p_tc[pb][:, j * 128:(j + 1) * 128],
                        zb16[:, jj * 128:(jj + 1) * 128], ident[:, :]),
                        reads=["zb16", "ident", f"p_tc{pb}"],
                        writes=[f"p_tc{pb}"] if j == 3 else [], tag=f"trz{jj}.{tc}")
                em.op("dve", lambda e, pb=pb, j4=j4, tc=tc: e.tensor_copy(
                    bass.AP(zTp[:, :].tensor,
                            zTp[:, :].offset + j4 * 4 * 256 + (tc % 2) * 128,
                            [[32 * 256, 128], [256, 4], [1, 128]]),
                    p_tc[pb][:, 0:512].rearrange("p (a t) -> p a t", a=4)),
                    reads=[f"p_tc{pb}", "zTp"], writes=["zTp", f"p_tc{pb}"],
                    tag=f"evz{j4}.{tc}")
            if tc % 2 == 1:
                em.dma(zT_scr[:, :, (tc - 1) * Q:(tc + 1) * Q].rearrange("a p t -> p a t"),
                       zTp[:, :].rearrange("p (a t) -> p a t", a=32), q="qact",
                       reads=["zTp"], writes=["zTp"], tag=f"spzT{tc}")
        close_scope()

    if STAGES[3]:
        zTb = sbs("zTb", [128, 32 * T], BF16)
        gw_sb = sbs("gw_sb", [128, 32 * NE])
        gw16 = sbs("gw16", [128, 32 * NE], BF16)
        fstg = [sbs("fstg0", [128, 32 * 128])]
        fb16 = [sbs(f"fb16{i}", [128, 32 * 128], BF16) for i in range(2)]
        lgb = sbs("lgb", [128, NE])
        lgm = sbs("lgm", [128, NE])
        rc = sbs("rc", [128, 4])
        xlb = sbs("xlb", [128, T], BF16)
        rbuf = sbs("rbuf", [128, 512])
        h1b = sbs("h1b", [128, T], BF16)
        wdstg = sbs("wdstg", [128, T])
        wdb16 = sbs("wdb16", [128, T], BF16)
        p_d = [pss(f"p_d{i}", [128, 512]) for i in range(2)]
        p_lg = pss("p_lg", [128, 512])

        em.dma(zTb[:, :].rearrange("p (a t) -> p a t", a=32),
               zT_scr.rearrange("a p t -> p a t"), writes=["zTb"], tag="ldzT")
        em.dma(gw_sb[:, :].rearrange("p (a k) -> p a k", a=32),
               w_gate.rearrange("(a p) k -> p a k", p=128), writes=["gwf"], tag="ldgw")
        em.op("act", lambda e: e.copy(gw16[:, :], gw_sb[:, :]),
              reads=["gwf"], writes=["gw"], tag="cvgw")
        for tc in range(NCH):
            for d in range(32):
                em.op("pe", lambda e, d=d, tc=tc: e.matmul(
                    p_lg[:, 0:NE], zTb[:, d * T + tc * Q:d * T + tc * Q + Q]
                    if False else zTb[:, d * T + tc * 128:d * T + tc * 128 + 128],
                    gw16[:, d * NE:(d + 1) * NE],
                    start=(d == 0), stop=(d == 31)),
                    reads=["zTb", "gw", "p_lg"],
                    writes=["p_lg"] if d in (0, 31) else [], tag=f"mmL{tc}.{d}")
            em.op("dve", lambda e: e.tensor_copy(lgb[:, :], p_lg[:, 0:NE]),
                  reads=["p_lg"], writes=["lgb", "p_lg"], tag=f"evL{tc}", force=True)
            em.op("dve", lambda e: e.tensor_reduce(rc[:, 0:1], lgb[:, :], AX.X, ALU.max),
                  reads=["lgb"], writes=["rc0"], tag=f"m1.{tc}", force=True)
            em.op("dve", lambda e: e.tensor_scalar(lgm[:, :], lgb[:, :], rc[:, 0:1],
                                                   None, ALU.is_lt),
                  reads=["lgb", "rc0"], writes=["lgm"], tag=f"msk.{tc}", force=True)
            em.op("dve", lambda e: e.scalar_tensor_tensor(
                lgm[:, :], lgm[:, :], 1.0e4, lgb[:, :], ALU.mult, ALU.add),
                reads=["lgm", "lgb"], writes=["lgm"], tag=f"mu.{tc}", force=True)
            em.op("dve", lambda e: e.tensor_reduce(rc[:, 1:2], lgm[:, :], AX.X, ALU.max),
                  reads=["lgm"], writes=["rc1"], tag=f"m2.{tc}", force=True)
            em.op("dve", lambda e: e.tensor_scalar_add(rc[:, 2:3], rc[:, 1:2], -1.0e4),
                  reads=["rc1"], writes=["rc2"], tag=f"m2b.{tc}", force=True)
            em.op("dve", lambda e, tc=tc: e.tensor_add(
                route_cols[:, tc:tc + 1], rc[:, 0:1], rc[:, 2:3]),
                reads=["rc0", "rc2"], writes=["route"], tag=f"rt.{tc}", force=True)
        em.dma(strided(r_scr, [[1, 128], [128, NCH]]), route_cols[:, :],
               reads=["route"], writes=["r_scr"], tag="spr",
               allow_slow_non_contiguous=True)
        em.dma(rbc[:, :], strided(r_scr, [[0, 128], [1, T]]),
               reads=["r_scr"], writes=["rbc"], tag="ldrbc")
        def rec_f(ltj, wsrc, tagc):
            bj = ltj % 2
            em.dma(fstg[0][:, :].rearrange("p (a c) -> p a c", a=32),
                   wsrc[:, ltj * 128:(ltj + 1) * 128].rearrange("(a p) c -> p a c", p=128),
                   writes=["fstg0"], tag=f"ld{tagc}{ltj}")
            em.op("act", lambda e, bj=bj: e.copy(fb16[bj][:, :], fstg[0][:, :]),
                  reads=["fstg0"], writes=[f"fb16{bj}"], tag=f"cv{tagc}{ltj}")

        for lt in range(16):
            bb = lt % 2
            if lt == 0:
                rec_f(0, w_fc1, "F")
            if lt + 1 < 16:
                rec_f(lt + 1, w_fc1, "F")
            for t4 in range(4):
                pp = t4 % 2
                for d in range(32):
                    em.op("pe", lambda e, bb=bb, t4=t4, d=d, pp=pp: e.matmul(
                        p_d[pp][:, :], fb16[bb][:, d * 128:(d + 1) * 128],
                        zTb[:, d * T + t4 * 512:d * T + (t4 + 1) * 512],
                        start=(d == 0), stop=(d == 31)),
                        reads=[f"fb16{bb}", "zTb", f"p_d{pp}"],
                        writes=[f"p_d{pp}"] if d in (0, 31) else [], tag=f"mmF{lt}.{t4}.{d}")
                em.op("act", lambda e, pp=pp, t4=t4: e.copy(
                    xlb[:, t4 * 512:(t4 + 1) * 512], p_d[pp][:, :]),
                    reads=[f"p_d{pp}"], writes=["xlb", f"p_d{pp}"], tag=f"evF{lt}.{t4}")
            em.dma(xlat_scr[lt], xlb[:, :], q="qact", reads=["xlb"], tag=f"spF{lt}")
        def rec_wd(k):
            ih, half = k // 2, k % 2
            em.dma(wdstg[:, :], w_down[ih * 128:(ih + 1) * 128,
                                       half * 2048:(half + 1) * 2048],
                   q="qact", reads=["wdstg"], writes=["wdstgw"], tag=f"ldWd{k}")
            em.op("act", lambda e: e.copy(wdb16[:, :], wdstg[:, :]),
                  reads=["wdstgw"], writes=["wdb16", "wdstg"], tag=f"cvWd{k}")
            em.dma(wdb_scr[ih][:, half * 2048:(half + 1) * 2048], wdb16[:, :],
                   q="qact", reads=["wdb16"], tag=f"spWd{k}")

        for it in range(64):
            bb = it % 2
            if it == 0:
                rec_f(0, w_up, "U")
            if it + 1 < 64:
                rec_f(it + 1, w_up, "U")
            for k in range(it * 2, it * 2 + 2):
                rec_wd(k)
            for t4 in range(4):
                pp = t4 % 2
                for d in range(32):
                    em.op("pe", lambda e, bb=bb, t4=t4, d=d, pp=pp: e.matmul(
                        p_d[pp][:, :], fb16[bb][:, d * 128:(d + 1) * 128],
                        zTb[:, d * T + t4 * 512:d * T + (t4 + 1) * 512],
                        start=(d == 0), stop=(d == 31)),
                        reads=[f"fb16{bb}", "zTb", f"p_d{pp}"],
                        writes=[f"p_d{pp}"] if d in (0, 31) else [], tag=f"mmU{it}.{t4}.{d}")
                em.op("act", lambda e, pp=pp: e.activation(
                    rbuf[:, :], p_d[pp][:, :], AF.Relu),
                    reads=[f"p_d{pp}"], writes=["rbuf", f"p_d{pp}"], tag=f"rl{it}.{t4}")
                em.op("dve", lambda e, t4=t4: e.tensor_tensor(
                    h1b[:, t4 * 512:(t4 + 1) * 512], rbuf[:, :], rbuf[:, :], ALU.mult),
                    reads=["rbuf"], writes=["h1b"], tag=f"sq{it}.{t4}")
            em.dma(h1_scr[it], h1b[:, :], q="qact", reads=["h1b"], tag=f"spU{it}")
        close_scope()

        esT = sbs("esT", [128, 16 * T], BF16)
        xlT_h = nc.sbuf_tensor("xlT", [128, 16 * T], BF16)
        xlT = xlT_h.__enter__()
        estg_h = nc.sbuf_tensor("estg0", [128, 16 * 128], F32)
        estg = [estg_h.__enter__()]
        eb16_h = [nc.sbuf_tensor(f"eb16{i}", [128, 16 * 128], BF16) for i in range(2)]
        eb16 = [h.__enter__() for h in eb16_h]
        p_d2 = [pss(f"p_d2{i}", [128, 512]) for i in range(2)]

        em.dma(xlT[:, :].rearrange("p (a t) -> p a t", a=16),
               xlat_scr.rearrange("a p t -> p a t"), writes=["xlT"], tag="ldxl")
        def rec_e(mtj):
            bj = mtj % 2
            em.dma(estg[0][:, :].rearrange("p (a c) -> p a c", a=16),
                   w_exp[:, mtj * 128:(mtj + 1) * 128].rearrange("(a p) c -> p a c", p=128),
                   writes=["estg0"], tag=f"ldE{mtj}")
            em.op("act", lambda e, bj=bj: e.copy(eb16[bj][:, :], estg[0][:, :]),
                  reads=["estg0"], writes=[f"eb16{bj}"], tag=f"cvE{mtj}")

        for mt in range(16):
            bb = mt % 2
            if mt == 0:
                rec_e(0)
            if mt + 1 < 16:
                rec_e(mt + 1)
            for t4 in range(4):
                pp = t4 % 2
                for l in range(16):
                    em.op("pe", lambda e, bb=bb, t4=t4, l=l, pp=pp: e.matmul(
                        p_d2[pp][:, :], eb16[bb][:, l * 128:(l + 1) * 128],
                        xlT[:, l * T + t4 * 512:l * T + (t4 + 1) * 512],
                        start=(l == 0), stop=(l == 15)),
                        reads=[f"eb16{bb}", "xlT", f"p_d2{pp}"],
                        writes=[f"p_d2{pp}"] if l in (0, 15) else [], tag=f"mmE{mt}.{t4}.{l}")
                em.op("dve", lambda e, pp=pp, mt=mt, t4=t4: e.tensor_tensor(
                    esT[:, mt * T + t4 * 512:mt * T + (t4 + 1) * 512],
                    p_d2[pp][:, :], rbc[:, t4 * 512:(t4 + 1) * 512], ALU.mult),
                    reads=[f"p_d2{pp}", "rbc"], writes=["esT", f"p_d2{pp}"],
                    tag=f"evE{mt}.{t4}")
        em.fence()
        for h in [eb16_h[1], eb16_h[0], estg_h, xlT_h]:
            h.__exit__(None, None, None)
        f2stg = [sbs("f2stg0", [128, 16 * 256])]
        f2b16 = [sbs(f"f2b16{i}", [128, 16 * 512], BF16) for i in range(2)]
        rb = [sbs(f"rb{i}", [128, 512]) for i in range(2)]

        def rec_f2(dpj):
            fbj = dpj % 2
            for hq in range(2):
                em.dma(f2stg[0][:, :].rearrange("p (a c) -> p a c", a=16),
                       w_fc2[:, dpj * 512 + hq * 256:dpj * 512 + (hq + 1) * 256]
                       .rearrange("(a p) c -> p a c", p=128),
                       writes=["f2stg0"], tag=f"ldF2{dpj}.{hq}")
                em.op("act", lambda e, hq=hq, fbj=fbj: e.copy(
                    bass.AP(f2b16[fbj][:, :].tensor, f2b16[fbj][:, :].offset + hq * 256,
                            [[16 * 512, 128], [512, 16], [1, 256]]),
                    f2stg[0][:, :].rearrange("p (a c) -> p a c", a=16)),
                    reads=["f2stg0"], writes=[f"f2b16{fbj}"],
                    tag=f"cvF2{dpj}.{hq}")

        for dp2 in range(8):
            fb = dp2 % 2
            if dp2 == 0:
                rec_f2(0)
            if dp2 + 1 < 8:
                rec_f2(dp2 + 1)
            for tc in range(NCH):
                pp = tc % 2
                for m in range(16):
                    em.op("pe", lambda e, m=m, tc=tc, pp=pp, fb=fb: e.matmul(
                        p_d2[pp][:, :], esT[:, m * T + tc * 128:m * T + tc * 128 + 128],
                        f2b16[fb][:, m * 512:(m + 1) * 512],
                        start=(m == 0), stop=(m == 15)),
                        reads=["esT", f"f2b16{fb}", f"p_d2{pp}"],
                        writes=[f"p_d2{pp}"] if m in (0, 15) else [], tag=f"mmR{dp2}.{tc}.{m}")
                em.op("dve", lambda e, pp=pp: e.tensor_copy(rb[pp][:, :], p_d2[pp][:, :]),
                      reads=[f"p_d2{pp}"], writes=[f"rb{pp}", f"p_d2{pp}"],
                      tag=f"evR{dp2}.{tc}")
                em.dma(routed_scr[tc][:, dp2 * 512:(dp2 + 1) * 512], rb[pp][:, :],
                       q="qact", reads=[f"rb{pp}"], tag=f"spR{dp2}.{tc}")
        close_scope()

        wpan = sbs("wpan", [128, 64 * 512], BF16)
        h1r = [sbs(f"h1r{i}", [128, 64 * 256], BF16) for i in range(2)]
        rtb = [sbs(f"rtb{i}", [128, 512]) for i in range(2)]
        acb = [sbs(f"acb{i}", [128, 512]) for i in range(2)]
        p_d3 = [pss(f"p_d3{i}", [128, 512]) for i in range(2)]

        for dp in range(8):
            em.dma(wpan[:, :].rearrange("p (a c) -> p a c", a=64),
                   wdb_scr[:, :, dp * 512:(dp + 1) * 512].rearrange("a p c -> p a c"),
                   reads=["wpan"], writes=["wpanw"], tag=f"ldD{dp}")
            for tc in range(NCH):
                hb = (tc // 2) % 2
                if tc % 2 == 0:
                    em.dma(h1r[hb][:, :].rearrange("p (a t) -> p a t", a=64),
                           h1_scr[:, :, tc * Q:(tc + 2) * Q].rearrange("a p t -> p a t"),
                           reads=[f"h1r{hb}"], writes=[f"h1r{hb}w"], tag=f"ldh{dp}.{tc}")
                pp = tc % 2
                for i in range(64):
                    em.op("pe", lambda e, hb=hb, i=i, tc=tc, pp=pp: e.matmul(
                        p_d3[pp][:, :],
                        h1r[hb][:, i * 256 + (tc % 2) * 128:i * 256 + (tc % 2) * 128 + 128],
                        wpan[:, i * 512:(i + 1) * 512],
                        start=(i == 0), stop=(i == 63)),
                        reads=[f"h1r{hb}w", "wpanw", f"p_d3{pp}"],
                        writes=[f"p_d3{pp}"] if i in (0, 63) else [], tag=f"mmD{dp}.{tc}.{i}")
                em.dma(rtb[pp][:, :], routed_scr[tc][:, dp * 512:(dp + 1) * 512],
                       reads=[f"rtb{pp}"], writes=[f"rtb{pp}w"], tag=f"ldrt{dp}.{tc}")
                em.op("dve", lambda e, pp=pp: e.tensor_add(
                    acb[pp][:, :], p_d3[pp][:, :], rtb[pp][:, :]),
                    reads=[f"p_d3{pp}", f"rtb{pp}w"],
                    writes=[f"acb{pp}", f"p_d3{pp}", f"rtb{pp}"], tag=f"evD{dp}.{tc}")
                em.dma(acc_scr[tc][:, dp * 512:(dp + 1) * 512], acb[pp][:, :],
                       q="qact", reads=[f"acb{pp}"], tag=f"spD{dp}.{tc}")
        close_scope()

        gbc = sbs("gbc", [128, D])
        bbc = sbs("bbc", [128, D])
        ac4_ = [sbs(f"ac4_{i}", [128, D]) for i in range(2)]
        zc4_ = [sbs(f"zc4_{i}", [128, D]) for i in range(2)]
        f1_ = [sbs("f1_s", [128, D])] * 2
        f2_ = [sbs("f2_s", [128, D])] * 2
        trash5 = sbs("trash5", [128, D])
        obuf_ = [sbs(f"obuf_{i}", [128, D]) for i in range(2)]
        lc_ = [sbs(f"lc_{i}", [128, 10]) for i in range(2)]

        em.dma(gbc[:, :], strided(gam_in, [[0, 128], [1, D]]), writes=["gbc"], tag="ldg")
        em.dma(bbc[:, :], strided(bet_in, [[0, 128], [1, D]]), writes=["bbc"], tag="ldb")
        for tc in range(NCH):
            pp4 = tc % 2
            ac4, zc4, f1, f2, obuf, lc = (ac4_[pp4], zc4_[pp4], f1_[pp4], f2_[pp4], obuf_[pp4], lc_[pp4])
            em.dma(ac4[:, :], acc_scr[tc], reads=["ac4" + str(pp4)], writes=["ac4w" + str(pp4)], tag=f"lda4{tc}")
            em.dma(zc4[:, :], z_scr[tc], reads=["zc4" + str(pp4)], writes=["zc4w" + str(pp4)], tag=f"ldz4{tc}")
            em.op("dve", lambda e, ac4=ac4, zc4=zc4, f1=f1, f2=f2, obuf=obuf, lc=lc: e.tensor_reduce(lc[:, 0:1], ac4[:, :], AX.X, ALU.add),
                  reads=["ac4w" + str(pp4)], writes=["lc0" + str(pp4)], tag=f"s4.{tc}", force=True)
            em.op("act", lambda e, ac4=ac4, zc4=zc4, f1=f1, f2=f2, obuf=obuf, lc=lc: e.activation(trash5[:, :], ac4[:, :], AF.Square,
                                                accum_out=lc[:, 1:2]),
                  reads=["ac4w" + str(pp4)], writes=["trash5", "lc1" + str(pp4)], tag=f"q4.{tc}", force=True)
            em.op("dve", lambda e, ac4=ac4, zc4=zc4, f1=f1, f2=f2, obuf=obuf, lc=lc: e.tensor_scalar_mul(lc[:, 2:3], lc[:, 0:1], 1.0 / D),
                  reads=["lc0" + str(pp4)], writes=["lc2" + str(pp4)], tag=f"mn4.{tc}", force=True)
            em.op("dve", lambda e, ac4=ac4, zc4=zc4, f1=f1, f2=f2, obuf=obuf, lc=lc: e.tensor_tensor(lc[:, 3:4], lc[:, 2:3], lc[:, 2:3],
                                                   ALU.mult),
                  reads=["lc2" + str(pp4)], writes=["lc3" + str(pp4)], tag=f"m24.{tc}", force=True)
            em.op("dve", lambda e, ac4=ac4, zc4=zc4, f1=f1, f2=f2, obuf=obuf, lc=lc: e.tensor_scalar(lc[:, 4:5], lc[:, 1:2], 1.0 / D,
                                                   lc[:, 3:4], ALU.mult, ALU.subtract),
                  reads=["lc1" + str(pp4), "lc3" + str(pp4)], writes=["lc4" + str(pp4)], tag=f"v4.{tc}", force=True)
            em.op("act", lambda e, ac4=ac4, zc4=zc4, f1=f1, f2=f2, obuf=obuf, lc=lc: e.activation(lc[:, 5:6], lc[:, 4:5], AF.Ln,
                                                bias=cst[:, 0:1]),
                  reads=["lc4" + str(pp4), "cst"], writes=["lc5" + str(pp4)], tag=f"l4.{tc}", force=True)
            em.op("act", lambda e, ac4=ac4, zc4=zc4, f1=f1, f2=f2, obuf=obuf, lc=lc: e.activation(lc[:, 6:7], lc[:, 5:6], AF.Exp,
                                                scale=cst[:, 2:3]),
                  reads=["lc5" + str(pp4), "cst"], writes=["lc6" + str(pp4)], tag=f"r4.{tc}", force=True)
            em.op("dve", lambda e, ac4=ac4, zc4=zc4, f1=f1, f2=f2, obuf=obuf, lc=lc: e.tensor_scalar(
                f1[:, :], ac4[:, :], lc[:, 2:3], lc[:, 6:7], ALU.subtract, ALU.mult),
                reads=["ac4w" + str(pp4), "lc2" + str(pp4), "lc6" + str(pp4)], writes=["f1", "ac4" + str(pp4)], tag=f"n4.{tc}")
            em.op("pool", lambda e, f1=f1, f2=f2: e.tensor_mul(f2[:, :], f1[:, :], gbc[:, :]),
                  reads=["f1", "gbc"], writes=["f2"], tag=f"g4.{tc}")
            em.op("pool", lambda e, zc4=zc4, f1=f1: e.tensor_add(f1[:, :], zc4[:, :], bbc[:, :]),
                  reads=["zc4w" + str(pp4), "bbc"], writes=["f1", "zc4" + str(pp4)], tag=f"zb4.{tc}")
            em.op("dve", lambda e, ac4=ac4, zc4=zc4, f1=f1, f2=f2, obuf=obuf, lc=lc: e.scalar_tensor_tensor(
                f2[:, :], f2[:, :], 1.0, f1[:, :], ALU.mult, ALU.add,
                accum_out=lc[:, 7:8]),
                reads=["f1", "f2"], writes=["f2", "lc7" + str(pp4)], tag=f"z2.{tc}", force=True)
            em.op("act", lambda e, ac4=ac4, zc4=zc4, f1=f1, f2=f2, obuf=obuf, lc=lc: e.activation(trash5[:, :], f2[:, :], AF.Square,
                                                accum_out=lc[:, 8:9]),
                  reads=["f2"], writes=["trash5", "lc8" + str(pp4)], tag=f"q5.{tc}", force=True)
            em.op("dve", lambda e, ac4=ac4, zc4=zc4, f1=f1, f2=f2, obuf=obuf, lc=lc: e.tensor_scalar_mul(lc[:, 2:3], lc[:, 7:8], 1.0 / D),
                  reads=["lc7" + str(pp4)], writes=["lc2" + str(pp4)], tag=f"mn5.{tc}", force=True)
            em.op("dve", lambda e, ac4=ac4, zc4=zc4, f1=f1, f2=f2, obuf=obuf, lc=lc: e.tensor_tensor(lc[:, 3:4], lc[:, 2:3], lc[:, 2:3],
                                                   ALU.mult),
                  reads=["lc2" + str(pp4)], writes=["lc3" + str(pp4)], tag=f"m25.{tc}", force=True)
            em.op("dve", lambda e, ac4=ac4, zc4=zc4, f1=f1, f2=f2, obuf=obuf, lc=lc: e.tensor_scalar(lc[:, 4:5], lc[:, 8:9], 1.0 / D,
                                                   lc[:, 3:4], ALU.mult, ALU.subtract),
                  reads=["lc8" + str(pp4), "lc3" + str(pp4)], writes=["lc4" + str(pp4)], tag=f"v5.{tc}", force=True)
            em.op("act", lambda e, ac4=ac4, zc4=zc4, f1=f1, f2=f2, obuf=obuf, lc=lc: e.activation(lc[:, 5:6], lc[:, 4:5], AF.Ln,
                                                bias=cst[:, 0:1]),
                  reads=["lc4" + str(pp4), "cst"], writes=["lc5" + str(pp4)], tag=f"l5.{tc}", force=True)
            em.op("act", lambda e, ac4=ac4, zc4=zc4, f1=f1, f2=f2, obuf=obuf, lc=lc: e.activation(lc[:, 6:7], lc[:, 5:6], AF.Exp,
                                                scale=cst[:, 2:3]),
                  reads=["lc5" + str(pp4), "cst"], writes=["lc6" + str(pp4)], tag=f"r5.{tc}", force=True)
            em.op("dve", lambda e, ac4=ac4, zc4=zc4, f1=f1, f2=f2, obuf=obuf, lc=lc: e.tensor_scalar(
                obuf[:, :], f2[:, :], lc[:, 2:3], lc[:, 6:7], ALU.subtract, ALU.mult),
                reads=["f2", "lc2" + str(pp4), "lc6" + str(pp4)], writes=["obuf" + str(pp4)], tag=f"o.{tc}")
            em.dma(out[tc * Q:(tc + 1) * Q, :], obuf[:, :], q="qact",
                   reads=["obuf" + str(pp4)], writes=["obuf" + str(pp4)], tag=f"out{tc}")
        close_scope()

    em.fence()
    em.op("sp", lambda e: e.nop(), tag="final")
    em.emit(nc)
    for h in reversed(persist):
        h.__exit__(None, None, None)
    return nc


_NC_CACHE = None


def _get_nc():
    global _NC_CACHE
    if _NC_CACHE is None:
        nc = bass.Bass("TRN2", target_bir_lowering=False)
        build(nc)
        _NC_CACHE = nc
    return _NC_CACHE


def make_in_maps(inputs):
    x = np.asarray(inputs["L0_attn_hidden_states"], dtype=np.float32)
    B = x.shape[0]
    shared = {
        "winT": np.ascontiguousarray(np.asarray(inputs["L0_attn_in_proj_w"]).T),
        "woutT": np.ascontiguousarray(np.asarray(inputs["L0_attn_out_proj_w"]).T),
        "w_up": np.ascontiguousarray(np.asarray(inputs["L0_ffn_up_w"])),
        "w_down": np.ascontiguousarray(np.asarray(inputs["L0_ffn_down_w"])),
        "w_fc1": np.ascontiguousarray(np.asarray(inputs["L0_ffn_fc1_w"])),
        "w_fc2": np.ascontiguousarray(np.asarray(inputs["L0_ffn_fc2_w"])),
        "w_exp": np.ascontiguousarray(np.asarray(inputs["L0_ffn_expert_w"])),
        "w_gate": np.ascontiguousarray(np.asarray(inputs["L0_ffn_gate_w"])),
        "convw": np.ascontiguousarray(
            np.asarray(inputs["L0_attn_conv_weight"]).reshape(48, 128, 4).transpose(1, 0, 2)),
        "convb": np.ascontiguousarray(
            np.asarray(inputs["L0_attn_conv_bias"]).reshape(48, 128).T),
        "dtbias": np.ascontiguousarray(
            np.asarray(inputs["L0_attn_dt_bias"]).reshape(NH, 1)),
        "alog": np.ascontiguousarray(np.asarray(inputs["L0_attn_A_log"]).reshape(NH, 1)),
        "dvec": np.ascontiguousarray(np.asarray(inputs["L0_attn_D"])),
        "nw": np.ascontiguousarray(
            np.asarray(inputs["L0_attn_norm_weight"]).reshape(32, 128).T),
        "gam": np.ascontiguousarray(np.asarray(inputs["L0_ffn_ln_gamma"])),
        "bet": np.ascontiguousarray(np.asarray(inputs["L0_ffn_ln_beta"])),
    }
    in_maps = []
    for b in range(B):
        m = dict(shared)
        m["x"] = np.ascontiguousarray(x[b])
        m["xT"] = np.ascontiguousarray(x[b].T)
        in_maps.append(m)
    return in_maps


def kernel(**inputs):
    in_maps = make_in_maps(inputs)
    nc = _get_nc()
    res = run_bass_kernel_spmd(nc, in_maps, core_ids=list(range(len(in_maps))))
    return np.stack([np.asarray(r["out"], dtype=np.float32) for r in res.results])


# version 08
# speedup vs baseline: 1.0190x; 1.0190x over previous
import numpy as np
import ml_dtypes
import concourse.bass as bass
import concourse.mybir as mybir
from concourse.bass_utils import run_bass_kernel_spmd

F32 = mybir.dt.float32
BF16 = mybir.dt.bfloat16
AF = mybir.ActivationFunctionType
ALU = mybir.AluOpType
AX = mybir.AxisListType

DEBUG = False
STAGES = (1, 1, 1, 1)

T = 2048
D = 4096
NH, HP, SN, NG = 64, 64, 128, 8
I_SSM = NH * HP
GN = NG * SN
CONV = I_SSM + 2 * GN
PROJ = I_SSM + CONV + NH
Q = 128
NCH = T // Q
INT, LAT, NE = 8192, 2048, 8
EPS = 1e-5

ENGINES = ("pe", "act", "dve", "pool", "sp")


class Em:

    def __init__(self):
        self.ops = []
        self.tok = {}
        self.fence_ids = set()
        self.dma_counts = {"qsp": 0, "qact": 0}

    def op(self, eng, fn, reads=(), writes=(), dma_q=None, tag="", force=False):
        oid = len(self.ops)
        deps = set()
        for t in reads:
            st = self.tok.get(t)
            if st and st["w"] is not None:
                deps.add(st["w"])
        for t in writes:
            st = self.tok.get(t)
            if st:
                if st["w"] is not None:
                    deps.add(st["w"])
                deps.update(st["r"])
        deps.update(self.fence_ids)
        deps.discard(oid)
        if deps:
            best = {}
            for d in deps:
                p = self.ops[d]
                key = ("q", p["dma_q"]) if p["dma_q"] is not None else ("e", p["eng"])
                if key not in best or d > best[key]:
                    best[key] = d
            deps = set(best.values())
        dma_idx = None
        if dma_q is not None:
            self.dma_counts[dma_q] += 1
            dma_idx = self.dma_counts[dma_q]
        self.ops.append({"id": oid, "eng": eng, "fn": fn, "deps": deps,
                         "dma_q": dma_q, "dma_idx": dma_idx, "tag": tag,
                         "force": force})
        for t in reads:
            self.tok.setdefault(t, {"w": None, "r": []})["r"].append(oid)
        for t in writes:
            self.tok[t] = {"w": oid, "r": []}
        return oid

    def dma(self, dst, src, q="qsp", reads=(), writes=(), tag="", **kw):
        eng = "sp" if q == "qsp" else "act"
        return self.op(eng, lambda e, d=dst, s=src: e.dma_start(d, s, **kw),
                       reads=reads, writes=writes, dma_q=q, tag=tag)

    def fence(self):
        last_eng = {}
        last_dma = {}
        for o in self.ops:
            if o["dma_q"] is not None:
                last_dma[o["dma_q"]] = o["id"]
            else:
                last_eng[o["eng"]] = o["id"]
        self.fence_ids = set(last_eng.values()) | set(last_dma.values())

    @staticmethod
    def _elide(p, c):
        if c["dma_q"] is not None or p.get("force"):
            return False
        return p["dma_q"] is None and p["eng"] == c["eng"]

    def emit(self, nc):
        ops = self.ops
        needed = set()
        for o in ops:
            for d in o["deps"]:
                if not self._elide(ops[d], o):
                    needed.add(d)
        ms = {e: 0 for e in ENGINES}
        for o in ops:
            if o["dma_q"] is not None:
                o["ms"] = (o["dma_q"], 16 * o["dma_idx"])
            elif o["id"] in needed:
                ms[o["eng"]] += 1
                o["ms"] = (o["eng"], ms[o["eng"]])
            else:
                o["ms"] = None
        clocks = {e: {} for e in ENGINES}
        per_eng = {e: [] for e in ENGINES}
        for o in ops:
            c = clocks[o["eng"]]
            waits = []
            for d in sorted(o["deps"]):
                p = ops[d]
                if self._elide(p, o):
                    continue
                sem, val = p["ms"]
                if c.get(sem, 0) < val:
                    waits.append((sem, val))
                    c[sem] = val
            o["waits"] = waits
            per_eng[o["eng"]].append(o)

        sems = {}
        hs = []
        for name in ("pe", "act", "dve", "pool", "qsp", "qact"):
            h = nc.semaphore("sem_" + name)
            sems[name] = h.__enter__()
            hs.append(h)

        def run(engine_obj, eng_name):
            for o in per_eng[eng_name]:
                for (sem, val) in o["waits"]:
                    engine_obj.wait_ge(sems[sem], val)
                inst = o["fn"](engine_obj)
                if o["dma_q"] is not None:
                    inst.then_inc(sems[o["dma_q"]], 16)
                elif o["id"] in needed:
                    inst.then_inc(sems[o["eng"]], 1)

        with nc.Block() as blk:
            @blk.tensor
            def _(e):
                run(e, "pe")

            @blk.scalar
            def _(e):
                run(e, "act")

            @blk.vector
            def _(e):
                run(e, "dve")

            @blk.gpsimd
            def _(e):
                run(e, "pool")

            @blk.sync
            def _(e):
                run(e, "sp")
        for h in reversed(hs):
            h.__exit__(None, None, None)


def strided(base_ap, pattern):
    return bass.AP(base_ap.tensor, base_ap.offset, pattern)


def build(nc):
    em = Em()

    def din(name, shape, dt=F32):
        return nc.dram_tensor(name, shape, dt, kind="ExternalInput").ap()

    def dscr(name, shape, dt=F32):
        kind = "ExternalOutput" if DEBUG else "Internal"
        return nc.dram_tensor(name, shape, dt, kind=kind).ap()

    xT_in = din("xT", [D, T])
    x_in = din("x", [T, D])
    winT = din("winT", [D, PROJ])
    woutT = din("woutT", [I_SSM, D])
    w_up = din("w_up", [D, INT])
    w_down = din("w_down", [INT, D])
    w_fc1 = din("w_fc1", [D, LAT])
    w_fc2 = din("w_fc2", [LAT, D])
    w_exp = din("w_exp", [LAT, LAT])
    w_gate = din("w_gate", [D, NE])
    convw = din("convw", [128, 48, 4])
    convb = din("convb", [128, 48])
    dtbias = din("dtbias", [NH, 1])
    alog = din("alog", [NH, 1])
    dvec = din("dvec", [NH])
    nw_in = din("nw", [128, 32])
    gam_in = din("gam", [D])
    bet_in = din("bet", [D])
    out = nc.dram_tensor("out", [T, D], F32, kind="ExternalOutput").ap()

    gate_scr = dscr("gate_scr", [32, 128, T], BF16)
    xbc_scr = dscr("xbc_scr", [48, 128, T], BF16)
    ynT_scr = dscr("ynT_scr", [32, 128, T], BF16)
    attn_scr = dscr("attn_scr", [NCH, 128, D])
    z_scr = dscr("z_scr", [NCH, 128, D])
    zT_scr = dscr("zT_scr", [32, 128, T], BF16)
    s_scr = dscr("s_scr", [NH, Q])
    r_scr = dscr("r_scr", [1, T])
    xlat_scr = dscr("xlat_scr", [16, 128, T], BF16)
    h1_scr = dscr("h1_scr", [64, 128, T], BF16)
    wdb_scr = dscr("wdb_scr", [64, 128, D], BF16)
    routed_scr = dscr("routed_scr", [NCH, 128, D])
    acc_scr = dscr("acc_scr", [NCH, 128, D])
    if DEBUG:
        dt_dbg = dscr("dt_dbg", [NH, T])
        dbg_dtsT = dscr("dbg_dtsT", [128, 128])
        dbg_lam = dscr("dbg_lam", [128, NH * Q], BF16)
        dbg_xnat = dscr("dbg_xnat", [128, I_SSM], BF16)
        dbg_xdt = dscr("dbg_xdt", [128, 512], BF16)
        dbg_gam = dscr("dbg_gam", [128, 8 * Q], BF16)
        dbg_yg = dscr("dbg_yg", [128, 512])
        dbg_y1 = dscr("dbg_y1", [128, 512])
        dbg_sb = dscr("dbg_sb", [128, I_SSM], BF16)
        dbg_yn16 = dscr("dbg_yn16", [128, 512], BF16)
        dbg_gcol = dscr("dbg_gcol", [128, 4])
        dbg_ynTb = dscr("dbg_ynTb", [128, 32 * 256], BF16)

    persist = []

    def sbp(name, shape, dt=F32):
        h = nc.sbuf_tensor(name, shape, dt)
        t = h.__enter__()
        persist.append(h)
        return t

    scope_hs = []

    def sbs(name, shape, dt=F32):
        h = nc.sbuf_tensor(name, shape, dt)
        t = h.__enter__()
        scope_hs.append(h)
        return t

    def pss(name, shape, dt=F32):
        h = nc.psum_tensor(name, shape, dt)
        t = h.__enter__()
        scope_hs.append(h)
        return t

    def close_scope():
        em.fence()
        for h in reversed(scope_hs):
            h.__exit__(None, None, None)
        scope_hs.clear()

    dt_sb = sbp("dt_sb", [NH, T])
    ident = sbp("ident", [128, 128], BF16)
    identf = sbp("identf", [128, 128], F32)
    convw_sb = sbp("convw_sb", [128, 48 * 4])
    convb_sb = sbp("convb_sb", [128, 48])
    small = sbp("small", [NH, 8])
    dbcp = sbp("dbcp", [128, NH])
    cst = sbp("cst", [128, 4])

    em.op("dve", lambda e: e.memset(cst[:, 0:1], EPS), writes=["cst"])
    em.op("dve", lambda e: e.memset(cst[:, 1:2], 1.0 / 512), writes=["cst"])
    em.op("dve", lambda e: e.memset(cst[:, 2:3], -0.5), writes=["cst"])
    em.op("pool", lambda e: e.memset(ident[:, :], 1.0), writes=["ident"])
    em.op("pool", lambda e: e.affine_select(ident[:, :], ident[:, :], [[1, 128]],
                                            ALU.is_equal, 0.0, channel_multiplier=-1),
          reads=["ident"], writes=["ident"])
    em.op("pool", lambda e: e.memset(identf[:, :], 1.0), writes=["identf"])
    em.op("pool", lambda e: e.affine_select(identf[:, :], identf[:, :], [[1, 128]],
                                            ALU.is_equal, 0.0, channel_multiplier=-1),
          reads=["identf"], writes=["identf"])
    em.dma(convw_sb[:, :], convw.rearrange("p a k -> p (a k)"), writes=["convw"])
    em.dma(convb_sb[:, :], convb, writes=["convb"])
    em.dma(small[:, 0:1], dtbias, writes=["small0"])
    em.dma(small[:, 1:2], alog, writes=["small1"])
    em.dma(dbcp[:, :], strided(dvec, [[0, 128], [1, NH]]), writes=["dbcp"])
    em.op("act", lambda e: e.activation(small[:, 2:3], small[:, 1:2], AF.Exp),
          reads=["small1"], writes=["smallA"])
    em.op("dve", lambda e: e.tensor_scalar_mul(small[:, 2:3], small[:, 2:3], -1.0),
          reads=["smallA"], writes=["smallA"])

    if STAGES[0]:
        xTb = sbs("xTb", [128, 32 * T], BF16)
        wstg = [sbs("wstg0", [128, 32 * 128])]
        wb16 = [sbs(f"wb16{i}", [128, 32 * 128], BF16) for i in range(2)]
        hpad = sbs("hpad", [128, 3 + T])
        accA = sbs("accA", [128, T])
        accB = sbs("accB", [128, T])
        xstg = [accA, accB]
        xout = [sbs("xout0", [128, T], BF16)]
        pbank = [pss(f"pbA{i}", [128, 512]) for i in range(8)]

        em.op("dve", lambda e: e.memset(hpad[:, 0:3], 0.0), writes=["hpad"])

        XSTOK = ["accA", "accB"]
        for dt_i in range(32):
            b = dt_i % 2
            em.dma(xstg[b][:, :], xT_in[dt_i * 128:(dt_i + 1) * 128, :],
                   writes=[XSTOK[b]], tag=f"ld_xT{dt_i}")
            em.op("act", lambda e, b=b, dt_i=dt_i: e.copy(
                xTb[:, dt_i * T:(dt_i + 1) * T], xstg[b][:, :]),
                reads=[XSTOK[b]], writes=[f"xTb{dt_i}"], tag=f"cv_xT{dt_i}")

        CT_ORDER = list(range(32, 81)) + list(range(32))

        def rec_w(cj):
            ctj = CT_ORDER[cj]
            cwj = 64 if ctj == 80 else 128
            bj = cj % 2
            em.dma(wstg[0][:, 0:32 * cwj].rearrange("p (a c) -> p a c", a=32),
                   winT[:, ctj * 128:ctj * 128 + cwj].rearrange("(a p) c -> p a c", p=128),
                   writes=["wstg0"], tag=f"ld_w{ctj}")
            em.op("act", lambda e, bj=bj, cwj=cwj: e.copy(
                wb16[bj][:, 0:32 * cwj], wstg[0][:, 0:32 * cwj]),
                reads=["wstg0"], writes=[f"wb16{bj}"], tag=f"cv_w{ctj}")

        for ci, ct in enumerate(CT_ORDER):
            cw = 64 if ct == 80 else 128
            b = ci % 2
            bank0 = 4 * (ci % 2)
            if ci == 0:
                rec_w(0)
            if ci + 1 < len(CT_ORDER):
                rec_w(ci + 1)
            for t in range(4):
                bank = pbank[bank0 + t]
                btok = f"pbA{bank0 + t}"
                for d in range(32):
                    first, last = (d == 0), (d == 31)
                    oap = bank[:, :] if cw == 128 else bank[0:64, :]
                    em.op("pe", lambda e, b=b, cw=cw, t=t, d=d, oap=oap:
                          e.matmul(oap, wb16[b][:, d * cw:(d + 1) * cw],
                                   xTb[:, d * T + t * 512:d * T + (t + 1) * 512],
                                   start=(d == 0), stop=(d == 31)),
                          reads=[f"wb16{b}", f"xTb{d}"],
                          writes=[btok] if (first or last) else [],
                          tag=f"mmA{ct}.{t}.{d}")
            if ct < 32:
                xo = xout[0]
                for t in range(4):
                    btok = f"pbA{bank0 + t}"
                    em.op("act", lambda e, t=t, bank=pbank[bank0 + t], xo=xo:
                          e.activation(xo[:, t * 512:(t + 1) * 512], bank[:, :], AF.Silu),
                          reads=[btok], writes=["xout0", btok], tag=f"siluG{ct}.{t}")
                em.dma(gate_scr[ct], xo[:, :], q="qact",
                       reads=["xout0"], tag=f"spG{ct}")
            elif ct < 80:
                j = ct - 32
                for t in range(4):
                    btok = f"pbA{bank0 + t}"
                    em.op("dve", lambda e, t=t, bank=pbank[bank0 + t]:
                          e.tensor_copy(hpad[:, 3 + t * 512:3 + (t + 1) * 512], bank[:, :]),
                          reads=[btok], writes=["hpad", btok], tag=f"evH{ct}.{t}")
                em.op("dve", lambda e, j=j: e.tensor_scalar(
                    accA[:, :], hpad[:, 3:3 + T], convw_sb[:, j * 4 + 3:j * 4 + 4],
                    None, ALU.mult),
                    reads=["hpad", "convw"], writes=["accA"], tag=f"cv3.{j}")
                em.op("dve", lambda e, j=j: e.scalar_tensor_tensor(
                    accB[:, :], hpad[:, 2:2 + T], convw_sb[:, j * 4 + 2:j * 4 + 3],
                    accA[:, :], ALU.mult, ALU.add),
                    reads=["hpad", "convw", "accA"], writes=["accB"], tag=f"cv2.{j}")
                em.op("dve", lambda e, j=j: e.scalar_tensor_tensor(
                    accA[:, :], hpad[:, 1:1 + T], convw_sb[:, j * 4 + 1:j * 4 + 2],
                    accB[:, :], ALU.mult, ALU.add),
                    reads=["hpad", "convw", "accB"], writes=["accA"], tag=f"cv1.{j}")
                em.op("dve", lambda e, j=j: e.scalar_tensor_tensor(
                    accB[:, :], hpad[:, 0:0 + T], convw_sb[:, j * 4 + 0:j * 4 + 1],
                    accA[:, :], ALU.mult, ALU.add),
                    reads=["hpad", "convw", "accA"], writes=["accB"], tag=f"cv0.{j}")
                xo = xout[0]
                em.op("act", lambda e, j=j, xo=xo: e.activation(
                    xo[:, :], accB[:, :], AF.Silu, bias=convb_sb[:, j:j + 1]),
                    reads=["accB", "convb"], writes=["xout0"], tag=f"siluX{j}")
                em.dma(xbc_scr[j], xo[:, :], q="qact",
                       reads=["xout0"], tag=f"spX{j}")
            else:
                for t in range(4):
                    btok = f"pbA{bank0 + t}"
                    em.op("dve", lambda e, t=t, bank=pbank[bank0 + t]:
                          e.tensor_copy(dt_sb[:, t * 512:(t + 1) * 512], bank[0:64, :]),
                          reads=[btok], writes=["dt_sb", btok], tag=f"evDT{t}")
        if DEBUG:
            em.dma(dt_dbg, dt_sb[:, :], q="qact", reads=["dt_sb"])
        close_scope()

    rbc = sbp("rbc", [128, T])
    route_cols = sbp("route_cols", [128, NCH])

    if STAGES[1]:
        xbcC = [sbs("xbcC0", [128, 48 * 256], BF16)]
        gateC = [sbs("gateC0", [128, 32 * 256], BF16)]
        S_f32 = sbs("S_f32", [128, I_SSM])
        S_b16 = sbs("S_b16", [128, I_SSM], BF16)
        bcast = sbs("bcast", [128, NH * Q])
        tb = bcast[0:64, 0:T]
        tb2 = bcast[0:64, T:2 * T]
        dtp = bcast[0:64, 2 * T:3 * T]
        lamT = sbs("lamT", [128, NH * Q], BF16)
        LcumF = sbs("LcumF", [128, 128])
        Abc = sbs("Abc", [128, NH])
        onesF = sbs("onesF", [1, 128])
        dtpT_all = sbs("dtpT_all", [128, NCH * NH])
        aT_all = sbs("aT_all", [128, NCH * NH])
        sT_all = sbs("sT_all", [128, NCH * NH])
        s_q = sbs("s_q", [NH, 512])
        slast_row = sbs("slast_row", [1, NCH * NH])
        edecSarg = sbs("edecSarg", [128, NCH * NH])
        edecS_all = sbs("edecS_all", [128, NCH * NH])
        expS_all = aT_all
        X_nat = sbs("X_nat", [128, I_SSM], BF16)
        xdt_g = [sbs(f"xdt_g{i}", [128, 512], BF16) for i in range(2)]
        xd2_g = [sbs(f"xd2_g{i}", [128, 512], BF16) for i in range(2)]
        gam_g = [sbs(f"gam_g{i}", [128, 8 * Q], BF16) for i in range(2)]
        bn_g = [sbs(f"bn_g{i}", [128, 128], BF16) for i in range(2)]
        ytmp = sbs("ytmp", [128, 512])
        xdterm = [sbs(f"xdterm{i}", [128, 512]) for i in range(2)]
        yg = [sbs(f"yg{i}", [128, 512]) for i in range(2)]
        gcol = sbs("gcol", [128, 4])
        yn16 = [sbs(f"yn16{i}", [128, 512], BF16) for i in range(2)]
        stmp = [sbs(f"stmpB{i}", [128, 512]) for i in range(2)]
        ynTb = sbs("ynTb", [128, 32 * 256], BF16)
        trashB = sbs("trashB", [128, 512])
        p_cb = [pss("p_cb0", [128, 512])]
        p_y1 = [pss(f"p_y1{i}", [128, 512]) for i in range(2)]
        p_y2 = [pss(f"p_y2{i}", [128, 512]) for i in range(2)]
        p_ds = pss("p_ds", [128, 512])
        p_tr = [pss(f"p_tr{i}", [128, 1024], BF16) for i in range(2)]
        p_st = p_y1[0]

        a_dram = dscr("a_dram", [NH, 1])
        s_scr_all = dscr("s_scr_all", [NCH, NH, Q])

        em.op("pool", lambda e: e.memset(LcumF[:, :], 1.0), writes=["LcumF"])
        em.op("pool", lambda e: e.affine_select(LcumF[:, :], LcumF[:, :], [[1, 128]],
                                                ALU.is_ge, 0.0, channel_multiplier=-1),
              reads=["LcumF"], writes=["LcumF"])
        em.op("dve", lambda e: e.memset(onesF[:, :], 1.0), writes=["onesF"])

        em.op("dve", lambda e: e.tensor_scalar(tb, dt_sb[:, :],
                                               small[:, 0:1], None, ALU.add),
              reads=["dt_sb", "small0", "bcast"], writes=["tbB", "bcast"], tag="sp_add")
        em.op("act", lambda e: e.activation(tb2, tb, AF.Abs),
              reads=["tbB", "bcast"], writes=["tb2B", "bcast"], tag="sp_abs")
        em.op("act", lambda e: e.activation(tb2, tb2, AF.Exp, scale=-1.0),
              reads=["tb2B"], writes=["tb2B"], tag="sp_exp")
        em.op("act", lambda e: e.activation(tb2, tb2, AF.Ln, bias=1.0),
              reads=["tb2B"], writes=["tb2B"], tag="sp_ln")
        em.op("dve", lambda e: e.tensor_relu(dtp, tb),
              reads=["tbB"], writes=["dtpB", "bcast"], tag="sp_relu")
        em.op("dve", lambda e: e.tensor_add(dtp, dtp, tb2),
              reads=["dtpB", "tb2B"], writes=["dtpB", "bcast"], tag="sp_sum")

        em.dma(a_dram, small[:, 2:3], reads=["smallA"], writes=["a_dram"], tag="sp_aw")
        em.dma(Abc[:, :], strided(a_dram, [[0, 128], [1, NH]]),
               reads=["a_dram"], writes=["Abc"], tag="sp_ar")

        for half in range(2):
            for j in range(8):
                c = half * 8 + j
                em.op("pe", lambda e, c=c, j=j: e.matmul(
                    p_st[:, j * 64:(j + 1) * 64], dtp[:, c * Q:(c + 1) * Q],
                    identf[0:64, 0:64], start=True, stop=True),
                    reads=["dtpB", "identf", "p_y10"],
                    writes=["p_y10"] if j in (0, 7) else [], tag=f"trdtp{c}")
            em.op("dve", lambda e, half=half: e.tensor_copy(
                dtpT_all[:, half * 512:(half + 1) * 512], p_st[:, :]),
                reads=["p_y10"], writes=["dtpT", "p_y10"], tag=f"evdtp{half}")
        em.op("dve", lambda e: e.tensor_tensor(
            aT_all[:, :].rearrange("p (c h) -> p c h", c=NCH),
            dtpT_all[:, :].rearrange("p (c h) -> p c h", c=NCH),
            strided(Abc[:, :], [[NH, 128], [0, NCH], [1, NH]]), ALU.mult),
            reads=["dtpT", "Abc"], writes=["aT_all"], tag="aTmul")
        for half in range(2):
            for j in range(8):
                c = half * 8 + j
                em.op("pe", lambda e, c=c, j=j: e.matmul(
                    p_st[:, j * 64:(j + 1) * 64], LcumF[:, :],
                    aT_all[:, c * NH:(c + 1) * NH], start=True, stop=True),
                    reads=["LcumF", "aT_all", "p_y10"],
                    writes=["p_y10"] if j in (0, 7) else [], tag=f"sT{c}")
            em.op("dve", lambda e, half=half: e.tensor_copy(
                sT_all[:, half * 512:(half + 1) * 512], p_st[:, :]),
                reads=["p_y10"], writes=["sT_all", "p_y10"], tag=f"evsT{half}")
        for qtr in range(4):
            for j in range(4):
                c = qtr * 4 + j
                em.op("pe", lambda e, c=c, j=j: e.matmul(
                    p_st[0:64, j * 128:(j + 1) * 128],
                    aT_all[:, c * NH:(c + 1) * NH], LcumF[:, :],
                    start=True, stop=True),
                    reads=["aT_all", "LcumF", "p_y10"],
                    writes=["p_y10"] if j in (0, 3) else [], tag=f"sN{c}")
            em.op("dve", lambda e, qtr=qtr: e.tensor_copy(
                s_q[:, :], p_st[0:64, :]),
                reads=["p_y10", "s_qT"], writes=["s_q", "p_y10"], tag=f"evsN{qtr}")
            em.dma(bass.AP(s_scr_all.tensor, s_scr_all.offset + qtr * 4 * NH * Q,
                           [[Q, NH], [NH * Q, 4], [1, Q]]),
                   s_q[:, :].rearrange("h (c t) -> h c t", c=4),
                   reads=["s_q"], writes=["s_qT", "s_scr"], tag=f"sp_s{qtr}")
        em.dma(slast_row[0:1, :], sT_all[127:128, :],
               reads=["sT_all"], writes=["slast"], tag="slast")
        for j in range(2):
            em.op("pe", lambda e, j=j: e.matmul(
                p_st[:, :], onesF[0:1, :], slast_row[0:1, j * 512:(j + 1) * 512],
                start=True, stop=True),
                reads=["onesF", "slast", "p_y10"], writes=["p_y10"], tag=f"r1s{j}")
            em.op("dve", lambda e, j=j: e.tensor_copy(
                edecSarg[:, j * 512:(j + 1) * 512], p_st[:, :]),
                reads=["p_y10"], writes=["edecSarg", "p_y10"], tag=f"evr1{j}")
        em.op("act", lambda e: e.activation(edecS_all[:, :], edecSarg[:, :], AF.Exp),
              reads=["edecSarg"], writes=["edecS_all"], tag="edecS")
        em.op("act", lambda e: e.activation(expS_all[:, :], sT_all[:, :], AF.Exp),
              reads=["sT_all"], writes=["expS_all", "aT_all"], tag="expS")
        em.op("dve", lambda e: e.tensor_sub(edecSarg[:, :], edecSarg[:, :], sT_all[:, :]),
              reads=["edecSarg", "sT_all"], writes=["edecSarg"], tag="edTa")
        em.op("act", lambda e: e.activation(edecSarg[:, :], edecSarg[:, :], AF.Exp),
              reads=["edecSarg"], writes=["edT_all"], tag="edTb")
        edT_all = edecSarg

        S_GRP = [f"Sf_{g}" for g in range(NG)]
        SB_GRP = [f"Sb_{g}" for g in range(NG)]
        em.op("dve", lambda e: e.memset(S_f32[:, :], 0.0), writes=S_GRP, tag="S0")
        em.op("dve", lambda e: e.memset(S_b16[:, :], 0.0), writes=SB_GRP, tag="Sb0")

        def rep8(buf, c, g, pstep=NCH * NH):
            base = buf[:, :]
            return bass.AP(base.tensor, base.offset + c * NH + g * 8,
                           [[pstep, 128], [1, 8], [0, HP]])

        def prefix(c):
            em.dma(bcast[:, :],
                   strided(bass.AP(s_scr_all.tensor, s_scr_all.offset + c * NH * Q,
                                   [[0, 128], [1, NH * Q]]), [[0, 128], [1, NH * Q]]),
                   reads=["s_scr"], writes=["bcast"], tag=f"bc{c}")

        def prefix2(c):
            em.op("dve", lambda e, c=c: e.tensor_tensor(
                bcast[:, :].rearrange("p (h t) -> p h t", h=NH),
                bcast[:, :].rearrange("p (h t) -> p h t", h=NH),
                bass.AP(sT_all[:, :].tensor, sT_all[:, :].offset + c * NH,
                        [[NCH * NH, 128], [1, NH], [0, Q]]),
                ALU.subtract),
                reads=["bcast", "sT_all"], writes=["bcast"], tag=f"arg{c}")

        def prefix3(c):
            em.op("pool", lambda e: e.affine_select(
                bcast[:, :].rearrange("p (h t) -> p h t", h=NH),
                bcast[:, :].rearrange("p (h t) -> p h t", h=NH),
                [[0, NH], [1, Q]], ALU.is_ge, -1e9, channel_multiplier=-1),
                reads=["bcast"], writes=["bcast"], tag=f"mask{c}")

        def prefix4(c):
            em.op("act", lambda e: e.activation(lamT[:, :], bcast[:, :], AF.Exp),
                  reads=["bcast"], writes=["lamT"], tag=f"lam{c}")

        pend_yn = []

        def flush_yn():
            while pend_yn:
                g, gb, toff, c = pend_yn.pop()
                for j in range(4):
                    em.op("pe", lambda e, j=j, gb=gb: e.transpose(
                        p_tr[1][:, j * 128:(j + 1) * 128],
                        yn16[gb][:, j * 128:(j + 1) * 128], ident[:, :]),
                        reads=[f"yn16{gb}", "ident", "p_tr1"],
                        writes=["p_tr1"] if j in (0, 3) else [], tag=f"trY{g}.{j}.{c}")
                em.op("dve", lambda e, g=g, toff=toff: e.tensor_tensor(
                    bass.AP(ynTb[:, :].tensor, ynTb[:, :].offset + 4 * g * 256 + toff,
                            [[32 * 256, 128], [256, 4], [1, 128]]),
                    p_tr[1][:, 0:512].rearrange("p (a t) -> p a t", a=4),
                    bass.AP(gateC[0][:, :].tensor,
                            gateC[0][:, :].offset + 4 * g * 256 + toff,
                            [[32 * 256, 128], [256, 4], [1, 128]]),
                    ALU.mult),
                    reads=["p_tr1", "gateC0", f"ynTb{g}"], writes=[f"ynTb{g}", "p_tr1"],
                    tag=f"evY{g}.{c}")

        prefix(0)
        prefix2(0)
        prefix3(0)
        prefix4(0)
        for c in range(NCH):
            toff = (c % 2) * 128
            if c % 2 == 0:
                flush_yn()
                em.dma(xbcC[0][:, :].rearrange("p (a t) -> p a t", a=48),
                       xbc_scr[:, :, c * Q:(c + 2) * Q].rearrange("a p t -> p a t"),
                       writes=["xbcC0"], tag=f"ldxbc{c}")
                em.dma(gateC[0][:, :].rearrange("p (a t) -> p a t", a=32),
                       gate_scr[:, :, c * Q:(c + 2) * Q].rearrange("a p t -> p a t"),
                       writes=["gateC0"], tag=f"ldg{c}")
            xbtok, gttok = "xbcC0", "gateC0"

            for ct4 in range(8):
                ptb = ct4 % 2
                for j in range(4):
                    ct = ct4 * 4 + j
                    em.op("pe", lambda e, ptb=ptb, j=j, ct=ct, toff=toff:
                          e.transpose(p_tr[ptb][:, j * 128:(j + 1) * 128],
                                      xbcC[0][:, ct * 256 + toff:ct * 256 + toff + 128],
                                      ident[:, :]),
                          reads=[xbtok, "ident", f"p_tr{ptb}"],
                          writes=[f"p_tr{ptb}"] if j in (0, 3) else [],
                          tag=f"trX{ct}.{c}")
                em.op("act", lambda e, ptb=ptb, ct4=ct4: e.copy(
                    X_nat[:, ct4 * 512:(ct4 + 1) * 512], p_tr[ptb][:, 0:512]),
                    reads=[f"p_tr{ptb}"], writes=["X_nat", f"p_tr{ptb}"],
                    tag=f"evX{ct4}.{c}")

            for g in range(NG):
                gb = g % 2
                cbb = 0
                cbt = "p_cb0"
                bslice = lambda g=g, toff=toff: xbcC[0][
                    :, (32 + g) * 256 + toff:(32 + g) * 256 + toff + 128]
                cslice = lambda g=g, toff=toff: xbcC[0][
                    :, (40 + g) * 256 + toff:(40 + g) * 256 + toff + 128]
                em.op("pe", lambda e, bs=bslice, cs=cslice, cbb=cbb: e.matmul(
                    p_cb[cbb][:, 0:128], bs(), cs(), start=True, stop=True),
                    reads=[xbtok, cbt], writes=[cbt], tag=f"cb{g}.{c}")
                em.op("pe", lambda e, bs=bslice: e.transpose(
                    p_tr[0][:, 0:128], bs(), ident[:, :]),
                    reads=[xbtok, "ident", "p_tr0"], writes=["p_tr0"], tag=f"trB{g}.{c}")
                em.op("act", lambda e, gb=gb: e.copy(bn_g[gb][:, :], p_tr[0][:, 0:128]),
                      reads=["p_tr0"], writes=[f"bn_g{gb}", "p_tr0"], tag=f"evB{g}.{c}")
                em.op("pool", lambda e, g=g, gb=gb, c=c: e.tensor_tensor(
                    xdt_g[gb][:, :].rearrange("p (h f) -> p h f", h=8),
                    X_nat[:, g * 512:(g + 1) * 512].rearrange("p (h f) -> p h f", h=8),
                    rep8(dtpT_all, c, g), ALU.mult),
                    reads=["X_nat", "dtpT"], writes=[f"xdt_g{gb}"], tag=f"xdt{g}.{c}")
                em.op("pool", lambda e, g=g, gb=gb, c=c: e.tensor_tensor(
                    xd2_g[gb][:, :].rearrange("p (h f) -> p h f", h=8),
                    xdt_g[gb][:, :].rearrange("p (h f) -> p h f", h=8),
                    rep8(edT_all, c, g), ALU.mult),
                    reads=[f"xdt_g{gb}", "edT_all"], writes=[f"xd2_g{gb}"], tag=f"xd2{g}.{c}")
                em.op("dve", lambda e, g=g, gb=gb, cbb=cbb: e.tensor_tensor(
                    gam_g[gb][:, :].rearrange("p (h t) -> p h t", h=8),
                    lamT[:, g * 8 * Q:(g + 1) * 8 * Q].rearrange("p (h t) -> p h t", h=8),
                    strided(p_cb[cbb][:, 0:128], [[512, 128], [0, 8], [1, 128]]),
                    ALU.mult),
                    reads=["lamT", cbt], writes=[f"gam_g{gb}", cbt], tag=f"gam{g}.{c}")
                for hh in range(8):
                    em.op("pe", lambda e, hh=hh, gb=gb: e.matmul(
                        p_y1[gb][:, hh * 64:(hh + 1) * 64],
                        gam_g[gb][:, hh * Q:(hh + 1) * Q],
                        xdt_g[gb][:, hh * 64:(hh + 1) * 64],
                        start=True, stop=True),
                        reads=[f"gam_g{gb}", f"xdt_g{gb}", f"p_y1{gb}"],
                        writes=[f"p_y1{gb}"] if hh in (0, 7) else [], tag=f"y1.{g}.{hh}.{c}")
                em.op("pe", lambda e, cs=cslice, g=g, gb=gb: e.matmul(
                    p_y2[gb][:, :], cs(), S_b16[:, g * 512:(g + 1) * 512],
                    start=True, stop=True),
                    reads=[xbtok, SB_GRP[g], f"p_y2{gb}"], writes=[f"p_y2{gb}"], tag=f"y2.{g}.{c}")
                em.op("pe", lambda e, gb=gb: e.matmul(
                    p_ds[:, :], bn_g[gb][:, :], xd2_g[gb][:, :], start=True, stop=True),
                    reads=[f"bn_g{gb}", f"xd2_g{gb}", "p_ds"], writes=["p_ds"],
                    tag=f"ds{g}.{c}")
                em.op("dve", lambda e, g=g, c=c, gb=gb: e.tensor_tensor(
                    ytmp[:, :].rearrange("p (h f) -> p h f", h=8),
                    p_y2[gb][:, :].rearrange("p (h f) -> p h f", h=8),
                    rep8(expS_all, c, g), ALU.mult),
                    reads=[f"p_y2{gb}", "expS_all"], writes=["ytmp", f"p_y2{gb}"],
                    tag=f"ytmp{g}.{c}")
                em.op("dve", lambda e, gb=gb: e.tensor_add(
                    ytmp[:, :], p_y1[gb][:, :], ytmp[:, :]),
                    reads=[f"p_y1{gb}", "ytmp"], writes=["ytmp", f"p_y1{gb}"],
                    tag=f"ycmb{g}.{c}")
                em.op("pool", lambda e, g=g, gb=gb: e.tensor_tensor(
                    xdterm[gb][:, :].rearrange("p (h f) -> p h f", h=8),
                    X_nat[:, g * 512:(g + 1) * 512].rearrange("p (h f) -> p h f", h=8),
                    strided(dbcp[:, g * 8:g * 8 + 8], [[NH, 128], [1, 8], [0, HP]]),
                    ALU.mult),
                    reads=["X_nat", "dbcp"], writes=[f"xdterm{gb}"], tag=f"xD{g}.{c}")
                em.op("dve", lambda e, gb=gb: e.tensor_add(
                    yg[gb][:, :], ytmp[:, :], xdterm[gb][:, :]),
                    reads=["ytmp", f"xdterm{gb}"], writes=[f"yg{gb}"], tag=f"ygv{g}.{c}")
                em.op("act", lambda e, gb=gb: e.activation(
                    trashB[:, :], yg[gb][:, :], AF.Square, accum_out=gcol[:, 0:1]),
                    reads=[f"yg{gb}"], writes=["trashB", "gcol"],
                    tag=f"gn1.{g}.{c}", force=True)
                em.op("act", lambda e: e.activation(
                    gcol[:, 1:2], gcol[:, 0:1], AF.Ln, bias=cst[:, 0:1],
                    scale=cst[:, 1:2]),
                    reads=["gcol", "cst"], writes=["gcol1"], tag=f"gn2.{g}.{c}", force=True)
                em.op("act", lambda e: e.activation(
                    gcol[:, 2:3], gcol[:, 1:2], AF.Exp, scale=cst[:, 2:3]),
                    reads=["gcol1", "cst"], writes=["gcol2"], tag=f"gn3.{g}.{c}", force=True)
                em.op("act", lambda e, gb=gb: e.activation(
                    yn16[gb][:, :], yg[gb][:, :], AF.Copy, scale=gcol[:, 2:3]),
                    reads=[f"yg{gb}", "gcol2"], writes=[f"yn16{gb}"], tag=f"yn{g}.{c}")
                flush_yn()
                pend_yn.append((g, gb, toff, c))
                em.op("pool", lambda e, g=g, gb=gb, c=c: e.tensor_tensor(
                    stmp[gb][:, :].rearrange("p (h f) -> p h f", h=8),
                    S_f32[:, g * 512:(g + 1) * 512].rearrange("p (h f) -> p h f", h=8),
                    rep8(edecS_all, c, g), ALU.mult),
                    reads=[S_GRP[g], "edecS_all"], writes=[f"stmpB{gb}"], tag=f"sdec{g}.{c}")
                em.op("dve", lambda e, g=g, gb=gb: e.tensor_add(
                    S_f32[:, g * 512:(g + 1) * 512], stmp[gb][:, :], p_ds[:, :]),
                    reads=[f"stmpB{gb}", "p_ds"], writes=[S_GRP[g], "p_ds"],
                    tag=f"supd{g}.{c}")
                em.op("act", lambda e, g=g: e.copy(
                    S_b16[:, g * 512:(g + 1) * 512], S_f32[:, g * 512:(g + 1) * 512]),
                    reads=[S_GRP[g]], writes=[SB_GRP[g]], tag=f"sb16{g}.{c}")
                if c + 1 < NCH:
                    if g == 0:
                        prefix(c + 1)
                    elif g == 2:
                        prefix2(c + 1)
                    elif g == 3:
                        prefix3(c + 1)
                    elif g == 4:
                        prefix4(c + 1)
            if c % 2 == 1:
                flush_yn()
                em.dma(ynT_scr[:, :, (c - 1) * Q:(c + 1) * Q].rearrange("a p t -> p a t"),
                       ynTb[:, :].rearrange("p (a t) -> p a t", a=32), q="qact",
                       reads=[f"ynTb{g2}" for g2 in range(NG)],
                       writes=[f"ynTb{g2}" for g2 in range(NG)],
                       tag=f"spyn{c}")
        close_scope()

    if STAGES[2]:
        ynr = [sbs(f"ynr{i}", [128, 32 * 512], BF16) for i in range(2)]
        wcs = [sbs("wcs0", [128, 32 * 256])]
        wcb = [sbs(f"wcb{i}", [128, 32 * 512], BF16) for i in range(2)]
        nwT = sbs("nwT", [128, 32])
        attb = [sbs(f"attb{i}", [128, 512]) for i in range(2)]
        p_c = [pss(f"p_c{i}", [128, 512]) for i in range(2)]

        em.dma(nwT[:, :], nw_in, writes=["nwT"], tag="ldnw")

        def rec_wo(dpj):
            wbj = dpj % 2
            for hq in range(2):
                em.dma(wcs[0][:, :].rearrange("p (a c) -> p a c", a=32),
                       woutT[:, dpj * 512 + hq * 256:dpj * 512 + (hq + 1) * 256]
                       .rearrange("(a p) c -> p a c", p=128),
                       q="qact", reads=["wcs0"], writes=["wcs0w"], tag=f"ldWo{dpj}.{hq}")
                for ih2 in range(32):
                    if ih2 % 2 == 0:
                        em.op("act", lambda e, wbj=wbj, hq=hq, ih2=ih2: e.activation(
                            wcb[wbj][:, ih2 * 512 + hq * 256:ih2 * 512 + (hq + 1) * 256],
                            wcs[0][:, ih2 * 256:(ih2 + 1) * 256],
                            AF.Copy, scale=nwT[:, ih2:ih2 + 1]),
                            reads=["wcs0w", "nwT"],
                            writes=[f"wcb{wbj}"] if ih2 in (0, 30) else [],
                            tag=f"cvWo{dpj}.{hq}.{ih2}")
                    else:
                        em.op("dve", lambda e, wbj=wbj, hq=hq, ih2=ih2: e.tensor_scalar(
                            wcb[wbj][:, ih2 * 512 + hq * 256:ih2 * 512 + (hq + 1) * 256],
                            wcs[0][:, ih2 * 256:(ih2 + 1) * 256],
                            nwT[:, ih2:ih2 + 1], None, ALU.mult),
                            reads=["wcs0w", "nwT"],
                            writes=[f"wcb{wbj}"] if ih2 == 31 else [],
                            tag=f"cvWo{dpj}.{hq}.{ih2}")

        for dp in range(8):
            wb = dp % 2
            if dp == 0:
                rec_wo(0)
            for tc in range(NCH):
                if tc == 5 and dp + 1 < 8:
                    rec_wo(dp + 1)
                yb = (tc // 4) % 2
                if tc % 4 == 0:
                    em.dma(ynr[yb][:, :].rearrange("p (a t) -> p a t", a=32),
                           ynT_scr[:, :, tc * Q:(tc + 4) * Q].rearrange("a p t -> p a t"),
                           q=("qsp" if (tc // 4) % 2 == 0 else "qact"),
                           reads=[f"ynr{yb}"], writes=[f"ynr{yb}w"], tag=f"ldyn{dp}.{tc}")
                pc = tc % 2
                for i in range(32):
                    em.op("pe", lambda e, yb=yb, wb=wb, i=i, tc=tc, pc=pc: e.matmul(
                        p_c[pc][:, :],
                        ynr[yb][:, i * 512 + (tc % 4) * 128:i * 512 + (tc % 4) * 128 + 128],
                        wcb[wb][:, i * 512:(i + 1) * 512],
                        start=(i == 0), stop=(i == 31)),
                        reads=[f"ynr{yb}w", f"wcb{wb}", f"p_c{pc}"],
                        writes=[f"p_c{pc}"] if i in (0, 31) else [],
                        tag=f"mmC{dp}.{tc}.{i}")
                em.op("dve", lambda e, pc=pc: e.tensor_copy(attb[pc][:, :], p_c[pc][:, :]),
                      reads=[f"p_c{pc}"], writes=[f"attb{pc}", f"p_c{pc}"],
                      tag=f"evC{dp}.{tc}")
                em.dma(attn_scr[tc][:, dp * 512:(dp + 1) * 512], attb[pc][:, :],
                       q="qact", reads=[f"attb{pc}"], tag=f"spC{dp}.{tc}")
                if tc == NCH - 1:
                    pass
        close_scope()

        attc = sbs("attc", [128, D])
        xc = sbs("xc", [128, D])
        z0 = sbs("z0", [128, D])
        trash4 = sbs("trash4", [128, D])
        zb = sbs("zb", [128, D])
        zb16 = sbs("zb16", [128, D], BF16)
        zTp = sbs("zTp", [128, 32 * 256], BF16)
        lncol = sbs("lncol", [128, 8])
        p_tc = [pss(f"p_tc{i}", [128, 1024], BF16) for i in range(2)]

        for tc in range(NCH):
            em.dma(attc[:, :], attn_scr[tc], reads=["attc"], writes=["attcw"],
                   tag=f"ldat{tc}")
            em.dma(xc[:, :], x_in[tc * Q:(tc + 1) * Q, :], reads=["xc"],
                   writes=["xcw"], tag=f"ldx{tc}")
            em.op("dve", lambda e: e.scalar_tensor_tensor(
                z0[:, :], attc[:, :], 1.0, xc[:, :], ALU.mult, ALU.add,
                accum_out=lncol[:, 0:1]),
                reads=["attcw", "xcw"], writes=["z0", "lncol0", "attc", "xc"],
                tag=f"z0.{tc}", force=True)
            em.op("act", lambda e: e.activation(
                trash4[:, :], z0[:, :], AF.Square, accum_out=lncol[:, 1:2]),
                reads=["z0"], writes=["trash4", "lncol1"], tag=f"sq.{tc}", force=True)
            em.op("dve", lambda e: e.tensor_scalar_mul(lncol[:, 2:3], lncol[:, 0:1], 1.0 / D),
                  reads=["lncol0"], writes=["lncol2"], tag=f"mean.{tc}", force=True)
            em.op("dve", lambda e: e.tensor_tensor(lncol[:, 3:4], lncol[:, 2:3],
                                                   lncol[:, 2:3], ALU.mult),
                  reads=["lncol2"], writes=["lncol3"], tag=f"m2.{tc}", force=True)
            em.op("dve", lambda e: e.tensor_scalar(lncol[:, 4:5], lncol[:, 1:2],
                                                   1.0 / D, lncol[:, 3:4], ALU.mult,
                                                   ALU.subtract),
                  reads=["lncol1", "lncol3"], writes=["lncol4"], tag=f"var.{tc}", force=True)
            em.op("act", lambda e: e.activation(lncol[:, 5:6], lncol[:, 4:5], AF.Ln,
                                                bias=cst[:, 0:1]),
                  reads=["lncol4", "cst"], writes=["lncol5"], tag=f"lnv.{tc}", force=True)
            em.op("act", lambda e: e.activation(lncol[:, 6:7], lncol[:, 5:6], AF.Exp,
                                                scale=cst[:, 2:3]),
                  reads=["lncol5", "cst"], writes=["lncol6"], tag=f"rstd.{tc}", force=True)
            em.op("dve", lambda e: e.tensor_scalar(
                zb[:, :], z0[:, :], lncol[:, 2:3], lncol[:, 6:7],
                ALU.subtract, ALU.mult),
                reads=["z0", "lncol2", "lncol6"], writes=["zb"], tag=f"zn.{tc}")
            em.dma(z_scr[tc], zb[:, :], q="qact", reads=["zb"], tag=f"spz{tc}")
            em.op("act", lambda e: e.copy(zb16[:, :], zb[:, :]),
                  reads=["zb"], writes=["zb16"], tag=f"zb16.{tc}")
            for j4 in range(8):
                pb = j4 % 2
                for j in range(4):
                    jj = j4 * 4 + j
                    em.op("pe", lambda e, pb=pb, j=j, jj=jj: e.transpose(
                        p_tc[pb][:, j * 128:(j + 1) * 128],
                        zb16[:, jj * 128:(jj + 1) * 128], ident[:, :]),
                        reads=["zb16", "ident", f"p_tc{pb}"],
                        writes=[f"p_tc{pb}"] if j == 3 else [], tag=f"trz{jj}.{tc}")
                em.op("dve", lambda e, pb=pb, j4=j4, tc=tc: e.tensor_copy(
                    bass.AP(zTp[:, :].tensor,
                            zTp[:, :].offset + j4 * 4 * 256 + (tc % 2) * 128,
                            [[32 * 256, 128], [256, 4], [1, 128]]),
                    p_tc[pb][:, 0:512].rearrange("p (a t) -> p a t", a=4)),
                    reads=[f"p_tc{pb}", "zTp"], writes=["zTp", f"p_tc{pb}"],
                    tag=f"evz{j4}.{tc}")
            if tc % 2 == 1:
                em.dma(zT_scr[:, :, (tc - 1) * Q:(tc + 1) * Q].rearrange("a p t -> p a t"),
                       zTp[:, :].rearrange("p (a t) -> p a t", a=32), q="qact",
                       reads=["zTp"], writes=["zTp"], tag=f"spzT{tc}")
        close_scope()

    if STAGES[3]:
        zTb = sbs("zTb", [128, 32 * T], BF16)
        gw_sb = sbs("gw_sb", [128, 32 * NE])
        gw16 = sbs("gw16", [128, 32 * NE], BF16)
        fstg = [sbs("fstg0", [128, 32 * 128])]
        fb16 = [sbs(f"fb16{i}", [128, 32 * 128], BF16) for i in range(2)]
        lgb = sbs("lgb", [128, NE])
        lgm = sbs("lgm", [128, NE])
        rc = sbs("rc", [128, 4])
        xlb = sbs("xlb", [128, T], BF16)
        rbuf = sbs("rbuf", [128, 512])
        h1b = sbs("h1b", [128, T], BF16)
        wdstg = sbs("wdstg", [128, T])
        wdb16 = sbs("wdb16", [128, T], BF16)
        p_d = [pss(f"p_d{i}", [128, 512]) for i in range(2)]
        p_lg = pss("p_lg", [128, 512])

        em.dma(zTb[:, :].rearrange("p (a t) -> p a t", a=32),
               zT_scr.rearrange("a p t -> p a t"), writes=["zTb"], tag="ldzT")
        em.dma(gw_sb[:, :].rearrange("p (a k) -> p a k", a=32),
               w_gate.rearrange("(a p) k -> p a k", p=128), writes=["gwf"], tag="ldgw")
        em.op("act", lambda e: e.copy(gw16[:, :], gw_sb[:, :]),
              reads=["gwf"], writes=["gw"], tag="cvgw")
        for tc in range(NCH):
            for d in range(32):
                em.op("pe", lambda e, d=d, tc=tc: e.matmul(
                    p_lg[:, 0:NE], zTb[:, d * T + tc * Q:d * T + tc * Q + Q]
                    if False else zTb[:, d * T + tc * 128:d * T + tc * 128 + 128],
                    gw16[:, d * NE:(d + 1) * NE],
                    start=(d == 0), stop=(d == 31)),
                    reads=["zTb", "gw", "p_lg"],
                    writes=["p_lg"] if d in (0, 31) else [], tag=f"mmL{tc}.{d}")
            em.op("dve", lambda e: e.tensor_copy(lgb[:, :], p_lg[:, 0:NE]),
                  reads=["p_lg"], writes=["lgb", "p_lg"], tag=f"evL{tc}", force=True)
            em.op("dve", lambda e: e.tensor_reduce(rc[:, 0:1], lgb[:, :], AX.X, ALU.max),
                  reads=["lgb"], writes=["rc0"], tag=f"m1.{tc}", force=True)
            em.op("dve", lambda e: e.tensor_scalar(lgm[:, :], lgb[:, :], rc[:, 0:1],
                                                   None, ALU.is_lt),
                  reads=["lgb", "rc0"], writes=["lgm"], tag=f"msk.{tc}", force=True)
            em.op("dve", lambda e: e.scalar_tensor_tensor(
                lgm[:, :], lgm[:, :], 1.0e4, lgb[:, :], ALU.mult, ALU.add),
                reads=["lgm", "lgb"], writes=["lgm"], tag=f"mu.{tc}", force=True)
            em.op("dve", lambda e: e.tensor_reduce(rc[:, 1:2], lgm[:, :], AX.X, ALU.max),
                  reads=["lgm"], writes=["rc1"], tag=f"m2.{tc}", force=True)
            em.op("dve", lambda e: e.tensor_scalar_add(rc[:, 2:3], rc[:, 1:2], -1.0e4),
                  reads=["rc1"], writes=["rc2"], tag=f"m2b.{tc}", force=True)
            em.op("dve", lambda e, tc=tc: e.tensor_add(
                route_cols[:, tc:tc + 1], rc[:, 0:1], rc[:, 2:3]),
                reads=["rc0", "rc2"], writes=["route"], tag=f"rt.{tc}", force=True)
        em.dma(strided(r_scr, [[1, 128], [128, NCH]]), route_cols[:, :],
               reads=["route"], writes=["r_scr"], tag="spr",
               allow_slow_non_contiguous=True)
        em.dma(rbc[:, :], strided(r_scr, [[0, 128], [1, T]]),
               reads=["r_scr"], writes=["rbc"], tag="ldrbc")
        def rec_f(ltj, wsrc, tagc):
            bj = ltj % 2
            em.dma(fstg[0][:, :].rearrange("p (a c) -> p a c", a=32),
                   wsrc[:, ltj * 128:(ltj + 1) * 128].rearrange("(a p) c -> p a c", p=128),
                   writes=["fstg0"], tag=f"ld{tagc}{ltj}")
            em.op("act", lambda e, bj=bj: e.copy(fb16[bj][:, :], fstg[0][:, :]),
                  reads=["fstg0"], writes=[f"fb16{bj}"], tag=f"cv{tagc}{ltj}")

        for lt in range(16):
            bb = lt % 2
            if lt == 0:
                rec_f(0, w_fc1, "F")
            if lt + 1 < 16:
                rec_f(lt + 1, w_fc1, "F")
            for t4 in range(4):
                pp = t4 % 2
                for d in range(32):
                    em.op("pe", lambda e, bb=bb, t4=t4, d=d, pp=pp: e.matmul(
                        p_d[pp][:, :], fb16[bb][:, d * 128:(d + 1) * 128],
                        zTb[:, d * T + t4 * 512:d * T + (t4 + 1) * 512],
                        start=(d == 0), stop=(d == 31)),
                        reads=[f"fb16{bb}", "zTb", f"p_d{pp}"],
                        writes=[f"p_d{pp}"] if d in (0, 31) else [], tag=f"mmF{lt}.{t4}.{d}")
                em.op("act", lambda e, pp=pp, t4=t4: e.copy(
                    xlb[:, t4 * 512:(t4 + 1) * 512], p_d[pp][:, :]),
                    reads=[f"p_d{pp}"], writes=["xlb", f"p_d{pp}"], tag=f"evF{lt}.{t4}")
            em.dma(xlat_scr[lt], xlb[:, :], q="qact", reads=["xlb"], tag=f"spF{lt}")
        def rec_wd(k):
            ih, half = k // 2, k % 2
            em.dma(wdstg[:, :], w_down[ih * 128:(ih + 1) * 128,
                                       half * 2048:(half + 1) * 2048],
                   q="qact", reads=["wdstg"], writes=["wdstgw"], tag=f"ldWd{k}")
            em.op("act", lambda e: e.copy(wdb16[:, :], wdstg[:, :]),
                  reads=["wdstgw"], writes=["wdb16", "wdstg"], tag=f"cvWd{k}")
            em.dma(wdb_scr[ih][:, half * 2048:(half + 1) * 2048], wdb16[:, :],
                   q="qact", reads=["wdb16"], tag=f"spWd{k}")

        for it in range(64):
            bb = it % 2
            if it == 0:
                rec_f(0, w_up, "U")
            if it + 1 < 64:
                rec_f(it + 1, w_up, "U")
            for k in range(it * 2, it * 2 + 2):
                rec_wd(k)
            for t4 in range(4):
                pp = t4 % 2
                for d in range(32):
                    em.op("pe", lambda e, bb=bb, t4=t4, d=d, pp=pp: e.matmul(
                        p_d[pp][:, :], fb16[bb][:, d * 128:(d + 1) * 128],
                        zTb[:, d * T + t4 * 512:d * T + (t4 + 1) * 512],
                        start=(d == 0), stop=(d == 31)),
                        reads=[f"fb16{bb}", "zTb", f"p_d{pp}"],
                        writes=[f"p_d{pp}"] if d in (0, 31) else [], tag=f"mmU{it}.{t4}.{d}")
                em.op("act", lambda e, pp=pp: e.activation(
                    rbuf[:, :], p_d[pp][:, :], AF.Relu),
                    reads=[f"p_d{pp}"], writes=["rbuf", f"p_d{pp}"], tag=f"rl{it}.{t4}")
                em.op("dve", lambda e, t4=t4: e.tensor_tensor(
                    h1b[:, t4 * 512:(t4 + 1) * 512], rbuf[:, :], rbuf[:, :], ALU.mult),
                    reads=["rbuf"], writes=["h1b"], tag=f"sq{it}.{t4}")
            em.dma(h1_scr[it], h1b[:, :], q="qact", reads=["h1b"], tag=f"spU{it}")
        close_scope()

        esT = sbs("esT", [128, 16 * T], BF16)
        xlT_h = nc.sbuf_tensor("xlT", [128, 16 * T], BF16)
        xlT = xlT_h.__enter__()
        estg_h = nc.sbuf_tensor("estg0", [128, 16 * 128], F32)
        estg = [estg_h.__enter__()]
        eb16_h = [nc.sbuf_tensor(f"eb16{i}", [128, 16 * 128], BF16) for i in range(2)]
        eb16 = [h.__enter__() for h in eb16_h]
        p_d2 = [pss(f"p_d2{i}", [128, 512]) for i in range(2)]

        em.dma(xlT[:, :].rearrange("p (a t) -> p a t", a=16),
               xlat_scr.rearrange("a p t -> p a t"), writes=["xlT"], tag="ldxl")
        def rec_e(mtj):
            bj = mtj % 2
            em.dma(estg[0][:, :].rearrange("p (a c) -> p a c", a=16),
                   w_exp[:, mtj * 128:(mtj + 1) * 128].rearrange("(a p) c -> p a c", p=128),
                   writes=["estg0"], tag=f"ldE{mtj}")
            em.op("act", lambda e, bj=bj: e.copy(eb16[bj][:, :], estg[0][:, :]),
                  reads=["estg0"], writes=[f"eb16{bj}"], tag=f"cvE{mtj}")

        for mt in range(16):
            bb = mt % 2
            if mt == 0:
                rec_e(0)
            if mt + 1 < 16:
                rec_e(mt + 1)
            for t4 in range(4):
                pp = t4 % 2
                for l in range(16):
                    em.op("pe", lambda e, bb=bb, t4=t4, l=l, pp=pp: e.matmul(
                        p_d2[pp][:, :], eb16[bb][:, l * 128:(l + 1) * 128],
                        xlT[:, l * T + t4 * 512:l * T + (t4 + 1) * 512],
                        start=(l == 0), stop=(l == 15)),
                        reads=[f"eb16{bb}", "xlT", f"p_d2{pp}"],
                        writes=[f"p_d2{pp}"] if l in (0, 15) else [], tag=f"mmE{mt}.{t4}.{l}")
                em.op("dve", lambda e, pp=pp, mt=mt, t4=t4: e.tensor_tensor(
                    esT[:, mt * T + t4 * 512:mt * T + (t4 + 1) * 512],
                    p_d2[pp][:, :], rbc[:, t4 * 512:(t4 + 1) * 512], ALU.mult),
                    reads=[f"p_d2{pp}", "rbc"], writes=["esT", f"p_d2{pp}"],
                    tag=f"evE{mt}.{t4}")
        em.fence()
        for h in [eb16_h[1], eb16_h[0], estg_h, xlT_h]:
            h.__exit__(None, None, None)
        f2stg = [sbs("f2stg0", [128, 16 * 256])]
        f2b16 = [sbs(f"f2b16{i}", [128, 16 * 512], BF16) for i in range(2)]
        rb = [sbs(f"rb{i}", [128, 512]) for i in range(2)]

        def rec_f2(dpj):
            fbj = dpj % 2
            for hq in range(2):
                em.dma(f2stg[0][:, :].rearrange("p (a c) -> p a c", a=16),
                       w_fc2[:, dpj * 512 + hq * 256:dpj * 512 + (hq + 1) * 256]
                       .rearrange("(a p) c -> p a c", p=128),
                       writes=["f2stg0"], tag=f"ldF2{dpj}.{hq}")
                em.op("act", lambda e, hq=hq, fbj=fbj: e.copy(
                    bass.AP(f2b16[fbj][:, :].tensor, f2b16[fbj][:, :].offset + hq * 256,
                            [[16 * 512, 128], [512, 16], [1, 256]]),
                    f2stg[0][:, :].rearrange("p (a c) -> p a c", a=16)),
                    reads=["f2stg0"], writes=[f"f2b16{fbj}"],
                    tag=f"cvF2{dpj}.{hq}")

        for dp2 in range(8):
            fb = dp2 % 2
            if dp2 == 0:
                rec_f2(0)
            if dp2 + 1 < 8:
                rec_f2(dp2 + 1)
            for tc in range(NCH):
                pp = tc % 2
                for m in range(16):
                    em.op("pe", lambda e, m=m, tc=tc, pp=pp, fb=fb: e.matmul(
                        p_d2[pp][:, :], esT[:, m * T + tc * 128:m * T + tc * 128 + 128],
                        f2b16[fb][:, m * 512:(m + 1) * 512],
                        start=(m == 0), stop=(m == 15)),
                        reads=["esT", f"f2b16{fb}", f"p_d2{pp}"],
                        writes=[f"p_d2{pp}"] if m in (0, 15) else [], tag=f"mmR{dp2}.{tc}.{m}")
                em.op("dve", lambda e, pp=pp: e.tensor_copy(rb[pp][:, :], p_d2[pp][:, :]),
                      reads=[f"p_d2{pp}"], writes=[f"rb{pp}", f"p_d2{pp}"],
                      tag=f"evR{dp2}.{tc}")
                em.dma(routed_scr[tc][:, dp2 * 512:(dp2 + 1) * 512], rb[pp][:, :],
                       q="qact", reads=[f"rb{pp}"], tag=f"spR{dp2}.{tc}")
        close_scope()

        wpan = sbs("wpan", [128, 64 * 512], BF16)
        h1r = [sbs(f"h1r{i}", [128, 64 * 256], BF16) for i in range(2)]
        rtb = [sbs(f"rtb{i}", [128, 512]) for i in range(2)]
        acb = [sbs(f"acb{i}", [128, 512]) for i in range(2)]
        p_d3 = [pss(f"p_d3{i}", [128, 512]) for i in range(2)]

        for dp in range(8):
            em.dma(wpan[:, :].rearrange("p (a c) -> p a c", a=64),
                   wdb_scr[:, :, dp * 512:(dp + 1) * 512].rearrange("a p c -> p a c"),
                   reads=["wpan"], writes=["wpanw"], tag=f"ldD{dp}")
            for tc in range(NCH):
                hb = (tc // 2) % 2
                if tc % 2 == 0:
                    em.dma(h1r[hb][:, :].rearrange("p (a t) -> p a t", a=64),
                           h1_scr[:, :, tc * Q:(tc + 2) * Q].rearrange("a p t -> p a t"),
                           reads=[f"h1r{hb}"], writes=[f"h1r{hb}w"], tag=f"ldh{dp}.{tc}")
                pp = tc % 2
                for i in range(64):
                    em.op("pe", lambda e, hb=hb, i=i, tc=tc, pp=pp: e.matmul(
                        p_d3[pp][:, :],
                        h1r[hb][:, i * 256 + (tc % 2) * 128:i * 256 + (tc % 2) * 128 + 128],
                        wpan[:, i * 512:(i + 1) * 512],
                        start=(i == 0), stop=(i == 63)),
                        reads=[f"h1r{hb}w", "wpanw", f"p_d3{pp}"],
                        writes=[f"p_d3{pp}"] if i in (0, 63) else [], tag=f"mmD{dp}.{tc}.{i}")
                em.dma(rtb[pp][:, :], routed_scr[tc][:, dp * 512:(dp + 1) * 512],
                       reads=[f"rtb{pp}"], writes=[f"rtb{pp}w"], tag=f"ldrt{dp}.{tc}")
                em.op("dve", lambda e, pp=pp: e.tensor_add(
                    acb[pp][:, :], p_d3[pp][:, :], rtb[pp][:, :]),
                    reads=[f"p_d3{pp}", f"rtb{pp}w"],
                    writes=[f"acb{pp}", f"p_d3{pp}", f"rtb{pp}"], tag=f"evD{dp}.{tc}")
                em.dma(acc_scr[tc][:, dp * 512:(dp + 1) * 512], acb[pp][:, :],
                       q="qact", reads=[f"acb{pp}"], tag=f"spD{dp}.{tc}")
        close_scope()

        gbc = sbs("gbc", [128, D])
        bbc = sbs("bbc", [128, D])
        ac4_ = [sbs(f"ac4_{i}", [128, D]) for i in range(2)]
        zc4_ = [sbs(f"zc4_{i}", [128, D]) for i in range(2)]
        f1_ = [sbs("f1_s", [128, D])] * 2
        f2_ = [sbs("f2_s", [128, D])] * 2
        trash5 = sbs("trash5", [128, D])
        obuf_ = [sbs(f"obuf_{i}", [128, D]) for i in range(2)]
        lc_ = [sbs(f"lc_{i}", [128, 10]) for i in range(2)]

        em.dma(gbc[:, :], strided(gam_in, [[0, 128], [1, D]]), writes=["gbc"], tag="ldg")
        em.dma(bbc[:, :], strided(bet_in, [[0, 128], [1, D]]), writes=["bbc"], tag="ldb")
        for tc in range(NCH):
            pp4 = tc % 2
            ac4, zc4, f1, f2, obuf, lc = (ac4_[pp4], zc4_[pp4], f1_[pp4], f2_[pp4], obuf_[pp4], lc_[pp4])
            em.dma(ac4[:, :], acc_scr[tc], reads=["ac4" + str(pp4)], writes=["ac4w" + str(pp4)], tag=f"lda4{tc}")
            em.dma(zc4[:, :], z_scr[tc], reads=["zc4" + str(pp4)], writes=["zc4w" + str(pp4)], tag=f"ldz4{tc}")
            em.op("dve", lambda e, ac4=ac4, zc4=zc4, f1=f1, f2=f2, obuf=obuf, lc=lc: e.tensor_reduce(lc[:, 0:1], ac4[:, :], AX.X, ALU.add),
                  reads=["ac4w" + str(pp4)], writes=["lc0" + str(pp4)], tag=f"s4.{tc}", force=True)
            em.op("act", lambda e, ac4=ac4, zc4=zc4, f1=f1, f2=f2, obuf=obuf, lc=lc: e.activation(trash5[:, :], ac4[:, :], AF.Square,
                                                accum_out=lc[:, 1:2]),
                  reads=["ac4w" + str(pp4)], writes=["trash5", "lc1" + str(pp4)], tag=f"q4.{tc}", force=True)
            em.op("dve", lambda e, ac4=ac4, zc4=zc4, f1=f1, f2=f2, obuf=obuf, lc=lc: e.tensor_scalar_mul(lc[:, 2:3], lc[:, 0:1], 1.0 / D),
                  reads=["lc0" + str(pp4)], writes=["lc2" + str(pp4)], tag=f"mn4.{tc}", force=True)
            em.op("dve", lambda e, ac4=ac4, zc4=zc4, f1=f1, f2=f2, obuf=obuf, lc=lc: e.tensor_tensor(lc[:, 3:4], lc[:, 2:3], lc[:, 2:3],
                                                   ALU.mult),
                  reads=["lc2" + str(pp4)], writes=["lc3" + str(pp4)], tag=f"m24.{tc}", force=True)
            em.op("dve", lambda e, ac4=ac4, zc4=zc4, f1=f1, f2=f2, obuf=obuf, lc=lc: e.tensor_scalar(lc[:, 4:5], lc[:, 1:2], 1.0 / D,
                                                   lc[:, 3:4], ALU.mult, ALU.subtract),
                  reads=["lc1" + str(pp4), "lc3" + str(pp4)], writes=["lc4" + str(pp4)], tag=f"v4.{tc}", force=True)
            em.op("act", lambda e, ac4=ac4, zc4=zc4, f1=f1, f2=f2, obuf=obuf, lc=lc: e.activation(lc[:, 5:6], lc[:, 4:5], AF.Ln,
                                                bias=cst[:, 0:1]),
                  reads=["lc4" + str(pp4), "cst"], writes=["lc5" + str(pp4)], tag=f"l4.{tc}", force=True)
            em.op("act", lambda e, ac4=ac4, zc4=zc4, f1=f1, f2=f2, obuf=obuf, lc=lc: e.activation(lc[:, 6:7], lc[:, 5:6], AF.Exp,
                                                scale=cst[:, 2:3]),
                  reads=["lc5" + str(pp4), "cst"], writes=["lc6" + str(pp4)], tag=f"r4.{tc}", force=True)
            em.op("dve", lambda e, ac4=ac4, zc4=zc4, f1=f1, f2=f2, obuf=obuf, lc=lc: e.tensor_scalar(
                f1[:, :], ac4[:, :], lc[:, 2:3], lc[:, 6:7], ALU.subtract, ALU.mult),
                reads=["ac4w" + str(pp4), "lc2" + str(pp4), "lc6" + str(pp4)], writes=["f1", "ac4" + str(pp4)], tag=f"n4.{tc}")
            em.op("pool", lambda e, f1=f1, f2=f2: e.tensor_mul(f2[:, :], f1[:, :], gbc[:, :]),
                  reads=["f1", "gbc"], writes=["f2"], tag=f"g4.{tc}")
            em.op("pool", lambda e, zc4=zc4, f1=f1: e.tensor_add(f1[:, :], zc4[:, :], bbc[:, :]),
                  reads=["zc4w" + str(pp4), "bbc"], writes=["f1", "zc4" + str(pp4)], tag=f"zb4.{tc}")
            em.op("dve", lambda e, ac4=ac4, zc4=zc4, f1=f1, f2=f2, obuf=obuf, lc=lc: e.scalar_tensor_tensor(
                f2[:, :], f2[:, :], 1.0, f1[:, :], ALU.mult, ALU.add,
                accum_out=lc[:, 7:8]),
                reads=["f1", "f2"], writes=["f2", "lc7" + str(pp4)], tag=f"z2.{tc}", force=True)
            em.op("act", lambda e, ac4=ac4, zc4=zc4, f1=f1, f2=f2, obuf=obuf, lc=lc: e.activation(trash5[:, :], f2[:, :], AF.Square,
                                                accum_out=lc[:, 8:9]),
                  reads=["f2"], writes=["trash5", "lc8" + str(pp4)], tag=f"q5.{tc}", force=True)
            em.op("dve", lambda e, ac4=ac4, zc4=zc4, f1=f1, f2=f2, obuf=obuf, lc=lc: e.tensor_scalar_mul(lc[:, 2:3], lc[:, 7:8], 1.0 / D),
                  reads=["lc7" + str(pp4)], writes=["lc2" + str(pp4)], tag=f"mn5.{tc}", force=True)
            em.op("dve", lambda e, ac4=ac4, zc4=zc4, f1=f1, f2=f2, obuf=obuf, lc=lc: e.tensor_tensor(lc[:, 3:4], lc[:, 2:3], lc[:, 2:3],
                                                   ALU.mult),
                  reads=["lc2" + str(pp4)], writes=["lc3" + str(pp4)], tag=f"m25.{tc}", force=True)
            em.op("dve", lambda e, ac4=ac4, zc4=zc4, f1=f1, f2=f2, obuf=obuf, lc=lc: e.tensor_scalar(lc[:, 4:5], lc[:, 8:9], 1.0 / D,
                                                   lc[:, 3:4], ALU.mult, ALU.subtract),
                  reads=["lc8" + str(pp4), "lc3" + str(pp4)], writes=["lc4" + str(pp4)], tag=f"v5.{tc}", force=True)
            em.op("act", lambda e, ac4=ac4, zc4=zc4, f1=f1, f2=f2, obuf=obuf, lc=lc: e.activation(lc[:, 5:6], lc[:, 4:5], AF.Ln,
                                                bias=cst[:, 0:1]),
                  reads=["lc4" + str(pp4), "cst"], writes=["lc5" + str(pp4)], tag=f"l5.{tc}", force=True)
            em.op("act", lambda e, ac4=ac4, zc4=zc4, f1=f1, f2=f2, obuf=obuf, lc=lc: e.activation(lc[:, 6:7], lc[:, 5:6], AF.Exp,
                                                scale=cst[:, 2:3]),
                  reads=["lc5" + str(pp4), "cst"], writes=["lc6" + str(pp4)], tag=f"r5.{tc}", force=True)
            em.op("dve", lambda e, ac4=ac4, zc4=zc4, f1=f1, f2=f2, obuf=obuf, lc=lc: e.tensor_scalar(
                obuf[:, :], f2[:, :], lc[:, 2:3], lc[:, 6:7], ALU.subtract, ALU.mult),
                reads=["f2", "lc2" + str(pp4), "lc6" + str(pp4)], writes=["obuf" + str(pp4)], tag=f"o.{tc}")
            em.dma(out[tc * Q:(tc + 1) * Q, :], obuf[:, :], q="qact",
                   reads=["obuf" + str(pp4)], writes=["obuf" + str(pp4)], tag=f"out{tc}")
        close_scope()

    em.fence()
    em.op("sp", lambda e: e.nop(), tag="final")
    em.emit(nc)
    for h in reversed(persist):
        h.__exit__(None, None, None)
    return nc


_NC_CACHE = None


def _get_nc():
    global _NC_CACHE
    if _NC_CACHE is None:
        nc = bass.Bass("TRN2", target_bir_lowering=False)
        build(nc)
        _NC_CACHE = nc
    return _NC_CACHE


def make_in_maps(inputs):
    x = np.asarray(inputs["L0_attn_hidden_states"], dtype=np.float32)
    B = x.shape[0]
    shared = {
        "winT": np.ascontiguousarray(np.asarray(inputs["L0_attn_in_proj_w"]).T),
        "woutT": np.ascontiguousarray(np.asarray(inputs["L0_attn_out_proj_w"]).T),
        "w_up": np.ascontiguousarray(np.asarray(inputs["L0_ffn_up_w"])),
        "w_down": np.ascontiguousarray(np.asarray(inputs["L0_ffn_down_w"])),
        "w_fc1": np.ascontiguousarray(np.asarray(inputs["L0_ffn_fc1_w"])),
        "w_fc2": np.ascontiguousarray(np.asarray(inputs["L0_ffn_fc2_w"])),
        "w_exp": np.ascontiguousarray(np.asarray(inputs["L0_ffn_expert_w"])),
        "w_gate": np.ascontiguousarray(np.asarray(inputs["L0_ffn_gate_w"])),
        "convw": np.ascontiguousarray(
            np.asarray(inputs["L0_attn_conv_weight"]).reshape(48, 128, 4).transpose(1, 0, 2)),
        "convb": np.ascontiguousarray(
            np.asarray(inputs["L0_attn_conv_bias"]).reshape(48, 128).T),
        "dtbias": np.ascontiguousarray(
            np.asarray(inputs["L0_attn_dt_bias"]).reshape(NH, 1)),
        "alog": np.ascontiguousarray(np.asarray(inputs["L0_attn_A_log"]).reshape(NH, 1)),
        "dvec": np.ascontiguousarray(np.asarray(inputs["L0_attn_D"])),
        "nw": np.ascontiguousarray(
            np.asarray(inputs["L0_attn_norm_weight"]).reshape(32, 128).T),
        "gam": np.ascontiguousarray(np.asarray(inputs["L0_ffn_ln_gamma"])),
        "bet": np.ascontiguousarray(np.asarray(inputs["L0_ffn_ln_beta"])),
    }
    in_maps = []
    for b in range(B):
        m = dict(shared)
        m["x"] = np.ascontiguousarray(x[b])
        m["xT"] = np.ascontiguousarray(x[b].T)
        in_maps.append(m)
    return in_maps


def kernel(**inputs):
    in_maps = make_in_maps(inputs)
    nc = _get_nc()
    res = run_bass_kernel_spmd(nc, in_maps, core_ids=list(range(len(in_maps))))
    return np.stack([np.asarray(r["out"], dtype=np.float32) for r in res.results])


# version 09
# speedup vs baseline: 1.1163x; 1.0954x over previous
import numpy as np
import ml_dtypes
import concourse.bass as bass
import concourse.mybir as mybir
from concourse.bass_utils import run_bass_kernel_spmd

F32 = mybir.dt.float32
BF16 = mybir.dt.bfloat16
AF = mybir.ActivationFunctionType
ALU = mybir.AluOpType
AX = mybir.AxisListType

DEBUG = False
STAGES = (1, 1, 1, 1)

T = 2048
D = 4096
NH, HP, SN, NG = 64, 64, 128, 8
I_SSM = NH * HP
GN = NG * SN
CONV = I_SSM + 2 * GN
PROJ = I_SSM + CONV + NH
Q = 128
NCH = T // Q
INT, LAT, NE = 8192, 2048, 8
EPS = 1e-5

ENGINES = ("pe", "act", "dve", "pool", "sp")


class Em:

    def __init__(self):
        self.ops = []
        self.tok = {}
        self.fence_ids = set()
        self.dma_counts = {"qsp": 0, "qact": 0}

    def op(self, eng, fn, reads=(), writes=(), dma_q=None, tag="", force=False):
        oid = len(self.ops)
        deps = set()
        for t in reads:
            st = self.tok.get(t)
            if st and st["w"] is not None:
                deps.add(st["w"])
        for t in writes:
            st = self.tok.get(t)
            if st:
                if st["w"] is not None:
                    deps.add(st["w"])
                deps.update(st["r"])
        deps.update(self.fence_ids)
        deps.discard(oid)
        if deps:
            best = {}
            for d in deps:
                p = self.ops[d]
                key = ("q", p["dma_q"]) if p["dma_q"] is not None else ("e", p["eng"])
                if key not in best or d > best[key]:
                    best[key] = d
            deps = set(best.values())
        dma_idx = None
        if dma_q is not None:
            self.dma_counts[dma_q] += 1
            dma_idx = self.dma_counts[dma_q]
        self.ops.append({"id": oid, "eng": eng, "fn": fn, "deps": deps,
                         "dma_q": dma_q, "dma_idx": dma_idx, "tag": tag,
                         "force": force})
        for t in reads:
            self.tok.setdefault(t, {"w": None, "r": []})["r"].append(oid)
        for t in writes:
            self.tok[t] = {"w": oid, "r": []}
        return oid

    def dma(self, dst, src, q="qsp", reads=(), writes=(), tag="", **kw):
        eng = "sp" if q == "qsp" else "act"
        return self.op(eng, lambda e, d=dst, s=src: e.dma_start(d, s, **kw),
                       reads=reads, writes=writes, dma_q=q, tag=tag)

    def fence(self):
        last_eng = {}
        last_dma = {}
        for o in self.ops:
            if o["dma_q"] is not None:
                last_dma[o["dma_q"]] = o["id"]
            else:
                last_eng[o["eng"]] = o["id"]
        self.fence_ids = set(last_eng.values()) | set(last_dma.values())

    @staticmethod
    def _elide(p, c):
        if c["dma_q"] is not None or p.get("force"):
            return False
        return p["dma_q"] is None and p["eng"] == c["eng"]

    def emit(self, nc):
        ops = self.ops
        needed = set()
        for o in ops:
            for d in o["deps"]:
                if not self._elide(ops[d], o):
                    needed.add(d)
        ms = {e: 0 for e in ENGINES}
        for o in ops:
            if o["dma_q"] is not None:
                o["ms"] = (o["dma_q"], 16 * o["dma_idx"])
            elif o["id"] in needed:
                ms[o["eng"]] += 1
                o["ms"] = (o["eng"], ms[o["eng"]])
            else:
                o["ms"] = None
        clocks = {e: {} for e in ENGINES}
        per_eng = {e: [] for e in ENGINES}
        for o in ops:
            c = clocks[o["eng"]]
            waits = []
            for d in sorted(o["deps"]):
                p = ops[d]
                if self._elide(p, o):
                    continue
                sem, val = p["ms"]
                if c.get(sem, 0) < val:
                    waits.append((sem, val))
                    c[sem] = val
            o["waits"] = waits
            per_eng[o["eng"]].append(o)

        sems = {}
        hs = []
        for name in ("pe", "act", "dve", "pool", "qsp", "qact"):
            h = nc.semaphore("sem_" + name)
            sems[name] = h.__enter__()
            hs.append(h)

        def run(engine_obj, eng_name):
            for o in per_eng[eng_name]:
                for (sem, val) in o["waits"]:
                    engine_obj.wait_ge(sems[sem], val)
                inst = o["fn"](engine_obj)
                if o["dma_q"] is not None:
                    inst.then_inc(sems[o["dma_q"]], 16)
                elif o["id"] in needed:
                    inst.then_inc(sems[o["eng"]], 1)

        with nc.Block() as blk:
            @blk.tensor
            def _(e):
                run(e, "pe")

            @blk.scalar
            def _(e):
                run(e, "act")

            @blk.vector
            def _(e):
                run(e, "dve")

            @blk.gpsimd
            def _(e):
                run(e, "pool")

            @blk.sync
            def _(e):
                run(e, "sp")
        for h in reversed(hs):
            h.__exit__(None, None, None)


def strided(base_ap, pattern):
    return bass.AP(base_ap.tensor, base_ap.offset, pattern)


def build(nc):
    em = Em()

    def din(name, shape, dt=F32):
        return nc.dram_tensor(name, shape, dt, kind="ExternalInput").ap()

    def dscr(name, shape, dt=F32):
        kind = "ExternalOutput" if DEBUG else "Internal"
        return nc.dram_tensor(name, shape, dt, kind=kind).ap()

    xT_in = din("xT", [D, T])
    x_in = din("x", [T, D])
    winT = din("winT", [D, PROJ])
    woutT = din("woutT", [I_SSM, D])
    w_up = din("w_up", [D, INT])
    w_down = din("w_down", [INT, D])
    w_fc1 = din("w_fc1", [D, LAT])
    w_fc2 = din("w_fc2", [LAT, D])
    w_exp = din("w_exp", [LAT, LAT])
    w_gate = din("w_gate", [D, NE])
    convw = din("convw", [128, 48, 4])
    convb = din("convb", [128, 48])
    dtbias = din("dtbias", [NH, 1])
    alog = din("alog", [NH, 1])
    dvec = din("dvec", [NH])
    nw_in = din("nw", [128, 32])
    gam_in = din("gam", [D])
    bet_in = din("bet", [D])
    out = nc.dram_tensor("out", [T, D], F32, kind="ExternalOutput").ap()

    gate_scr = dscr("gate_scr", [32, 128, T], BF16)
    xbc_scr = dscr("xbc_scr", [48, 128, T], BF16)
    ynT_scr = dscr("ynT_scr", [32, 128, T], BF16)
    attn_scr = dscr("attn_scr", [NCH, 128, D])
    z_scr = dscr("z_scr", [NCH, 128, D])
    zT_scr = dscr("zT_scr", [32, 128, T], BF16)
    s_scr = dscr("s_scr", [NH, Q])
    r_scr = dscr("r_scr", [1, T])
    xlat_scr = dscr("xlat_scr", [16, 128, T], BF16)
    h1_scr = dscr("h1_scr", [64, 128, T], BF16)
    wdb_scr = dscr("wdb_scr", [64, 128, D], BF16)
    routed_scr = dscr("routed_scr", [NCH, 128, D])
    acc_scr = dscr("acc_scr", [NCH, 128, D])
    if DEBUG:
        dt_dbg = dscr("dt_dbg", [NH, T])
        dbg_dtsT = dscr("dbg_dtsT", [128, 128])
        dbg_lam = dscr("dbg_lam", [128, NH * Q], BF16)
        dbg_xnat = dscr("dbg_xnat", [128, I_SSM], BF16)
        dbg_xdt = dscr("dbg_xdt", [128, 512], BF16)
        dbg_gam = dscr("dbg_gam", [128, 8 * Q], BF16)
        dbg_yg = dscr("dbg_yg", [128, 512])
        dbg_y1 = dscr("dbg_y1", [128, 512])
        dbg_sb = dscr("dbg_sb", [128, I_SSM], BF16)
        dbg_yn16 = dscr("dbg_yn16", [128, 512], BF16)
        dbg_gcol = dscr("dbg_gcol", [128, 4])
        dbg_ynTb = dscr("dbg_ynTb", [128, 32 * 256], BF16)

    persist = []

    def sbp(name, shape, dt=F32):
        h = nc.sbuf_tensor(name, shape, dt)
        t = h.__enter__()
        persist.append(h)
        return t

    scope_hs = []

    def sbs(name, shape, dt=F32):
        h = nc.sbuf_tensor(name, shape, dt)
        t = h.__enter__()
        scope_hs.append(h)
        return t

    def pss(name, shape, dt=F32):
        h = nc.psum_tensor(name, shape, dt)
        t = h.__enter__()
        scope_hs.append(h)
        return t

    def close_scope():
        em.fence()
        for h in reversed(scope_hs):
            h.__exit__(None, None, None)
        scope_hs.clear()

    dt_sb = sbp("dt_sb", [NH, T])
    ident = sbp("ident", [128, 128], BF16)
    identf = sbp("identf", [128, 128], F32)
    convw_sb = sbp("convw_sb", [128, 48 * 4])
    convb_sb = sbp("convb_sb", [128, 48])
    small = sbp("small", [NH, 8])
    dbcp = sbp("dbcp", [128, NH])
    cst = sbp("cst", [128, 4])

    em.op("dve", lambda e: e.memset(cst[:, 0:1], EPS), writes=["cst"])
    em.op("dve", lambda e: e.memset(cst[:, 1:2], 1.0 / 512), writes=["cst"])
    em.op("dve", lambda e: e.memset(cst[:, 2:3], -0.5), writes=["cst"])
    em.op("pool", lambda e: e.memset(ident[:, :], 1.0), writes=["ident"])
    em.op("pool", lambda e: e.affine_select(ident[:, :], ident[:, :], [[1, 128]],
                                            ALU.is_equal, 0.0, channel_multiplier=-1),
          reads=["ident"], writes=["ident"])
    em.op("pool", lambda e: e.memset(identf[:, :], 1.0), writes=["identf"])
    em.op("pool", lambda e: e.affine_select(identf[:, :], identf[:, :], [[1, 128]],
                                            ALU.is_equal, 0.0, channel_multiplier=-1),
          reads=["identf"], writes=["identf"])
    em.dma(convw_sb[:, :], convw.rearrange("p a k -> p (a k)"), writes=["convw"])
    em.dma(convb_sb[:, :], convb, writes=["convb"])
    em.dma(small[:, 0:1], dtbias, writes=["small0"])
    em.dma(small[:, 1:2], alog, writes=["small1"])
    em.dma(dbcp[:, :], strided(dvec, [[0, 128], [1, NH]]), writes=["dbcp"])
    em.op("act", lambda e: e.activation(small[:, 2:3], small[:, 1:2], AF.Exp),
          reads=["small1"], writes=["smallA"])
    em.op("dve", lambda e: e.tensor_scalar_mul(small[:, 2:3], small[:, 2:3], -1.0),
          reads=["smallA"], writes=["smallA"])

    if STAGES[0]:
        xTb = sbs("xTb", [128, 32 * T], BF16)
        wstg = [sbs("wstg0", [128, 32 * 128])]
        wb16 = [sbs(f"wb16{i}", [128, 32 * 128], BF16) for i in range(2)]
        hpad = sbs("hpad", [128, 3 + T])
        accA = sbs("accA", [128, T])
        accB = sbs("accB", [128, T])
        xstg = [accA, accB]
        xout = [sbs("xout0", [128, T], BF16)]
        pbank = [pss(f"pbA{i}", [128, 512]) for i in range(8)]

        em.op("dve", lambda e: e.memset(hpad[:, 0:3], 0.0), writes=["hpad"])

        XSTOK = ["accA", "accB"]
        for dt_i in range(32):
            b = dt_i % 2
            em.dma(xstg[b][:, :], xT_in[dt_i * 128:(dt_i + 1) * 128, :],
                   writes=[XSTOK[b]], tag=f"ld_xT{dt_i}")
            em.op("act", lambda e, b=b, dt_i=dt_i: e.copy(
                xTb[:, dt_i * T:(dt_i + 1) * T], xstg[b][:, :]),
                reads=[XSTOK[b]], writes=[f"xTb{dt_i}"], tag=f"cv_xT{dt_i}")

        CT_ORDER = list(range(32, 81)) + list(range(32))

        def rec_w(cj):
            ctj = CT_ORDER[cj]
            cwj = 64 if ctj == 80 else 128
            bj = cj % 2
            em.dma(wstg[0][:, 0:32 * cwj].rearrange("p (a c) -> p a c", a=32),
                   winT[:, ctj * 128:ctj * 128 + cwj].rearrange("(a p) c -> p a c", p=128),
                   writes=["wstg0"], tag=f"ld_w{ctj}")
            em.op("act", lambda e, bj=bj, cwj=cwj: e.copy(
                wb16[bj][:, 0:32 * cwj], wstg[0][:, 0:32 * cwj]),
                reads=["wstg0"], writes=[f"wb16{bj}"], tag=f"cv_w{ctj}")

        for ci, ct in enumerate(CT_ORDER):
            cw = 64 if ct == 80 else 128
            b = ci % 2
            bank0 = 4 * (ci % 2)
            if ci == 0:
                rec_w(0)
            if ci + 1 < len(CT_ORDER):
                rec_w(ci + 1)
            for t in range(4):
                bank = pbank[bank0 + t]
                btok = f"pbA{bank0 + t}"
                for d in range(32):
                    first, last = (d == 0), (d == 31)
                    oap = bank[:, :] if cw == 128 else bank[0:64, :]
                    em.op("pe", lambda e, b=b, cw=cw, t=t, d=d, oap=oap:
                          e.matmul(oap, wb16[b][:, d * cw:(d + 1) * cw],
                                   xTb[:, d * T + t * 512:d * T + (t + 1) * 512],
                                   start=(d == 0), stop=(d == 31)),
                          reads=[f"wb16{b}", f"xTb{d}"],
                          writes=[btok] if (first or last) else [],
                          tag=f"mmA{ct}.{t}.{d}")
            if ct < 32:
                xo = xout[0]
                for t in range(4):
                    btok = f"pbA{bank0 + t}"
                    em.op("act", lambda e, t=t, bank=pbank[bank0 + t], xo=xo:
                          e.activation(xo[:, t * 512:(t + 1) * 512], bank[:, :], AF.Silu),
                          reads=[btok], writes=["xout0", btok], tag=f"siluG{ct}.{t}")
                em.dma(gate_scr[ct], xo[:, :], q="qact",
                       reads=["xout0"], tag=f"spG{ct}")
            elif ct < 80:
                j = ct - 32
                for t in range(4):
                    btok = f"pbA{bank0 + t}"
                    em.op("dve", lambda e, t=t, bank=pbank[bank0 + t]:
                          e.tensor_copy(hpad[:, 3 + t * 512:3 + (t + 1) * 512], bank[:, :]),
                          reads=[btok], writes=["hpad", btok], tag=f"evH{ct}.{t}")
                em.op("dve", lambda e, j=j: e.tensor_scalar(
                    accA[:, :], hpad[:, 3:3 + T], convw_sb[:, j * 4 + 3:j * 4 + 4],
                    None, ALU.mult),
                    reads=["hpad", "convw"], writes=["accA"], tag=f"cv3.{j}")
                em.op("dve", lambda e, j=j: e.scalar_tensor_tensor(
                    accB[:, :], hpad[:, 2:2 + T], convw_sb[:, j * 4 + 2:j * 4 + 3],
                    accA[:, :], ALU.mult, ALU.add),
                    reads=["hpad", "convw", "accA"], writes=["accB"], tag=f"cv2.{j}")
                em.op("dve", lambda e, j=j: e.scalar_tensor_tensor(
                    accA[:, :], hpad[:, 1:1 + T], convw_sb[:, j * 4 + 1:j * 4 + 2],
                    accB[:, :], ALU.mult, ALU.add),
                    reads=["hpad", "convw", "accB"], writes=["accA"], tag=f"cv1.{j}")
                em.op("dve", lambda e, j=j: e.scalar_tensor_tensor(
                    accB[:, :], hpad[:, 0:0 + T], convw_sb[:, j * 4 + 0:j * 4 + 1],
                    accA[:, :], ALU.mult, ALU.add),
                    reads=["hpad", "convw", "accA"], writes=["accB"], tag=f"cv0.{j}")
                xo = xout[0]
                em.op("act", lambda e, j=j, xo=xo: e.activation(
                    xo[:, :], accB[:, :], AF.Silu, bias=convb_sb[:, j:j + 1]),
                    reads=["accB", "convb"], writes=["xout0"], tag=f"siluX{j}")
                em.dma(xbc_scr[j], xo[:, :], q="qact",
                       reads=["xout0"], tag=f"spX{j}")
            else:
                for t in range(4):
                    btok = f"pbA{bank0 + t}"
                    em.op("dve", lambda e, t=t, bank=pbank[bank0 + t]:
                          e.tensor_copy(dt_sb[:, t * 512:(t + 1) * 512], bank[0:64, :]),
                          reads=[btok], writes=["dt_sb", btok], tag=f"evDT{t}")
        if DEBUG:
            em.dma(dt_dbg, dt_sb[:, :], q="qact", reads=["dt_sb"])
        close_scope()

    rbc = sbp("rbc", [128, T])
    route_cols = sbp("route_cols", [128, NCH])

    if STAGES[1]:
        xbcC = [sbs("xbcC0", [128, 48 * 256], BF16)]
        gateC = [sbs("gateC0", [128, 32 * 256], BF16)]
        S_f32 = sbs("S_f32", [128, I_SSM])
        S_b16 = sbs("S_b16", [128, I_SSM], BF16)
        bcast = sbs("bcast", [128, NH * Q])
        tb = bcast[0:64, 0:T]
        tb2 = bcast[0:64, T:2 * T]
        dtp = bcast[0:64, 2 * T:3 * T]
        lamT = sbs("lamT", [128, NH * Q], BF16)
        LcumF = sbs("LcumF", [128, 128])
        Abc = sbs("Abc", [128, NH])
        onesF = sbs("onesF", [1, 128])
        dtpT_all = sbs("dtpT_all", [128, NCH * NH])
        aT_all = sbs("aT_all", [128, NCH * NH])
        sT_all = sbs("sT_all", [128, NCH * NH])
        s_q = sbs("s_q", [NH, 512])
        slast_row = sbs("slast_row", [1, NCH * NH])
        edecSarg = sbs("edecSarg", [128, NCH * NH])
        edecS_all = sbs("edecS_all", [128, NCH * NH])
        expS_all = aT_all
        X_nat = sbs("X_nat", [128, I_SSM], BF16)
        xdt_g = [sbs(f"xdt_g{i}", [128, 512], BF16) for i in range(2)]
        xd2_g = [sbs(f"xd2_g{i}", [128, 512], BF16) for i in range(2)]
        gam_g = [sbs(f"gam_g{i}", [128, 8 * Q], BF16) for i in range(2)]
        bn_g = [sbs(f"bn_g{i}", [128, 128], BF16) for i in range(2)]
        ytmp = sbs("ytmp", [128, 512])
        xdterm = [sbs(f"xdterm{i}", [128, 512]) for i in range(2)]
        yg = [sbs(f"yg{i}", [128, 512]) for i in range(2)]
        gcol = sbs("gcol", [128, 4])
        yn16 = [sbs(f"yn16{i}", [128, 512], BF16) for i in range(2)]
        stmp = [sbs(f"stmpB{i}", [128, 512]) for i in range(2)]
        ynTb = sbs("ynTb", [128, 32 * 256], BF16)
        trashB = sbs("trashB", [128, 512])
        p_cb = [pss("p_cb0", [128, 512])]
        p_y1 = [pss(f"p_y1{i}", [128, 512]) for i in range(2)]
        p_y2 = [pss(f"p_y2{i}", [128, 512]) for i in range(2)]
        p_ds = pss("p_ds", [128, 512])
        p_tr = [pss(f"p_tr{i}", [128, 1024], BF16) for i in range(2)]
        p_st = p_y1[0]

        a_dram = dscr("a_dram", [NH, 1])
        s_scr_all = dscr("s_scr_all", [NCH, NH, Q])

        em.op("pool", lambda e: e.memset(LcumF[:, :], 1.0), writes=["LcumF"])
        em.op("pool", lambda e: e.affine_select(LcumF[:, :], LcumF[:, :], [[1, 128]],
                                                ALU.is_ge, 0.0, channel_multiplier=-1),
              reads=["LcumF"], writes=["LcumF"])
        em.op("dve", lambda e: e.memset(onesF[:, :], 1.0), writes=["onesF"])

        em.op("dve", lambda e: e.tensor_scalar(tb, dt_sb[:, :],
                                               small[:, 0:1], None, ALU.add),
              reads=["dt_sb", "small0", "bcast"], writes=["tbB", "bcast"], tag="sp_add")
        em.op("act", lambda e: e.activation(tb2, tb, AF.Abs),
              reads=["tbB", "bcast"], writes=["tb2B", "bcast"], tag="sp_abs")
        em.op("act", lambda e: e.activation(tb2, tb2, AF.Exp, scale=-1.0),
              reads=["tb2B"], writes=["tb2B"], tag="sp_exp")
        em.op("act", lambda e: e.activation(tb2, tb2, AF.Ln, bias=1.0),
              reads=["tb2B"], writes=["tb2B"], tag="sp_ln")
        em.op("dve", lambda e: e.tensor_relu(dtp, tb),
              reads=["tbB"], writes=["dtpB", "bcast"], tag="sp_relu")
        em.op("dve", lambda e: e.tensor_add(dtp, dtp, tb2),
              reads=["dtpB", "tb2B"], writes=["dtpB", "bcast"], tag="sp_sum")

        em.dma(a_dram, small[:, 2:3], reads=["smallA"], writes=["a_dram"], tag="sp_aw")
        em.dma(Abc[:, :], strided(a_dram, [[0, 128], [1, NH]]),
               reads=["a_dram"], writes=["Abc"], tag="sp_ar")

        for half in range(2):
            for j in range(8):
                c = half * 8 + j
                em.op("pe", lambda e, c=c, j=j: e.matmul(
                    p_st[:, j * 64:(j + 1) * 64], dtp[:, c * Q:(c + 1) * Q],
                    identf[0:64, 0:64], start=True, stop=True),
                    reads=["dtpB", "identf", "p_y10"],
                    writes=["p_y10"] if j in (0, 7) else [], tag=f"trdtp{c}")
            em.op("dve", lambda e, half=half: e.tensor_copy(
                dtpT_all[:, half * 512:(half + 1) * 512], p_st[:, :]),
                reads=["p_y10"], writes=["dtpT", "p_y10"], tag=f"evdtp{half}")
        em.op("dve", lambda e: e.tensor_tensor(
            aT_all[:, :].rearrange("p (c h) -> p c h", c=NCH),
            dtpT_all[:, :].rearrange("p (c h) -> p c h", c=NCH),
            strided(Abc[:, :], [[NH, 128], [0, NCH], [1, NH]]), ALU.mult),
            reads=["dtpT", "Abc"], writes=["aT_all"], tag="aTmul")
        for half in range(2):
            for j in range(8):
                c = half * 8 + j
                em.op("pe", lambda e, c=c, j=j: e.matmul(
                    p_st[:, j * 64:(j + 1) * 64], LcumF[:, :],
                    aT_all[:, c * NH:(c + 1) * NH], start=True, stop=True),
                    reads=["LcumF", "aT_all", "p_y10"],
                    writes=["p_y10"] if j in (0, 7) else [], tag=f"sT{c}")
            em.op("dve", lambda e, half=half: e.tensor_copy(
                sT_all[:, half * 512:(half + 1) * 512], p_st[:, :]),
                reads=["p_y10"], writes=["sT_all", "p_y10"], tag=f"evsT{half}")
        for qtr in range(4):
            for j in range(4):
                c = qtr * 4 + j
                em.op("pe", lambda e, c=c, j=j: e.matmul(
                    p_st[0:64, j * 128:(j + 1) * 128],
                    aT_all[:, c * NH:(c + 1) * NH], LcumF[:, :],
                    start=True, stop=True),
                    reads=["aT_all", "LcumF", "p_y10"],
                    writes=["p_y10"] if j in (0, 3) else [], tag=f"sN{c}")
            em.op("dve", lambda e, qtr=qtr: e.tensor_copy(
                s_q[:, :], p_st[0:64, :]),
                reads=["p_y10", "s_qT"], writes=["s_q", "p_y10"], tag=f"evsN{qtr}")
            em.dma(bass.AP(s_scr_all.tensor, s_scr_all.offset + qtr * 4 * NH * Q,
                           [[Q, NH], [NH * Q, 4], [1, Q]]),
                   s_q[:, :].rearrange("h (c t) -> h c t", c=4),
                   reads=["s_q"], writes=["s_qT", "s_scr"], tag=f"sp_s{qtr}")
        em.dma(slast_row[0:1, :], sT_all[127:128, :],
               reads=["sT_all"], writes=["slast"], tag="slast")
        for j in range(2):
            em.op("pe", lambda e, j=j: e.matmul(
                p_st[:, :], onesF[0:1, :], slast_row[0:1, j * 512:(j + 1) * 512],
                start=True, stop=True),
                reads=["onesF", "slast", "p_y10"], writes=["p_y10"], tag=f"r1s{j}")
            em.op("dve", lambda e, j=j: e.tensor_copy(
                edecSarg[:, j * 512:(j + 1) * 512], p_st[:, :]),
                reads=["p_y10"], writes=["edecSarg", "p_y10"], tag=f"evr1{j}")
        em.op("act", lambda e: e.activation(edecS_all[:, :], edecSarg[:, :], AF.Exp),
              reads=["edecSarg"], writes=["edecS_all"], tag="edecS")
        em.op("act", lambda e: e.activation(expS_all[:, :], sT_all[:, :], AF.Exp),
              reads=["sT_all"], writes=["expS_all", "aT_all"], tag="expS")
        em.op("dve", lambda e: e.tensor_sub(edecSarg[:, :], edecSarg[:, :], sT_all[:, :]),
              reads=["edecSarg", "sT_all"], writes=["edecSarg"], tag="edTa")
        em.op("act", lambda e: e.activation(edecSarg[:, :], edecSarg[:, :], AF.Exp),
              reads=["edecSarg"], writes=["edT_all"], tag="edTb")
        edT_all = edecSarg

        S_GRP = [f"Sf_{g}" for g in range(NG)]
        SB_GRP = [f"Sb_{g}" for g in range(NG)]
        em.op("dve", lambda e: e.memset(S_f32[:, :], 0.0), writes=S_GRP, tag="S0")
        em.op("dve", lambda e: e.memset(S_b16[:, :], 0.0), writes=SB_GRP, tag="Sb0")

        def rep8(buf, c, g, pstep=NCH * NH):
            base = buf[:, :]
            return bass.AP(base.tensor, base.offset + c * NH + g * 8,
                           [[pstep, 128], [1, 8], [0, HP]])

        def prefix(c):
            em.dma(bcast[:, :],
                   strided(bass.AP(s_scr_all.tensor, s_scr_all.offset + c * NH * Q,
                                   [[0, 128], [1, NH * Q]]), [[0, 128], [1, NH * Q]]),
                   reads=["s_scr"], writes=["bcast"], tag=f"bc{c}")

        def prefix2(c):
            em.op("dve", lambda e, c=c: e.tensor_tensor(
                bcast[:, :].rearrange("p (h t) -> p h t", h=NH),
                bcast[:, :].rearrange("p (h t) -> p h t", h=NH),
                bass.AP(sT_all[:, :].tensor, sT_all[:, :].offset + c * NH,
                        [[NCH * NH, 128], [1, NH], [0, Q]]),
                ALU.subtract),
                reads=["bcast", "sT_all"], writes=["bcast"], tag=f"arg{c}")

        def prefix3(c):
            em.op("pool", lambda e: e.affine_select(
                bcast[:, :].rearrange("p (h t) -> p h t", h=NH),
                bcast[:, :].rearrange("p (h t) -> p h t", h=NH),
                [[0, NH], [1, Q]], ALU.is_ge, -1e9, channel_multiplier=-1),
                reads=["bcast"], writes=["bcast"], tag=f"mask{c}")

        def prefix4(c):
            em.op("act", lambda e: e.activation(lamT[:, :], bcast[:, :], AF.Exp),
                  reads=["bcast"], writes=["lamT"], tag=f"lam{c}")

        pend_yn = []

        def flush_yn():
            while pend_yn:
                g, gb, toff, c = pend_yn.pop()
                for j in range(4):
                    em.op("pe", lambda e, j=j, gb=gb: e.transpose(
                        p_tr[1][:, j * 128:(j + 1) * 128],
                        yn16[gb][:, j * 128:(j + 1) * 128], ident[:, :]),
                        reads=[f"yn16{gb}", "ident", "p_tr1"],
                        writes=["p_tr1"] if j in (0, 3) else [], tag=f"trY{g}.{j}.{c}")
                em.op("dve", lambda e, g=g, toff=toff: e.tensor_tensor(
                    bass.AP(ynTb[:, :].tensor, ynTb[:, :].offset + 4 * g * 256 + toff,
                            [[32 * 256, 128], [256, 4], [1, 128]]),
                    p_tr[1][:, 0:512].rearrange("p (a t) -> p a t", a=4),
                    bass.AP(gateC[0][:, :].tensor,
                            gateC[0][:, :].offset + 4 * g * 256 + toff,
                            [[32 * 256, 128], [256, 4], [1, 128]]),
                    ALU.mult),
                    reads=["p_tr1", "gateC0", f"ynTb{g}"], writes=[f"ynTb{g}", "p_tr1"],
                    tag=f"evY{g}.{c}")

        prefix(0)
        prefix2(0)
        prefix3(0)
        prefix4(0)
        for c in range(NCH):
            toff = (c % 2) * 128
            if c % 2 == 0:
                flush_yn()
                em.dma(xbcC[0][:, :].rearrange("p (a t) -> p a t", a=48),
                       xbc_scr[:, :, c * Q:(c + 2) * Q].rearrange("a p t -> p a t"),
                       writes=["xbcC0"], tag=f"ldxbc{c}")
                em.dma(gateC[0][:, :].rearrange("p (a t) -> p a t", a=32),
                       gate_scr[:, :, c * Q:(c + 2) * Q].rearrange("a p t -> p a t"),
                       writes=["gateC0"], tag=f"ldg{c}")
            xbtok, gttok = "xbcC0", "gateC0"

            for ct4 in range(8):
                ptb = ct4 % 2
                for j in range(4):
                    ct = ct4 * 4 + j
                    em.op("pe", lambda e, ptb=ptb, j=j, ct=ct, toff=toff:
                          e.transpose(p_tr[ptb][:, j * 128:(j + 1) * 128],
                                      xbcC[0][:, ct * 256 + toff:ct * 256 + toff + 128],
                                      ident[:, :]),
                          reads=[xbtok, "ident", f"p_tr{ptb}"],
                          writes=[f"p_tr{ptb}"] if j in (0, 3) else [],
                          tag=f"trX{ct}.{c}")
                em.op("act", lambda e, ptb=ptb, ct4=ct4: e.copy(
                    X_nat[:, ct4 * 512:(ct4 + 1) * 512], p_tr[ptb][:, 0:512]),
                    reads=[f"p_tr{ptb}"], writes=["X_nat", f"p_tr{ptb}"],
                    tag=f"evX{ct4}.{c}")

            for g in range(NG):
                gb = g % 2
                cbb = 0
                cbt = "p_cb0"
                bslice = lambda g=g, toff=toff: xbcC[0][
                    :, (32 + g) * 256 + toff:(32 + g) * 256 + toff + 128]
                cslice = lambda g=g, toff=toff: xbcC[0][
                    :, (40 + g) * 256 + toff:(40 + g) * 256 + toff + 128]
                em.op("pe", lambda e, bs=bslice, cs=cslice, cbb=cbb: e.matmul(
                    p_cb[cbb][:, 0:128], bs(), cs(), start=True, stop=True),
                    reads=[xbtok, cbt], writes=[cbt], tag=f"cb{g}.{c}")
                em.op("pe", lambda e, bs=bslice: e.transpose(
                    p_tr[0][:, 0:128], bs(), ident[:, :]),
                    reads=[xbtok, "ident", "p_tr0"], writes=["p_tr0"], tag=f"trB{g}.{c}")
                em.op("act", lambda e, gb=gb: e.copy(bn_g[gb][:, :], p_tr[0][:, 0:128]),
                      reads=["p_tr0"], writes=[f"bn_g{gb}", "p_tr0"], tag=f"evB{g}.{c}")
                em.op("pool", lambda e, g=g, gb=gb, c=c: e.tensor_tensor(
                    xdt_g[gb][:, :].rearrange("p (h f) -> p h f", h=8),
                    X_nat[:, g * 512:(g + 1) * 512].rearrange("p (h f) -> p h f", h=8),
                    rep8(dtpT_all, c, g), ALU.mult),
                    reads=["X_nat", "dtpT"], writes=[f"xdt_g{gb}"], tag=f"xdt{g}.{c}")
                em.op("pool", lambda e, g=g, gb=gb, c=c: e.tensor_tensor(
                    xd2_g[gb][:, :].rearrange("p (h f) -> p h f", h=8),
                    xdt_g[gb][:, :].rearrange("p (h f) -> p h f", h=8),
                    rep8(edT_all, c, g), ALU.mult),
                    reads=[f"xdt_g{gb}", "edT_all"], writes=[f"xd2_g{gb}"], tag=f"xd2{g}.{c}")
                em.op("dve", lambda e, g=g, gb=gb, cbb=cbb: e.tensor_tensor(
                    gam_g[gb][:, :].rearrange("p (h t) -> p h t", h=8),
                    lamT[:, g * 8 * Q:(g + 1) * 8 * Q].rearrange("p (h t) -> p h t", h=8),
                    strided(p_cb[cbb][:, 0:128], [[512, 128], [0, 8], [1, 128]]),
                    ALU.mult),
                    reads=["lamT", cbt], writes=[f"gam_g{gb}", cbt], tag=f"gam{g}.{c}")
                for hh in range(8):
                    em.op("pe", lambda e, hh=hh, gb=gb: e.matmul(
                        p_y1[gb][:, hh * 64:(hh + 1) * 64],
                        gam_g[gb][:, hh * Q:(hh + 1) * Q],
                        xdt_g[gb][:, hh * 64:(hh + 1) * 64],
                        start=True, stop=True),
                        reads=[f"gam_g{gb}", f"xdt_g{gb}", f"p_y1{gb}"],
                        writes=[f"p_y1{gb}"] if hh in (0, 7) else [], tag=f"y1.{g}.{hh}.{c}")
                em.op("pe", lambda e, cs=cslice, g=g, gb=gb: e.matmul(
                    p_y2[gb][:, :], cs(), S_b16[:, g * 512:(g + 1) * 512],
                    start=True, stop=True),
                    reads=[xbtok, SB_GRP[g], f"p_y2{gb}"], writes=[f"p_y2{gb}"], tag=f"y2.{g}.{c}")
                em.op("pe", lambda e, gb=gb: e.matmul(
                    p_ds[:, :], bn_g[gb][:, :], xd2_g[gb][:, :], start=True, stop=True),
                    reads=[f"bn_g{gb}", f"xd2_g{gb}", "p_ds"], writes=["p_ds"],
                    tag=f"ds{g}.{c}")
                em.op("dve", lambda e, g=g, c=c, gb=gb: e.tensor_tensor(
                    ytmp[:, :].rearrange("p (h f) -> p h f", h=8),
                    p_y2[gb][:, :].rearrange("p (h f) -> p h f", h=8),
                    rep8(expS_all, c, g), ALU.mult),
                    reads=[f"p_y2{gb}", "expS_all"], writes=["ytmp", f"p_y2{gb}"],
                    tag=f"ytmp{g}.{c}")
                em.op("dve", lambda e, gb=gb: e.tensor_add(
                    ytmp[:, :], p_y1[gb][:, :], ytmp[:, :]),
                    reads=[f"p_y1{gb}", "ytmp"], writes=["ytmp", f"p_y1{gb}"],
                    tag=f"ycmb{g}.{c}")
                em.op("pool", lambda e, g=g, gb=gb: e.tensor_tensor(
                    xdterm[gb][:, :].rearrange("p (h f) -> p h f", h=8),
                    X_nat[:, g * 512:(g + 1) * 512].rearrange("p (h f) -> p h f", h=8),
                    strided(dbcp[:, g * 8:g * 8 + 8], [[NH, 128], [1, 8], [0, HP]]),
                    ALU.mult),
                    reads=["X_nat", "dbcp"], writes=[f"xdterm{gb}"], tag=f"xD{g}.{c}")
                em.op("dve", lambda e, gb=gb: e.tensor_add(
                    yg[gb][:, :], ytmp[:, :], xdterm[gb][:, :]),
                    reads=["ytmp", f"xdterm{gb}"], writes=[f"yg{gb}"], tag=f"ygv{g}.{c}")
                em.op("act", lambda e, gb=gb: e.activation(
                    trashB[:, :], yg[gb][:, :], AF.Square, accum_out=gcol[:, 0:1]),
                    reads=[f"yg{gb}"], writes=["trashB", "gcol"],
                    tag=f"gn1.{g}.{c}", force=True)
                em.op("act", lambda e: e.activation(
                    gcol[:, 1:2], gcol[:, 0:1], AF.Ln, bias=cst[:, 0:1],
                    scale=cst[:, 1:2]),
                    reads=["gcol", "cst"], writes=["gcol1"], tag=f"gn2.{g}.{c}", force=True)
                em.op("act", lambda e: e.activation(
                    gcol[:, 2:3], gcol[:, 1:2], AF.Exp, scale=cst[:, 2:3]),
                    reads=["gcol1", "cst"], writes=["gcol2"], tag=f"gn3.{g}.{c}", force=True)
                em.op("act", lambda e, gb=gb: e.activation(
                    yn16[gb][:, :], yg[gb][:, :], AF.Copy, scale=gcol[:, 2:3]),
                    reads=[f"yg{gb}", "gcol2"], writes=[f"yn16{gb}"], tag=f"yn{g}.{c}")
                flush_yn()
                pend_yn.append((g, gb, toff, c))
                em.op("pool", lambda e, g=g, gb=gb, c=c: e.tensor_tensor(
                    stmp[gb][:, :].rearrange("p (h f) -> p h f", h=8),
                    S_f32[:, g * 512:(g + 1) * 512].rearrange("p (h f) -> p h f", h=8),
                    rep8(edecS_all, c, g), ALU.mult),
                    reads=[S_GRP[g], "edecS_all"], writes=[f"stmpB{gb}"], tag=f"sdec{g}.{c}")
                em.op("dve", lambda e, g=g, gb=gb: e.tensor_add(
                    S_f32[:, g * 512:(g + 1) * 512], stmp[gb][:, :], p_ds[:, :]),
                    reads=[f"stmpB{gb}", "p_ds"], writes=[S_GRP[g], "p_ds"],
                    tag=f"supd{g}.{c}")
                em.op("act", lambda e, g=g: e.copy(
                    S_b16[:, g * 512:(g + 1) * 512], S_f32[:, g * 512:(g + 1) * 512]),
                    reads=[S_GRP[g]], writes=[SB_GRP[g]], tag=f"sb16{g}.{c}")
                if c + 1 < NCH:
                    if g == 0:
                        prefix(c + 1)
                    elif g == 2:
                        prefix2(c + 1)
                    elif g == 3:
                        prefix3(c + 1)
                    elif g == 4:
                        prefix4(c + 1)
            if c % 2 == 1:
                flush_yn()
                em.dma(ynT_scr[:, :, (c - 1) * Q:(c + 1) * Q].rearrange("a p t -> p a t"),
                       ynTb[:, :].rearrange("p (a t) -> p a t", a=32), q="qact",
                       reads=[f"ynTb{g2}" for g2 in range(NG)],
                       writes=[f"ynTb{g2}" for g2 in range(NG)],
                       tag=f"spyn{c}")
        close_scope()

    if STAGES[2]:
        ynr = [sbs(f"ynr{i}", [128, 32 * 512], BF16) for i in range(2)]
        wcs = [sbs("wcs0", [128, 32 * 256])]
        wcb = [sbs(f"wcb{i}", [128, 32 * 512], BF16) for i in range(2)]
        nwT = sbs("nwT", [128, 32])
        attb = [sbs(f"attb{i}", [128, 512]) for i in range(2)]
        p_c = [pss(f"p_c{i}", [128, 512]) for i in range(2)]

        em.dma(nwT[:, :], nw_in, writes=["nwT"], tag="ldnw")

        def rec_wo(dpj):
            wbj = dpj % 2
            for hq in range(2):
                em.dma(wcs[0][:, :].rearrange("p (a c) -> p a c", a=32),
                       woutT[:, dpj * 512 + hq * 256:dpj * 512 + (hq + 1) * 256]
                       .rearrange("(a p) c -> p a c", p=128),
                       reads=["wcs0"], writes=["wcs0w"], tag=f"ldWo{dpj}.{hq}")
                for ih2 in range(32):
                    if ih2 % 2 == 0:
                        em.op("act", lambda e, wbj=wbj, hq=hq, ih2=ih2: e.activation(
                            wcb[wbj][:, ih2 * 512 + hq * 256:ih2 * 512 + (hq + 1) * 256],
                            wcs[0][:, ih2 * 256:(ih2 + 1) * 256],
                            AF.Copy, scale=nwT[:, ih2:ih2 + 1]),
                            reads=["wcs0w", "nwT"],
                            writes=[f"wcb{wbj}"] if ih2 in (0, 30) else [],
                            tag=f"cvWo{dpj}.{hq}.{ih2}")
                    else:
                        em.op("dve", lambda e, wbj=wbj, hq=hq, ih2=ih2: e.tensor_scalar(
                            wcb[wbj][:, ih2 * 512 + hq * 256:ih2 * 512 + (hq + 1) * 256],
                            wcs[0][:, ih2 * 256:(ih2 + 1) * 256],
                            nwT[:, ih2:ih2 + 1], None, ALU.mult),
                            reads=["wcs0w", "nwT"],
                            writes=[f"wcb{wbj}"] if ih2 == 31 else [],
                            tag=f"cvWo{dpj}.{hq}.{ih2}")

        for dp in range(8):
            wb = dp % 2
            if dp == 0:
                rec_wo(0)
            for tc in range(NCH):
                if tc == 5 and dp + 1 < 8:
                    rec_wo(dp + 1)
                yb = (tc // 4) % 2
                if tc % 4 == 0:
                    em.dma(ynr[yb][:, :].rearrange("p (a t) -> p a t", a=32),
                           ynT_scr[:, :, tc * Q:(tc + 4) * Q].rearrange("a p t -> p a t"),
                           reads=[f"ynr{yb}"], writes=[f"ynr{yb}w"], tag=f"ldyn{dp}.{tc}")
                pc = tc % 2
                for i in range(32):
                    em.op("pe", lambda e, yb=yb, wb=wb, i=i, tc=tc, pc=pc: e.matmul(
                        p_c[pc][:, :],
                        ynr[yb][:, i * 512 + (tc % 4) * 128:i * 512 + (tc % 4) * 128 + 128],
                        wcb[wb][:, i * 512:(i + 1) * 512],
                        start=(i == 0), stop=(i == 31)),
                        reads=[f"ynr{yb}w", f"wcb{wb}", f"p_c{pc}"],
                        writes=[f"p_c{pc}"] if i in (0, 31) else [],
                        tag=f"mmC{dp}.{tc}.{i}")
                em.op("dve", lambda e, pc=pc: e.tensor_copy(attb[pc][:, :], p_c[pc][:, :]),
                      reads=[f"p_c{pc}"], writes=[f"attb{pc}", f"p_c{pc}"],
                      tag=f"evC{dp}.{tc}")
                em.dma(attn_scr[tc][:, dp * 512:(dp + 1) * 512], attb[pc][:, :],
                       q="qact", reads=[f"attb{pc}"], tag=f"spC{dp}.{tc}")
                if tc == NCH - 1:
                    pass
        close_scope()

        attc = sbs("attc", [128, D])
        xc = sbs("xc", [128, D])
        z0 = sbs("z0", [128, D])
        trash4 = sbs("trash4", [128, D])
        zb = sbs("zb", [128, D])
        zb16 = sbs("zb16", [128, D], BF16)
        zTp = sbs("zTp", [128, 32 * 256], BF16)
        lncol = sbs("lncol", [128, 8])
        p_tc = [pss(f"p_tc{i}", [128, 1024], BF16) for i in range(2)]

        for tc in range(NCH):
            em.dma(attc[:, :], attn_scr[tc], reads=["attc"], writes=["attcw"],
                   tag=f"ldat{tc}")
            em.dma(xc[:, :], x_in[tc * Q:(tc + 1) * Q, :], reads=["xc"],
                   writes=["xcw"], tag=f"ldx{tc}")
            em.op("dve", lambda e: e.scalar_tensor_tensor(
                z0[:, :], attc[:, :], 1.0, xc[:, :], ALU.mult, ALU.add,
                accum_out=lncol[:, 0:1]),
                reads=["attcw", "xcw"], writes=["z0", "lncol0", "attc", "xc"],
                tag=f"z0.{tc}", force=True)
            em.op("act", lambda e: e.activation(
                trash4[:, :], z0[:, :], AF.Square, accum_out=lncol[:, 1:2]),
                reads=["z0"], writes=["trash4", "lncol1"], tag=f"sq.{tc}", force=True)
            em.op("dve", lambda e: e.tensor_scalar_mul(lncol[:, 2:3], lncol[:, 0:1], 1.0 / D),
                  reads=["lncol0"], writes=["lncol2"], tag=f"mean.{tc}", force=True)
            em.op("dve", lambda e: e.tensor_tensor(lncol[:, 3:4], lncol[:, 2:3],
                                                   lncol[:, 2:3], ALU.mult),
                  reads=["lncol2"], writes=["lncol3"], tag=f"m2.{tc}", force=True)
            em.op("dve", lambda e: e.tensor_scalar(lncol[:, 4:5], lncol[:, 1:2],
                                                   1.0 / D, lncol[:, 3:4], ALU.mult,
                                                   ALU.subtract),
                  reads=["lncol1", "lncol3"], writes=["lncol4"], tag=f"var.{tc}", force=True)
            em.op("act", lambda e: e.activation(lncol[:, 5:6], lncol[:, 4:5], AF.Ln,
                                                bias=cst[:, 0:1]),
                  reads=["lncol4", "cst"], writes=["lncol5"], tag=f"lnv.{tc}", force=True)
            em.op("act", lambda e: e.activation(lncol[:, 6:7], lncol[:, 5:6], AF.Exp,
                                                scale=cst[:, 2:3]),
                  reads=["lncol5", "cst"], writes=["lncol6"], tag=f"rstd.{tc}", force=True)
            em.op("dve", lambda e: e.tensor_scalar(
                zb[:, :], z0[:, :], lncol[:, 2:3], lncol[:, 6:7],
                ALU.subtract, ALU.mult),
                reads=["z0", "lncol2", "lncol6"], writes=["zb"], tag=f"zn.{tc}")
            em.dma(z_scr[tc], zb[:, :], q="qact", reads=["zb"], tag=f"spz{tc}")
            em.op("act", lambda e: e.copy(zb16[:, :], zb[:, :]),
                  reads=["zb"], writes=["zb16"], tag=f"zb16.{tc}")
            for j4 in range(8):
                pb = j4 % 2
                for j in range(4):
                    jj = j4 * 4 + j
                    em.op("pe", lambda e, pb=pb, j=j, jj=jj: e.transpose(
                        p_tc[pb][:, j * 128:(j + 1) * 128],
                        zb16[:, jj * 128:(jj + 1) * 128], ident[:, :]),
                        reads=["zb16", "ident", f"p_tc{pb}"],
                        writes=[f"p_tc{pb}"] if j == 3 else [], tag=f"trz{jj}.{tc}")
                em.op("dve", lambda e, pb=pb, j4=j4, tc=tc: e.tensor_copy(
                    bass.AP(zTp[:, :].tensor,
                            zTp[:, :].offset + j4 * 4 * 256 + (tc % 2) * 128,
                            [[32 * 256, 128], [256, 4], [1, 128]]),
                    p_tc[pb][:, 0:512].rearrange("p (a t) -> p a t", a=4)),
                    reads=[f"p_tc{pb}", "zTp"], writes=["zTp", f"p_tc{pb}"],
                    tag=f"evz{j4}.{tc}")
            if tc % 2 == 1:
                em.dma(zT_scr[:, :, (tc - 1) * Q:(tc + 1) * Q].rearrange("a p t -> p a t"),
                       zTp[:, :].rearrange("p (a t) -> p a t", a=32), q="qact",
                       reads=["zTp"], writes=["zTp"], tag=f"spzT{tc}")
        close_scope()

    if STAGES[3]:
        zTb = sbs("zTb", [128, 32 * T], BF16)
        gw_sb = sbs("gw_sb", [128, 32 * NE])
        gw16 = sbs("gw16", [128, 32 * NE], BF16)
        fstg = [sbs("fstg0", [128, 32 * 128])]
        fb16 = [sbs(f"fb16{i}", [128, 32 * 128], BF16) for i in range(2)]
        lgb = sbs("lgb", [128, NE])
        lgm = sbs("lgm", [128, NE])
        rc = sbs("rc", [128, 4])
        xlb = sbs("xlb", [128, T], BF16)
        rbuf = sbs("rbuf", [128, 512])
        h1b = sbs("h1b", [128, T], BF16)
        wdstg = sbs("wdstg", [128, T])
        wdb16 = sbs("wdb16", [128, T], BF16)
        p_d = [pss(f"p_d{i}", [128, 512]) for i in range(2)]
        p_lg = pss("p_lg", [128, 512])

        em.dma(zTb[:, :].rearrange("p (a t) -> p a t", a=32),
               zT_scr.rearrange("a p t -> p a t"), writes=["zTb"], tag="ldzT")
        em.dma(gw_sb[:, :].rearrange("p (a k) -> p a k", a=32),
               w_gate.rearrange("(a p) k -> p a k", p=128), writes=["gwf"], tag="ldgw")
        em.op("act", lambda e: e.copy(gw16[:, :], gw_sb[:, :]),
              reads=["gwf"], writes=["gw"], tag="cvgw")
        for tc in range(NCH):
            for d in range(32):
                em.op("pe", lambda e, d=d, tc=tc: e.matmul(
                    p_lg[:, 0:NE], zTb[:, d * T + tc * Q:d * T + tc * Q + Q]
                    if False else zTb[:, d * T + tc * 128:d * T + tc * 128 + 128],
                    gw16[:, d * NE:(d + 1) * NE],
                    start=(d == 0), stop=(d == 31)),
                    reads=["zTb", "gw", "p_lg"],
                    writes=["p_lg"] if d in (0, 31) else [], tag=f"mmL{tc}.{d}")
            em.op("dve", lambda e: e.tensor_copy(lgb[:, :], p_lg[:, 0:NE]),
                  reads=["p_lg"], writes=["lgb", "p_lg"], tag=f"evL{tc}", force=True)
            em.op("dve", lambda e: e.tensor_reduce(rc[:, 0:1], lgb[:, :], AX.X, ALU.max),
                  reads=["lgb"], writes=["rc0"], tag=f"m1.{tc}", force=True)
            em.op("dve", lambda e: e.tensor_scalar(lgm[:, :], lgb[:, :], rc[:, 0:1],
                                                   None, ALU.is_lt),
                  reads=["lgb", "rc0"], writes=["lgm"], tag=f"msk.{tc}", force=True)
            em.op("dve", lambda e: e.scalar_tensor_tensor(
                lgm[:, :], lgm[:, :], 1.0e4, lgb[:, :], ALU.mult, ALU.add),
                reads=["lgm", "lgb"], writes=["lgm"], tag=f"mu.{tc}", force=True)
            em.op("dve", lambda e: e.tensor_reduce(rc[:, 1:2], lgm[:, :], AX.X, ALU.max),
                  reads=["lgm"], writes=["rc1"], tag=f"m2.{tc}", force=True)
            em.op("dve", lambda e: e.tensor_scalar_add(rc[:, 2:3], rc[:, 1:2], -1.0e4),
                  reads=["rc1"], writes=["rc2"], tag=f"m2b.{tc}", force=True)
            em.op("dve", lambda e, tc=tc: e.tensor_add(
                route_cols[:, tc:tc + 1], rc[:, 0:1], rc[:, 2:3]),
                reads=["rc0", "rc2"], writes=["route"], tag=f"rt.{tc}", force=True)
        em.dma(strided(r_scr, [[1, 128], [128, NCH]]), route_cols[:, :],
               reads=["route"], writes=["r_scr"], tag="spr",
               allow_slow_non_contiguous=True)
        em.dma(rbc[:, :], strided(r_scr, [[0, 128], [1, T]]),
               reads=["r_scr"], writes=["rbc"], tag="ldrbc")
        def rec_f(ltj, wsrc, tagc):
            bj = ltj % 2
            em.dma(fstg[0][:, :].rearrange("p (a c) -> p a c", a=32),
                   wsrc[:, ltj * 128:(ltj + 1) * 128].rearrange("(a p) c -> p a c", p=128),
                   writes=["fstg0"], tag=f"ld{tagc}{ltj}")
            em.op("act", lambda e, bj=bj: e.copy(fb16[bj][:, :], fstg[0][:, :]),
                  reads=["fstg0"], writes=[f"fb16{bj}"], tag=f"cv{tagc}{ltj}")

        for lt in range(16):
            bb = lt % 2
            if lt == 0:
                rec_f(0, w_fc1, "F")
            if lt + 1 < 16:
                rec_f(lt + 1, w_fc1, "F")
            for t4 in range(4):
                pp = t4 % 2
                for d in range(32):
                    em.op("pe", lambda e, bb=bb, t4=t4, d=d, pp=pp: e.matmul(
                        p_d[pp][:, :], fb16[bb][:, d * 128:(d + 1) * 128],
                        zTb[:, d * T + t4 * 512:d * T + (t4 + 1) * 512],
                        start=(d == 0), stop=(d == 31)),
                        reads=[f"fb16{bb}", "zTb", f"p_d{pp}"],
                        writes=[f"p_d{pp}"] if d in (0, 31) else [], tag=f"mmF{lt}.{t4}.{d}")
                em.op("act", lambda e, pp=pp, t4=t4: e.copy(
                    xlb[:, t4 * 512:(t4 + 1) * 512], p_d[pp][:, :]),
                    reads=[f"p_d{pp}"], writes=["xlb", f"p_d{pp}"], tag=f"evF{lt}.{t4}")
            em.dma(xlat_scr[lt], xlb[:, :], q="qact", reads=["xlb"], tag=f"spF{lt}")
        def rec_wd(k):
            ih, half = k // 2, k % 2
            em.dma(wdstg[:, :], w_down[ih * 128:(ih + 1) * 128,
                                       half * 2048:(half + 1) * 2048],
                   reads=["wdstg"], writes=["wdstgw"], tag=f"ldWd{k}")
            em.op("act", lambda e: e.copy(wdb16[:, :], wdstg[:, :]),
                  reads=["wdstgw"], writes=["wdb16", "wdstg"], tag=f"cvWd{k}")
            em.dma(wdb_scr[ih][:, half * 2048:(half + 1) * 2048], wdb16[:, :],
                   q="qact", reads=["wdb16"], tag=f"spWd{k}")

        for it in range(64):
            bb = it % 2
            if it == 0:
                rec_f(0, w_up, "U")
            if it + 1 < 64:
                rec_f(it + 1, w_up, "U")
            for k in range(it * 2, it * 2 + 2):
                rec_wd(k)
            for t4 in range(4):
                pp = t4 % 2
                for d in range(32):
                    em.op("pe", lambda e, bb=bb, t4=t4, d=d, pp=pp: e.matmul(
                        p_d[pp][:, :], fb16[bb][:, d * 128:(d + 1) * 128],
                        zTb[:, d * T + t4 * 512:d * T + (t4 + 1) * 512],
                        start=(d == 0), stop=(d == 31)),
                        reads=[f"fb16{bb}", "zTb", f"p_d{pp}"],
                        writes=[f"p_d{pp}"] if d in (0, 31) else [], tag=f"mmU{it}.{t4}.{d}")
                em.op("act", lambda e, pp=pp: e.activation(
                    rbuf[:, :], p_d[pp][:, :], AF.Relu),
                    reads=[f"p_d{pp}"], writes=["rbuf", f"p_d{pp}"], tag=f"rl{it}.{t4}")
                em.op("dve", lambda e, t4=t4: e.tensor_tensor(
                    h1b[:, t4 * 512:(t4 + 1) * 512], rbuf[:, :], rbuf[:, :], ALU.mult),
                    reads=["rbuf"], writes=["h1b"], tag=f"sq{it}.{t4}")
            em.dma(h1_scr[it], h1b[:, :], q="qact", reads=["h1b"], tag=f"spU{it}")
        close_scope()

        esT = sbs("esT", [128, 16 * T], BF16)
        xlT_h = nc.sbuf_tensor("xlT", [128, 16 * T], BF16)
        xlT = xlT_h.__enter__()
        estg_h = nc.sbuf_tensor("estg0", [128, 16 * 128], F32)
        estg = [estg_h.__enter__()]
        eb16_h = [nc.sbuf_tensor(f"eb16{i}", [128, 16 * 128], BF16) for i in range(2)]
        eb16 = [h.__enter__() for h in eb16_h]
        p_d2 = [pss(f"p_d2{i}", [128, 512]) for i in range(2)]

        em.dma(xlT[:, :].rearrange("p (a t) -> p a t", a=16),
               xlat_scr.rearrange("a p t -> p a t"), writes=["xlT"], tag="ldxl")
        def rec_e(mtj):
            bj = mtj % 2
            em.dma(estg[0][:, :].rearrange("p (a c) -> p a c", a=16),
                   w_exp[:, mtj * 128:(mtj + 1) * 128].rearrange("(a p) c -> p a c", p=128),
                   writes=["estg0"], tag=f"ldE{mtj}")
            em.op("act", lambda e, bj=bj: e.copy(eb16[bj][:, :], estg[0][:, :]),
                  reads=["estg0"], writes=[f"eb16{bj}"], tag=f"cvE{mtj}")

        for mt in range(16):
            bb = mt % 2
            if mt == 0:
                rec_e(0)
            if mt + 1 < 16:
                rec_e(mt + 1)
            for t4 in range(4):
                pp = t4 % 2
                for l in range(16):
                    em.op("pe", lambda e, bb=bb, t4=t4, l=l, pp=pp: e.matmul(
                        p_d2[pp][:, :], eb16[bb][:, l * 128:(l + 1) * 128],
                        xlT[:, l * T + t4 * 512:l * T + (t4 + 1) * 512],
                        start=(l == 0), stop=(l == 15)),
                        reads=[f"eb16{bb}", "xlT", f"p_d2{pp}"],
                        writes=[f"p_d2{pp}"] if l in (0, 15) else [], tag=f"mmE{mt}.{t4}.{l}")
                em.op("dve", lambda e, pp=pp, mt=mt, t4=t4: e.tensor_tensor(
                    esT[:, mt * T + t4 * 512:mt * T + (t4 + 1) * 512],
                    p_d2[pp][:, :], rbc[:, t4 * 512:(t4 + 1) * 512], ALU.mult),
                    reads=[f"p_d2{pp}", "rbc"], writes=["esT", f"p_d2{pp}"],
                    tag=f"evE{mt}.{t4}")
        em.fence()
        for h in [eb16_h[1], eb16_h[0], estg_h, xlT_h]:
            h.__exit__(None, None, None)
        f2stg = [sbs("f2stg0", [128, 16 * 256])]
        f2b16 = [sbs(f"f2b16{i}", [128, 16 * 512], BF16) for i in range(2)]
        rb = [sbs(f"rb{i}", [128, 512]) for i in range(2)]

        def rec_f2(dpj):
            fbj = dpj % 2
            for hq in range(2):
                em.dma(f2stg[0][:, :].rearrange("p (a c) -> p a c", a=16),
                       w_fc2[:, dpj * 512 + hq * 256:dpj * 512 + (hq + 1) * 256]
                       .rearrange("(a p) c -> p a c", p=128),
                       writes=["f2stg0"], tag=f"ldF2{dpj}.{hq}")
                em.op("act", lambda e, hq=hq, fbj=fbj: e.copy(
                    bass.AP(f2b16[fbj][:, :].tensor, f2b16[fbj][:, :].offset + hq * 256,
                            [[16 * 512, 128], [512, 16], [1, 256]]),
                    f2stg[0][:, :].rearrange("p (a c) -> p a c", a=16)),
                    reads=["f2stg0"], writes=[f"f2b16{fbj}"],
                    tag=f"cvF2{dpj}.{hq}")

        for dp2 in range(8):
            fb = dp2 % 2
            if dp2 == 0:
                rec_f2(0)
            if dp2 + 1 < 8:
                rec_f2(dp2 + 1)
            for tc in range(NCH):
                pp = tc % 2
                for m in range(16):
                    em.op("pe", lambda e, m=m, tc=tc, pp=pp, fb=fb: e.matmul(
                        p_d2[pp][:, :], esT[:, m * T + tc * 128:m * T + tc * 128 + 128],
                        f2b16[fb][:, m * 512:(m + 1) * 512],
                        start=(m == 0), stop=(m == 15)),
                        reads=["esT", f"f2b16{fb}", f"p_d2{pp}"],
                        writes=[f"p_d2{pp}"] if m in (0, 15) else [], tag=f"mmR{dp2}.{tc}.{m}")
                em.op("dve", lambda e, pp=pp: e.tensor_copy(rb[pp][:, :], p_d2[pp][:, :]),
                      reads=[f"p_d2{pp}"], writes=[f"rb{pp}", f"p_d2{pp}"],
                      tag=f"evR{dp2}.{tc}")
                em.dma(routed_scr[tc][:, dp2 * 512:(dp2 + 1) * 512], rb[pp][:, :],
                       q="qact", reads=[f"rb{pp}"], tag=f"spR{dp2}.{tc}")
        close_scope()

        wpan = sbs("wpan", [128, 64 * 512], BF16)
        h1r = [sbs(f"h1r{i}", [128, 64 * 256], BF16) for i in range(2)]
        rtb = [sbs(f"rtb{i}", [128, 512]) for i in range(2)]
        acb = [sbs(f"acb{i}", [128, 512]) for i in range(2)]
        p_d3 = [pss(f"p_d3{i}", [128, 512]) for i in range(2)]

        for dp in range(8):
            em.dma(wpan[:, :].rearrange("p (a c) -> p a c", a=64),
                   wdb_scr[:, :, dp * 512:(dp + 1) * 512].rearrange("a p c -> p a c"),
                   reads=["wpan"], writes=["wpanw"], tag=f"ldD{dp}")
            for tc in range(NCH):
                hb = (tc // 2) % 2
                if tc % 2 == 0:
                    em.dma(h1r[hb][:, :].rearrange("p (a t) -> p a t", a=64),
                           h1_scr[:, :, tc * Q:(tc + 2) * Q].rearrange("a p t -> p a t"),
                           reads=[f"h1r{hb}"], writes=[f"h1r{hb}w"], tag=f"ldh{dp}.{tc}")
                pp = tc % 2
                for i in range(64):
                    em.op("pe", lambda e, hb=hb, i=i, tc=tc, pp=pp: e.matmul(
                        p_d3[pp][:, :],
                        h1r[hb][:, i * 256 + (tc % 2) * 128:i * 256 + (tc % 2) * 128 + 128],
                        wpan[:, i * 512:(i + 1) * 512],
                        start=(i == 0), stop=(i == 63)),
                        reads=[f"h1r{hb}w", "wpanw", f"p_d3{pp}"],
                        writes=[f"p_d3{pp}"] if i in (0, 63) else [], tag=f"mmD{dp}.{tc}.{i}")
                em.dma(rtb[pp][:, :], routed_scr[tc][:, dp * 512:(dp + 1) * 512],
                       reads=[f"rtb{pp}"], writes=[f"rtb{pp}w"], tag=f"ldrt{dp}.{tc}")
                em.op("dve", lambda e, pp=pp: e.tensor_add(
                    acb[pp][:, :], p_d3[pp][:, :], rtb[pp][:, :]),
                    reads=[f"p_d3{pp}", f"rtb{pp}w"],
                    writes=[f"acb{pp}", f"p_d3{pp}", f"rtb{pp}"], tag=f"evD{dp}.{tc}")
                em.dma(acc_scr[tc][:, dp * 512:(dp + 1) * 512], acb[pp][:, :],
                       q="qact", reads=[f"acb{pp}"], tag=f"spD{dp}.{tc}")
        close_scope()

        gbc = sbs("gbc", [128, D])
        bbc = sbs("bbc", [128, D])
        ac4_ = [sbs(f"ac4_{i}", [128, D]) for i in range(2)]
        zc4_ = [sbs(f"zc4_{i}", [128, D]) for i in range(2)]
        f1_ = [sbs("f1_s", [128, D])] * 2
        f2_ = [sbs("f2_s", [128, D])] * 2
        trash5 = sbs("trash5", [128, D])
        obuf_ = [sbs(f"obuf_{i}", [128, D]) for i in range(2)]
        lc_ = [sbs(f"lc_{i}", [128, 10]) for i in range(2)]

        em.dma(gbc[:, :], strided(gam_in, [[0, 128], [1, D]]), writes=["gbc"], tag="ldg")
        em.dma(bbc[:, :], strided(bet_in, [[0, 128], [1, D]]), writes=["bbc"], tag="ldb")
        for tc in range(NCH):
            pp4 = tc % 2
            ac4, zc4, f1, f2, obuf, lc = (ac4_[pp4], zc4_[pp4], f1_[pp4], f2_[pp4], obuf_[pp4], lc_[pp4])
            em.dma(ac4[:, :], acc_scr[tc], reads=["ac4" + str(pp4)], writes=["ac4w" + str(pp4)], tag=f"lda4{tc}")
            em.dma(zc4[:, :], z_scr[tc], reads=["zc4" + str(pp4)], writes=["zc4w" + str(pp4)], tag=f"ldz4{tc}")
            em.op("dve", lambda e, ac4=ac4, zc4=zc4, f1=f1, f2=f2, obuf=obuf, lc=lc: e.tensor_reduce(lc[:, 0:1], ac4[:, :], AX.X, ALU.add),
                  reads=["ac4w" + str(pp4)], writes=["lc0" + str(pp4)], tag=f"s4.{tc}", force=True)
            em.op("act", lambda e, ac4=ac4, zc4=zc4, f1=f1, f2=f2, obuf=obuf, lc=lc: e.activation(trash5[:, :], ac4[:, :], AF.Square,
                                                accum_out=lc[:, 1:2]),
                  reads=["ac4w" + str(pp4)], writes=["trash5", "lc1" + str(pp4)], tag=f"q4.{tc}", force=True)
            em.op("dve", lambda e, ac4=ac4, zc4=zc4, f1=f1, f2=f2, obuf=obuf, lc=lc: e.tensor_scalar_mul(lc[:, 2:3], lc[:, 0:1], 1.0 / D),
                  reads=["lc0" + str(pp4)], writes=["lc2" + str(pp4)], tag=f"mn4.{tc}", force=True)
            em.op("dve", lambda e, ac4=ac4, zc4=zc4, f1=f1, f2=f2, obuf=obuf, lc=lc: e.tensor_tensor(lc[:, 3:4], lc[:, 2:3], lc[:, 2:3],
                                                   ALU.mult),
                  reads=["lc2" + str(pp4)], writes=["lc3" + str(pp4)], tag=f"m24.{tc}", force=True)
            em.op("dve", lambda e, ac4=ac4, zc4=zc4, f1=f1, f2=f2, obuf=obuf, lc=lc: e.tensor_scalar(lc[:, 4:5], lc[:, 1:2], 1.0 / D,
                                                   lc[:, 3:4], ALU.mult, ALU.subtract),
                  reads=["lc1" + str(pp4), "lc3" + str(pp4)], writes=["lc4" + str(pp4)], tag=f"v4.{tc}", force=True)
            em.op("act", lambda e, ac4=ac4, zc4=zc4, f1=f1, f2=f2, obuf=obuf, lc=lc: e.activation(lc[:, 5:6], lc[:, 4:5], AF.Ln,
                                                bias=cst[:, 0:1]),
                  reads=["lc4" + str(pp4), "cst"], writes=["lc5" + str(pp4)], tag=f"l4.{tc}", force=True)
            em.op("act", lambda e, ac4=ac4, zc4=zc4, f1=f1, f2=f2, obuf=obuf, lc=lc: e.activation(lc[:, 6:7], lc[:, 5:6], AF.Exp,
                                                scale=cst[:, 2:3]),
                  reads=["lc5" + str(pp4), "cst"], writes=["lc6" + str(pp4)], tag=f"r4.{tc}", force=True)
            em.op("dve", lambda e, ac4=ac4, zc4=zc4, f1=f1, f2=f2, obuf=obuf, lc=lc: e.tensor_scalar(
                f1[:, :], ac4[:, :], lc[:, 2:3], lc[:, 6:7], ALU.subtract, ALU.mult),
                reads=["ac4w" + str(pp4), "lc2" + str(pp4), "lc6" + str(pp4)], writes=["f1", "ac4" + str(pp4)], tag=f"n4.{tc}")
            em.op("pool", lambda e, f1=f1, f2=f2: e.tensor_mul(f2[:, :], f1[:, :], gbc[:, :]),
                  reads=["f1", "gbc"], writes=["f2"], tag=f"g4.{tc}")
            em.op("pool", lambda e, zc4=zc4, f1=f1: e.tensor_add(f1[:, :], zc4[:, :], bbc[:, :]),
                  reads=["zc4w" + str(pp4), "bbc"], writes=["f1", "zc4" + str(pp4)], tag=f"zb4.{tc}")
            em.op("dve", lambda e, ac4=ac4, zc4=zc4, f1=f1, f2=f2, obuf=obuf, lc=lc: e.scalar_tensor_tensor(
                f2[:, :], f2[:, :], 1.0, f1[:, :], ALU.mult, ALU.add,
                accum_out=lc[:, 7:8]),
                reads=["f1", "f2"], writes=["f2", "lc7" + str(pp4)], tag=f"z2.{tc}", force=True)
            em.op("act", lambda e, ac4=ac4, zc4=zc4, f1=f1, f2=f2, obuf=obuf, lc=lc: e.activation(trash5[:, :], f2[:, :], AF.Square,
                                                accum_out=lc[:, 8:9]),
                  reads=["f2"], writes=["trash5", "lc8" + str(pp4)], tag=f"q5.{tc}", force=True)
            em.op("dve", lambda e, ac4=ac4, zc4=zc4, f1=f1, f2=f2, obuf=obuf, lc=lc: e.tensor_scalar_mul(lc[:, 2:3], lc[:, 7:8], 1.0 / D),
                  reads=["lc7" + str(pp4)], writes=["lc2" + str(pp4)], tag=f"mn5.{tc}", force=True)
            em.op("dve", lambda e, ac4=ac4, zc4=zc4, f1=f1, f2=f2, obuf=obuf, lc=lc: e.tensor_tensor(lc[:, 3:4], lc[:, 2:3], lc[:, 2:3],
                                                   ALU.mult),
                  reads=["lc2" + str(pp4)], writes=["lc3" + str(pp4)], tag=f"m25.{tc}", force=True)
            em.op("dve", lambda e, ac4=ac4, zc4=zc4, f1=f1, f2=f2, obuf=obuf, lc=lc: e.tensor_scalar(lc[:, 4:5], lc[:, 8:9], 1.0 / D,
                                                   lc[:, 3:4], ALU.mult, ALU.subtract),
                  reads=["lc8" + str(pp4), "lc3" + str(pp4)], writes=["lc4" + str(pp4)], tag=f"v5.{tc}", force=True)
            em.op("act", lambda e, ac4=ac4, zc4=zc4, f1=f1, f2=f2, obuf=obuf, lc=lc: e.activation(lc[:, 5:6], lc[:, 4:5], AF.Ln,
                                                bias=cst[:, 0:1]),
                  reads=["lc4" + str(pp4), "cst"], writes=["lc5" + str(pp4)], tag=f"l5.{tc}", force=True)
            em.op("act", lambda e, ac4=ac4, zc4=zc4, f1=f1, f2=f2, obuf=obuf, lc=lc: e.activation(lc[:, 6:7], lc[:, 5:6], AF.Exp,
                                                scale=cst[:, 2:3]),
                  reads=["lc5" + str(pp4), "cst"], writes=["lc6" + str(pp4)], tag=f"r5.{tc}", force=True)
            em.op("dve", lambda e, ac4=ac4, zc4=zc4, f1=f1, f2=f2, obuf=obuf, lc=lc: e.tensor_scalar(
                obuf[:, :], f2[:, :], lc[:, 2:3], lc[:, 6:7], ALU.subtract, ALU.mult),
                reads=["f2", "lc2" + str(pp4), "lc6" + str(pp4)], writes=["obuf" + str(pp4)], tag=f"o.{tc}")
            em.dma(out[tc * Q:(tc + 1) * Q, :], obuf[:, :], q="qact",
                   reads=["obuf" + str(pp4)], writes=["obuf" + str(pp4)], tag=f"out{tc}")
        close_scope()

    em.fence()
    em.op("sp", lambda e: e.nop(), tag="final")
    em.emit(nc)
    for h in reversed(persist):
        h.__exit__(None, None, None)
    return nc


_NC_CACHE = None


def _get_nc():
    global _NC_CACHE
    if _NC_CACHE is None:
        nc = bass.Bass("TRN2", target_bir_lowering=False)
        build(nc)
        _NC_CACHE = nc
    return _NC_CACHE


def make_in_maps(inputs):
    x = np.asarray(inputs["L0_attn_hidden_states"], dtype=np.float32)
    B = x.shape[0]
    shared = {
        "winT": np.ascontiguousarray(np.asarray(inputs["L0_attn_in_proj_w"]).T),
        "woutT": np.ascontiguousarray(np.asarray(inputs["L0_attn_out_proj_w"]).T),
        "w_up": np.ascontiguousarray(np.asarray(inputs["L0_ffn_up_w"])),
        "w_down": np.ascontiguousarray(np.asarray(inputs["L0_ffn_down_w"])),
        "w_fc1": np.ascontiguousarray(np.asarray(inputs["L0_ffn_fc1_w"])),
        "w_fc2": np.ascontiguousarray(np.asarray(inputs["L0_ffn_fc2_w"])),
        "w_exp": np.ascontiguousarray(np.asarray(inputs["L0_ffn_expert_w"])),
        "w_gate": np.ascontiguousarray(np.asarray(inputs["L0_ffn_gate_w"])),
        "convw": np.ascontiguousarray(
            np.asarray(inputs["L0_attn_conv_weight"]).reshape(48, 128, 4).transpose(1, 0, 2)),
        "convb": np.ascontiguousarray(
            np.asarray(inputs["L0_attn_conv_bias"]).reshape(48, 128).T),
        "dtbias": np.ascontiguousarray(
            np.asarray(inputs["L0_attn_dt_bias"]).reshape(NH, 1)),
        "alog": np.ascontiguousarray(np.asarray(inputs["L0_attn_A_log"]).reshape(NH, 1)),
        "dvec": np.ascontiguousarray(np.asarray(inputs["L0_attn_D"])),
        "nw": np.ascontiguousarray(
            np.asarray(inputs["L0_attn_norm_weight"]).reshape(32, 128).T),
        "gam": np.ascontiguousarray(np.asarray(inputs["L0_ffn_ln_gamma"])),
        "bet": np.ascontiguousarray(np.asarray(inputs["L0_ffn_ln_beta"])),
    }
    in_maps = []
    for b in range(B):
        m = dict(shared)
        m["x"] = np.ascontiguousarray(x[b])
        m["xT"] = np.ascontiguousarray(x[b].T)
        in_maps.append(m)
    return in_maps


def kernel(**inputs):
    in_maps = make_in_maps(inputs)
    nc = _get_nc()
    res = run_bass_kernel_spmd(nc, in_maps, core_ids=list(range(len(in_maps))))
    return np.stack([np.asarray(r["out"], dtype=np.float32) for r in res.results])


# version 15 (final)
# speedup vs baseline: 1.1724x; 1.0503x over previous
import numpy as np
import ml_dtypes
import concourse.bass as bass
import concourse.mybir as mybir
from concourse.bass_utils import run_bass_kernel_spmd

F32 = mybir.dt.float32
BF16 = mybir.dt.bfloat16
AF = mybir.ActivationFunctionType
ALU = mybir.AluOpType
AX = mybir.AxisListType

DEBUG = False
STAGES = (1, 1, 1, 1)

T = 2048
D = 4096
NH, HP, SN, NG = 64, 64, 128, 8
I_SSM = NH * HP
GN = NG * SN
CONV = I_SSM + 2 * GN
PROJ = I_SSM + CONV + NH
Q = 128
NCH = T // Q
INT, LAT, NE = 8192, 2048, 8
EPS = 1e-5

ENGINES = ("pe", "act", "dve", "pool", "sp")


class Em:

    def __init__(self):
        self.ops = []
        self.tok = {}
        self.fence_ids = set()
        self.dma_counts = {"qsp": 0, "qact": 0}

    def op(self, eng, fn, reads=(), writes=(), dma_q=None, tag="", force=False):
        oid = len(self.ops)
        deps = set()
        for t in reads:
            st = self.tok.get(t)
            if st and st["w"] is not None:
                deps.add(st["w"])
        for t in writes:
            st = self.tok.get(t)
            if st:
                if st["w"] is not None:
                    deps.add(st["w"])
                deps.update(st["r"])
        deps.update(self.fence_ids)
        deps.discard(oid)
        if deps:
            best = {}
            for d in deps:
                p = self.ops[d]
                key = ("q", p["dma_q"]) if p["dma_q"] is not None else ("e", p["eng"])
                if key not in best or d > best[key]:
                    best[key] = d
            deps = set(best.values())
        dma_idx = None
        if dma_q is not None:
            self.dma_counts[dma_q] += 1
            dma_idx = self.dma_counts[dma_q]
        self.ops.append({"id": oid, "eng": eng, "fn": fn, "deps": deps,
                         "dma_q": dma_q, "dma_idx": dma_idx, "tag": tag,
                         "force": force})
        for t in reads:
            self.tok.setdefault(t, {"w": None, "r": []})["r"].append(oid)
        for t in writes:
            self.tok[t] = {"w": oid, "r": []}
        return oid

    def dma(self, dst, src, q="qsp", reads=(), writes=(), tag="", **kw):
        eng = "sp" if q == "qsp" else "act"
        return self.op(eng, lambda e, d=dst, s=src: e.dma_start(d, s, **kw),
                       reads=reads, writes=writes, dma_q=q, tag=tag)

    def fence(self):
        last_eng = {}
        last_dma = {}
        for o in self.ops:
            if o["dma_q"] is not None:
                last_dma[o["dma_q"]] = o["id"]
            else:
                last_eng[o["eng"]] = o["id"]
        self.fence_ids = set(last_eng.values()) | set(last_dma.values())

    @staticmethod
    def _elide(p, c):
        if c["dma_q"] is not None or p.get("force"):
            return False
        return p["dma_q"] is None and p["eng"] == c["eng"]

    def emit(self, nc):
        ops = self.ops
        needed = set()
        for o in ops:
            for d in o["deps"]:
                if not self._elide(ops[d], o):
                    needed.add(d)
        ms = {e: 0 for e in ENGINES}
        for o in ops:
            if o["dma_q"] is not None:
                o["ms"] = (o["dma_q"], 16 * o["dma_idx"])
            elif o["id"] in needed:
                ms[o["eng"]] += 1
                o["ms"] = (o["eng"], ms[o["eng"]])
            else:
                o["ms"] = None
        clocks = {e: {} for e in ENGINES}
        per_eng = {e: [] for e in ENGINES}
        for o in ops:
            c = clocks[o["eng"]]
            waits = []
            for d in sorted(o["deps"]):
                p = ops[d]
                if self._elide(p, o):
                    continue
                sem, val = p["ms"]
                if c.get(sem, 0) < val:
                    waits.append((sem, val))
                    c[sem] = val
            o["waits"] = waits
            per_eng[o["eng"]].append(o)

        sems = {}
        hs = []
        for name in ("pe", "act", "dve", "pool", "qsp", "qact"):
            h = nc.semaphore("sem_" + name)
            sems[name] = h.__enter__()
            hs.append(h)

        def run(engine_obj, eng_name):
            for o in per_eng[eng_name]:
                for (sem, val) in o["waits"]:
                    engine_obj.wait_ge(sems[sem], val)
                inst = o["fn"](engine_obj)
                if o["dma_q"] is not None:
                    inst.then_inc(sems[o["dma_q"]], 16)
                elif o["id"] in needed:
                    inst.then_inc(sems[o["eng"]], 1)

        with nc.Block() as blk:
            @blk.tensor
            def _(e):
                run(e, "pe")

            @blk.scalar
            def _(e):
                run(e, "act")

            @blk.vector
            def _(e):
                run(e, "dve")

            @blk.gpsimd
            def _(e):
                run(e, "pool")

            @blk.sync
            def _(e):
                run(e, "sp")
        for h in reversed(hs):
            h.__exit__(None, None, None)


def strided(base_ap, pattern):
    return bass.AP(base_ap.tensor, base_ap.offset, pattern)


def build(nc):
    em = Em()

    def din(name, shape, dt=F32):
        return nc.dram_tensor(name, shape, dt, kind="ExternalInput").ap()

    def dscr(name, shape, dt=F32):
        kind = "ExternalOutput" if DEBUG else "Internal"
        return nc.dram_tensor(name, shape, dt, kind=kind).ap()

    xT_in = din("xT", [D, T])
    x_in = din("x", [T, D])
    winT = din("winT", [D, PROJ])
    woutT = din("woutT", [I_SSM, D])
    w_up = din("w_up", [D, INT])
    w_down = din("w_down", [INT, D])
    w_fc1 = din("w_fc1", [D, LAT])
    w_fc2 = din("w_fc2", [LAT, D])
    w_exp = din("w_exp", [LAT, LAT])
    w_gate = din("w_gate", [D, NE])
    convw = din("convw", [128, 48, 4])
    convb = din("convb", [128, 48])
    dtbias = din("dtbias", [NH, 1])
    alog = din("alog", [NH, 1])
    dvec = din("dvec", [NH])
    nw_in = din("nw", [128, 32])
    gam_in = din("gam", [D])
    bet_in = din("bet", [D])
    out = nc.dram_tensor("out", [T, D], F32, kind="ExternalOutput").ap()

    gate_scr = dscr("gate_scr", [32, 128, T], BF16)
    xbc_scr = dscr("xbc_scr", [48, 128, T], BF16)
    ynT_scr = dscr("ynT_scr", [32, 128, T], BF16)
    attn_scr = dscr("attn_scr", [NCH, 128, D])
    z_scr = dscr("z_scr", [NCH, 128, D])
    zT_scr = dscr("zT_scr", [32, 128, T], BF16)
    s_scr = dscr("s_scr", [NH, Q])
    r_scr = dscr("r_scr", [1, T])
    xlat_scr = dscr("xlat_scr", [16, 128, T], BF16)
    h1_scr = dscr("h1_scr", [64, 128, T], BF16)
    wdb_scr = dscr("wdb_scr", [64, 128, D], BF16)
    routed_scr = dscr("routed_scr", [NCH, 128, D])
    acc_scr = dscr("acc_scr", [NCH, 128, D])
    if DEBUG:
        dt_dbg = dscr("dt_dbg", [NH, T])
        dbg_dtsT = dscr("dbg_dtsT", [128, 128])
        dbg_lam = dscr("dbg_lam", [128, NH * Q], BF16)
        dbg_xnat = dscr("dbg_xnat", [128, I_SSM], BF16)
        dbg_xdt = dscr("dbg_xdt", [128, 512], BF16)
        dbg_gam = dscr("dbg_gam", [128, 8 * Q], BF16)
        dbg_yg = dscr("dbg_yg", [128, 512])
        dbg_y1 = dscr("dbg_y1", [128, 512])
        dbg_sb = dscr("dbg_sb", [128, I_SSM], BF16)
        dbg_yn16 = dscr("dbg_yn16", [128, 512], BF16)
        dbg_gcol = dscr("dbg_gcol", [128, 4])
        dbg_ynTb = dscr("dbg_ynTb", [128, 32 * 256], BF16)

    persist = []

    def sbp(name, shape, dt=F32):
        h = nc.sbuf_tensor(name, shape, dt)
        t = h.__enter__()
        persist.append(h)
        return t

    scope_hs = []

    def sbs(name, shape, dt=F32):
        h = nc.sbuf_tensor(name, shape, dt)
        t = h.__enter__()
        scope_hs.append(h)
        return t

    def pss(name, shape, dt=F32):
        h = nc.psum_tensor(name, shape, dt)
        t = h.__enter__()
        scope_hs.append(h)
        return t

    def close_scope():
        em.fence()
        for h in reversed(scope_hs):
            h.__exit__(None, None, None)
        scope_hs.clear()

    dt_sb = sbp("dt_sb", [NH, T])
    ident = sbp("ident", [128, 128], BF16)
    identf = sbp("identf", [128, 128], F32)
    convw_sb = sbp("convw_sb", [128, 48 * 4])
    convb_sb = sbp("convb_sb", [128, 48])
    small = sbp("small", [NH, 8])
    dbcp = sbp("dbcp", [128, NH])
    cst = sbp("cst", [128, 4])

    em.op("dve", lambda e: e.memset(cst[:, 0:1], EPS), writes=["cst"])
    em.op("dve", lambda e: e.memset(cst[:, 1:2], 1.0 / 512), writes=["cst"])
    em.op("dve", lambda e: e.memset(cst[:, 2:3], -0.5), writes=["cst"])
    em.op("pool", lambda e: e.memset(ident[:, :], 1.0), writes=["ident"])
    em.op("pool", lambda e: e.affine_select(ident[:, :], ident[:, :], [[1, 128]],
                                            ALU.is_equal, 0.0, channel_multiplier=-1),
          reads=["ident"], writes=["ident"])
    em.op("pool", lambda e: e.memset(identf[:, :], 1.0), writes=["identf"])
    em.op("pool", lambda e: e.affine_select(identf[:, :], identf[:, :], [[1, 128]],
                                            ALU.is_equal, 0.0, channel_multiplier=-1),
          reads=["identf"], writes=["identf"])
    em.dma(convw_sb[:, :], convw.rearrange("p a k -> p (a k)"), writes=["convw"])
    em.dma(convb_sb[:, :], convb, writes=["convb"])
    em.dma(small[:, 0:1], dtbias, writes=["small0"])
    em.dma(small[:, 1:2], alog, writes=["small1"])
    em.dma(dbcp[:, :], strided(dvec, [[0, 128], [1, NH]]), writes=["dbcp"])
    em.op("act", lambda e: e.activation(small[:, 2:3], small[:, 1:2], AF.Exp),
          reads=["small1"], writes=["smallA"])
    em.op("dve", lambda e: e.tensor_scalar_mul(small[:, 2:3], small[:, 2:3], -1.0),
          reads=["smallA"], writes=["smallA"])

    if STAGES[0]:
        xTb = sbs("xTb", [128, 32 * T], BF16)
        wstg = [sbs("wstg0", [128, 32 * 128])]
        wb16 = [sbs(f"wb16{i}", [128, 32 * 128], BF16) for i in range(2)]
        hpad = sbs("hpad", [128, 3 + T])
        accA = sbs("accA", [128, T])
        accB = sbs("accB", [128, T])
        xstg = [accA, accB]
        xout = [sbs("xout0", [128, T], BF16)]
        pbank = [pss(f"pbA{i}", [128, 512]) for i in range(8)]

        em.op("dve", lambda e: e.memset(hpad[:, 0:3], 0.0), writes=["hpad"])

        XSTOK = ["accA", "accB"]
        for dt_i in range(32):
            b = dt_i % 2
            em.dma(xstg[b][:, :], xT_in[dt_i * 128:(dt_i + 1) * 128, :],
                   writes=[XSTOK[b]], tag=f"ld_xT{dt_i}")
            em.op("act", lambda e, b=b, dt_i=dt_i: e.copy(
                xTb[:, dt_i * T:(dt_i + 1) * T], xstg[b][:, :]),
                reads=[XSTOK[b]], writes=[f"xTb{dt_i}"], tag=f"cv_xT{dt_i}")

        CT_ORDER = list(range(32, 81)) + list(range(32))

        def rec_w(cj):
            ctj = CT_ORDER[cj]
            cwj = 64 if ctj == 80 else 128
            bj = cj % 2
            em.dma(wstg[0][:, 0:32 * cwj].rearrange("p (a c) -> p a c", a=32),
                   winT[:, ctj * 128:ctj * 128 + cwj].rearrange("(a p) c -> p a c", p=128),
                   writes=["wstg0"], tag=f"ld_w{ctj}")
            em.op("act", lambda e, bj=bj, cwj=cwj: e.copy(
                wb16[bj][:, 0:32 * cwj], wstg[0][:, 0:32 * cwj]),
                reads=["wstg0"], writes=[f"wb16{bj}"], tag=f"cv_w{ctj}")

        for ci, ct in enumerate(CT_ORDER):
            cw = 64 if ct == 80 else 128
            b = ci % 2
            bank0 = 4 * (ci % 2)
            if ci == 0:
                rec_w(0)
            if ci + 1 < len(CT_ORDER):
                rec_w(ci + 1)
            for t in range(4):
                bank = pbank[bank0 + t]
                btok = f"pbA{bank0 + t}"
                for d in range(32):
                    first, last = (d == 0), (d == 31)
                    oap = bank[:, :] if cw == 128 else bank[0:64, :]
                    em.op("pe", lambda e, b=b, cw=cw, t=t, d=d, oap=oap:
                          e.matmul(oap, wb16[b][:, d * cw:(d + 1) * cw],
                                   xTb[:, d * T + t * 512:d * T + (t + 1) * 512],
                                   start=(d == 0), stop=(d == 31)),
                          reads=[f"wb16{b}", f"xTb{d}"],
                          writes=[btok] if (first or last) else [],
                          tag=f"mmA{ct}.{t}.{d}")
            if ct < 32:
                xo = xout[0]
                for t in range(4):
                    btok = f"pbA{bank0 + t}"
                    em.op("act", lambda e, t=t, bank=pbank[bank0 + t], xo=xo:
                          e.activation(xo[:, t * 512:(t + 1) * 512], bank[:, :], AF.Silu),
                          reads=[btok], writes=["xout0", btok], tag=f"siluG{ct}.{t}")
                em.dma(gate_scr[ct], xo[:, :], q="qact",
                       reads=["xout0"], tag=f"spG{ct}")
            elif ct < 80:
                j = ct - 32
                for t in range(4):
                    btok = f"pbA{bank0 + t}"
                    em.op("dve", lambda e, t=t, bank=pbank[bank0 + t]:
                          e.tensor_copy(hpad[:, 3 + t * 512:3 + (t + 1) * 512], bank[:, :]),
                          reads=[btok], writes=["hpad", btok], tag=f"evH{ct}.{t}")
                em.op("dve", lambda e, j=j: e.tensor_scalar(
                    accA[:, :], hpad[:, 3:3 + T], convw_sb[:, j * 4 + 3:j * 4 + 4],
                    None, ALU.mult),
                    reads=["hpad", "convw"], writes=["accA"], tag=f"cv3.{j}")
                em.op("dve", lambda e, j=j: e.scalar_tensor_tensor(
                    accB[:, :], hpad[:, 2:2 + T], convw_sb[:, j * 4 + 2:j * 4 + 3],
                    accA[:, :], ALU.mult, ALU.add),
                    reads=["hpad", "convw", "accA"], writes=["accB"], tag=f"cv2.{j}")
                em.op("dve", lambda e, j=j: e.scalar_tensor_tensor(
                    accA[:, :], hpad[:, 1:1 + T], convw_sb[:, j * 4 + 1:j * 4 + 2],
                    accB[:, :], ALU.mult, ALU.add),
                    reads=["hpad", "convw", "accB"], writes=["accA"], tag=f"cv1.{j}")
                em.op("dve", lambda e, j=j: e.scalar_tensor_tensor(
                    accB[:, :], hpad[:, 0:0 + T], convw_sb[:, j * 4 + 0:j * 4 + 1],
                    accA[:, :], ALU.mult, ALU.add),
                    reads=["hpad", "convw", "accA"], writes=["accB"], tag=f"cv0.{j}")
                xo = xout[0]
                em.op("act", lambda e, j=j, xo=xo: e.activation(
                    xo[:, :], accB[:, :], AF.Silu, bias=convb_sb[:, j:j + 1]),
                    reads=["accB", "convb"], writes=["xout0"], tag=f"siluX{j}")
                em.dma(xbc_scr[j], xo[:, :], q="qact",
                       reads=["xout0"], tag=f"spX{j}")
            else:
                for t in range(4):
                    btok = f"pbA{bank0 + t}"
                    em.op("dve", lambda e, t=t, bank=pbank[bank0 + t]:
                          e.tensor_copy(dt_sb[:, t * 512:(t + 1) * 512], bank[0:64, :]),
                          reads=[btok], writes=["dt_sb", btok], tag=f"evDT{t}")
        if DEBUG:
            em.dma(dt_dbg, dt_sb[:, :], q="qact", reads=["dt_sb"])
        close_scope()

    rbc = sbp("rbc", [128, T])
    route_cols = sbp("route_cols", [128, NCH])

    if STAGES[1]:
        xbcC = [sbs("xbcC0", [128, 48 * 256], BF16)]
        gateC = [sbs("gateC0", [128, 32 * 256], BF16)]
        S_f32 = sbs("S_f32", [128, I_SSM])
        S_b16 = sbs("S_b16", [128, I_SSM], BF16)
        bcast = sbs("bcast", [128, NH * Q])
        tb = bcast[0:64, 0:T]
        tb2 = bcast[0:64, T:2 * T]
        dtp = bcast[0:64, 2 * T:3 * T]
        lamT = sbs("lamT", [128, NH * Q], BF16)
        LcumF = sbs("LcumF", [128, 128])
        Abc = sbs("Abc", [128, NH])
        onesF = sbs("onesF", [1, 128])
        dtpT_all = sbs("dtpT_all", [128, NCH * NH])
        aT_all = sbs("aT_all", [128, NCH * NH])
        sT_all = sbs("sT_all", [128, NCH * NH])
        s_q = sbs("s_q", [NH, 512])
        slast_row = sbs("slast_row", [1, NCH * NH])
        edecSarg = sbs("edecSarg", [128, NCH * NH])
        edecS_all = sbs("edecS_all", [128, NCH * NH])
        expS_all = aT_all
        X_nat = sbs("X_nat", [128, I_SSM], BF16)
        xdt_g = [sbs(f"xdt_g{i}", [128, 512], BF16) for i in range(2)]
        xd2_g = [sbs(f"xd2_g{i}", [128, 512], BF16) for i in range(2)]
        gam_g = [sbs(f"gam_g{i}", [128, 8 * Q], BF16) for i in range(2)]
        bn_g = [sbs(f"bn_g{i}", [128, 128], BF16) for i in range(2)]
        ytmp = sbs("ytmp", [128, 512])
        xdterm = [sbs(f"xdterm{i}", [128, 512]) for i in range(2)]
        yg = [sbs(f"yg{i}", [128, 512]) for i in range(2)]
        gcol = sbs("gcol", [128, 4])
        yn16 = [sbs(f"yn16{i}", [128, 512], BF16) for i in range(2)]
        stmp = [sbs(f"stmpB{i}", [128, 512]) for i in range(2)]
        ynTb = sbs("ynTb", [128, 32 * 256], BF16)
        trashB = sbs("trashB", [128, 512])
        p_cb = [pss("p_cb0", [128, 512])]
        p_y1 = [pss(f"p_y1{i}", [128, 512]) for i in range(2)]
        p_y2 = [pss(f"p_y2{i}", [128, 512]) for i in range(2)]
        p_ds = pss("p_ds", [128, 512])
        p_tr = [pss(f"p_tr{i}", [128, 1024], BF16) for i in range(2)]
        p_st = p_y1[0]

        a_dram = dscr("a_dram", [NH, 1])
        s_scr_all = dscr("s_scr_all", [NCH, NH, Q])

        em.op("pool", lambda e: e.memset(LcumF[:, :], 1.0), writes=["LcumF"])
        em.op("pool", lambda e: e.affine_select(LcumF[:, :], LcumF[:, :], [[1, 128]],
                                                ALU.is_ge, 0.0, channel_multiplier=-1),
              reads=["LcumF"], writes=["LcumF"])
        em.op("dve", lambda e: e.memset(onesF[:, :], 1.0), writes=["onesF"])

        em.op("dve", lambda e: e.tensor_scalar(tb, dt_sb[:, :],
                                               small[:, 0:1], None, ALU.add),
              reads=["dt_sb", "small0", "bcast"], writes=["tbB", "bcast"], tag="sp_add")
        em.op("act", lambda e: e.activation(tb2, tb, AF.Abs),
              reads=["tbB", "bcast"], writes=["tb2B", "bcast"], tag="sp_abs")
        em.op("act", lambda e: e.activation(tb2, tb2, AF.Exp, scale=-1.0),
              reads=["tb2B"], writes=["tb2B"], tag="sp_exp")
        em.op("act", lambda e: e.activation(tb2, tb2, AF.Ln, bias=1.0),
              reads=["tb2B"], writes=["tb2B"], tag="sp_ln")
        em.op("dve", lambda e: e.tensor_relu(dtp, tb),
              reads=["tbB"], writes=["dtpB", "bcast"], tag="sp_relu")
        em.op("dve", lambda e: e.tensor_add(dtp, dtp, tb2),
              reads=["dtpB", "tb2B"], writes=["dtpB", "bcast"], tag="sp_sum")

        em.dma(a_dram, small[:, 2:3], reads=["smallA"], writes=["a_dram"], tag="sp_aw")
        em.dma(Abc[:, :], strided(a_dram, [[0, 128], [1, NH]]),
               reads=["a_dram"], writes=["Abc"], tag="sp_ar")

        for half in range(2):
            for j in range(8):
                c = half * 8 + j
                em.op("pe", lambda e, c=c, j=j: e.matmul(
                    p_st[:, j * 64:(j + 1) * 64], dtp[:, c * Q:(c + 1) * Q],
                    identf[0:64, 0:64], start=True, stop=True),
                    reads=["dtpB", "identf", "p_y10"],
                    writes=["p_y10"] if j in (0, 7) else [], tag=f"trdtp{c}")
            em.op("dve", lambda e, half=half: e.tensor_copy(
                dtpT_all[:, half * 512:(half + 1) * 512], p_st[:, :]),
                reads=["p_y10"], writes=["dtpT", "p_y10"], tag=f"evdtp{half}")
        em.op("dve", lambda e: e.tensor_tensor(
            aT_all[:, :].rearrange("p (c h) -> p c h", c=NCH),
            dtpT_all[:, :].rearrange("p (c h) -> p c h", c=NCH),
            strided(Abc[:, :], [[NH, 128], [0, NCH], [1, NH]]), ALU.mult),
            reads=["dtpT", "Abc"], writes=["aT_all"], tag="aTmul")
        for half in range(2):
            for j in range(8):
                c = half * 8 + j
                em.op("pe", lambda e, c=c, j=j: e.matmul(
                    p_st[:, j * 64:(j + 1) * 64], LcumF[:, :],
                    aT_all[:, c * NH:(c + 1) * NH], start=True, stop=True),
                    reads=["LcumF", "aT_all", "p_y10"],
                    writes=["p_y10"] if j in (0, 7) else [], tag=f"sT{c}")
            em.op("dve", lambda e, half=half: e.tensor_copy(
                sT_all[:, half * 512:(half + 1) * 512], p_st[:, :]),
                reads=["p_y10"], writes=["sT_all", "p_y10"], tag=f"evsT{half}")
        for qtr in range(4):
            for j in range(4):
                c = qtr * 4 + j
                em.op("pe", lambda e, c=c, j=j: e.matmul(
                    p_st[0:64, j * 128:(j + 1) * 128],
                    aT_all[:, c * NH:(c + 1) * NH], LcumF[:, :],
                    start=True, stop=True),
                    reads=["aT_all", "LcumF", "p_y10"],
                    writes=["p_y10"] if j in (0, 3) else [], tag=f"sN{c}")
            em.op("dve", lambda e, qtr=qtr: e.tensor_copy(
                s_q[:, :], p_st[0:64, :]),
                reads=["p_y10", "s_qT"], writes=["s_q", "p_y10"], tag=f"evsN{qtr}")
            em.dma(bass.AP(s_scr_all.tensor, s_scr_all.offset + qtr * 4 * NH * Q,
                           [[Q, NH], [NH * Q, 4], [1, Q]]),
                   s_q[:, :].rearrange("h (c t) -> h c t", c=4),
                   reads=["s_q"], writes=["s_qT", "s_scr"], tag=f"sp_s{qtr}")
        em.dma(slast_row[0:1, :], sT_all[127:128, :],
               reads=["sT_all"], writes=["slast"], tag="slast")
        for j in range(2):
            em.op("pe", lambda e, j=j: e.matmul(
                p_st[:, :], onesF[0:1, :], slast_row[0:1, j * 512:(j + 1) * 512],
                start=True, stop=True),
                reads=["onesF", "slast", "p_y10"], writes=["p_y10"], tag=f"r1s{j}")
            em.op("dve", lambda e, j=j: e.tensor_copy(
                edecSarg[:, j * 512:(j + 1) * 512], p_st[:, :]),
                reads=["p_y10"], writes=["edecSarg", "p_y10"], tag=f"evr1{j}")
        em.op("act", lambda e: e.activation(edecS_all[:, :], edecSarg[:, :], AF.Exp),
              reads=["edecSarg"], writes=["edecS_all"], tag="edecS")
        em.op("act", lambda e: e.activation(expS_all[:, :], sT_all[:, :], AF.Exp),
              reads=["sT_all"], writes=["expS_all", "aT_all"], tag="expS")
        em.op("dve", lambda e: e.tensor_sub(edecSarg[:, :], edecSarg[:, :], sT_all[:, :]),
              reads=["edecSarg", "sT_all"], writes=["edecSarg"], tag="edTa")
        em.op("act", lambda e: e.activation(edecSarg[:, :], edecSarg[:, :], AF.Exp),
              reads=["edecSarg"], writes=["edT_all"], tag="edTb")
        edT_all = edecSarg

        S_GRP = [f"Sf_{g}" for g in range(NG)]
        SB_GRP = [f"Sb_{g}" for g in range(NG)]
        em.op("dve", lambda e: e.memset(S_f32[:, :], 0.0), writes=S_GRP, tag="S0")
        em.op("dve", lambda e: e.memset(S_b16[:, :], 0.0), writes=SB_GRP, tag="Sb0")

        def rep8(buf, c, g, pstep=NCH * NH):
            base = buf[:, :]
            return bass.AP(base.tensor, base.offset + c * NH + g * 8,
                           [[pstep, 128], [1, 8], [0, HP]])

        def prefix(c):
            em.dma(bcast[:, :],
                   strided(bass.AP(s_scr_all.tensor, s_scr_all.offset + c * NH * Q,
                                   [[0, 128], [1, NH * Q]]), [[0, 128], [1, NH * Q]]),
                   reads=["s_scr"], writes=["bcast"], tag=f"bc{c}")

        def prefix2(c):
            em.op("dve", lambda e, c=c: e.tensor_tensor(
                bcast[:, :].rearrange("p (h t) -> p h t", h=NH),
                bcast[:, :].rearrange("p (h t) -> p h t", h=NH),
                bass.AP(sT_all[:, :].tensor, sT_all[:, :].offset + c * NH,
                        [[NCH * NH, 128], [1, NH], [0, Q]]),
                ALU.subtract),
                reads=["bcast", "sT_all"], writes=["bcast"], tag=f"arg{c}")

        def prefix3(c):
            em.op("pool", lambda e: e.affine_select(
                bcast[:, :].rearrange("p (h t) -> p h t", h=NH),
                bcast[:, :].rearrange("p (h t) -> p h t", h=NH),
                [[0, NH], [1, Q]], ALU.is_ge, -1e9, channel_multiplier=-1),
                reads=["bcast"], writes=["bcast"], tag=f"mask{c}")

        def prefix4(c):
            em.op("act", lambda e: e.activation(lamT[:, :], bcast[:, :], AF.Exp),
                  reads=["bcast"], writes=["lamT"], tag=f"lam{c}")

        pend_yn = []

        def flush_yn():
            while pend_yn:
                g, gb, toff, c = pend_yn.pop()
                for j in range(4):
                    em.op("pe", lambda e, j=j, gb=gb: e.transpose(
                        p_tr[1][:, j * 128:(j + 1) * 128],
                        yn16[gb][:, j * 128:(j + 1) * 128], ident[:, :]),
                        reads=[f"yn16{gb}", "ident", "p_tr1"],
                        writes=["p_tr1"] if j in (0, 3) else [], tag=f"trY{g}.{j}.{c}")
                em.op("dve", lambda e, g=g, toff=toff: e.tensor_tensor(
                    bass.AP(ynTb[:, :].tensor, ynTb[:, :].offset + 4 * g * 256 + toff,
                            [[32 * 256, 128], [256, 4], [1, 128]]),
                    p_tr[1][:, 0:512].rearrange("p (a t) -> p a t", a=4),
                    bass.AP(gateC[0][:, :].tensor,
                            gateC[0][:, :].offset + 4 * g * 256 + toff,
                            [[32 * 256, 128], [256, 4], [1, 128]]),
                    ALU.mult),
                    reads=["p_tr1", "gateC0", f"ynTb{g}"], writes=[f"ynTb{g}", "p_tr1"],
                    tag=f"evY{g}.{c}")

        prefix(0)
        prefix2(0)
        prefix3(0)
        prefix4(0)
        for c in range(NCH):
            toff = (c % 2) * 128
            if c % 2 == 0:
                flush_yn()
                em.dma(xbcC[0][:, :].rearrange("p (a t) -> p a t", a=48),
                       xbc_scr[:, :, c * Q:(c + 2) * Q].rearrange("a p t -> p a t"),
                       writes=["xbcC0"], tag=f"ldxbc{c}")
                em.dma(gateC[0][:, :].rearrange("p (a t) -> p a t", a=32),
                       gate_scr[:, :, c * Q:(c + 2) * Q].rearrange("a p t -> p a t"),
                       writes=["gateC0"], tag=f"ldg{c}")
            xbtok, gttok = "xbcC0", "gateC0"

            for ct4 in range(8):
                ptb = ct4 % 2
                for j in range(4):
                    ct = ct4 * 4 + j
                    em.op("pe", lambda e, ptb=ptb, j=j, ct=ct, toff=toff:
                          e.transpose(p_tr[ptb][:, j * 128:(j + 1) * 128],
                                      xbcC[0][:, ct * 256 + toff:ct * 256 + toff + 128],
                                      ident[:, :]),
                          reads=[xbtok, "ident", f"p_tr{ptb}"],
                          writes=[f"p_tr{ptb}"] if j in (0, 3) else [],
                          tag=f"trX{ct}.{c}")
                em.op("act", lambda e, ptb=ptb, ct4=ct4: e.copy(
                    X_nat[:, ct4 * 512:(ct4 + 1) * 512], p_tr[ptb][:, 0:512]),
                    reads=[f"p_tr{ptb}"], writes=["X_nat", f"p_tr{ptb}"],
                    tag=f"evX{ct4}.{c}")

            for g in range(NG):
                gb = g % 2
                cbb = 0
                cbt = "p_cb0"
                bslice = lambda g=g, toff=toff: xbcC[0][
                    :, (32 + g) * 256 + toff:(32 + g) * 256 + toff + 128]
                cslice = lambda g=g, toff=toff: xbcC[0][
                    :, (40 + g) * 256 + toff:(40 + g) * 256 + toff + 128]
                em.op("pe", lambda e, bs=bslice, cs=cslice, cbb=cbb: e.matmul(
                    p_cb[cbb][:, 0:128], bs(), cs(), start=True, stop=True),
                    reads=[xbtok, cbt], writes=[cbt], tag=f"cb{g}.{c}")
                em.op("pe", lambda e, bs=bslice: e.transpose(
                    p_tr[0][:, 0:128], bs(), ident[:, :]),
                    reads=[xbtok, "ident", "p_tr0"], writes=["p_tr0"], tag=f"trB{g}.{c}")
                em.op("act", lambda e, gb=gb: e.copy(bn_g[gb][:, :], p_tr[0][:, 0:128]),
                      reads=["p_tr0"], writes=[f"bn_g{gb}", "p_tr0"], tag=f"evB{g}.{c}")
                em.op("pool", lambda e, g=g, gb=gb, c=c: e.tensor_tensor(
                    xdt_g[gb][:, :].rearrange("p (h f) -> p h f", h=8),
                    X_nat[:, g * 512:(g + 1) * 512].rearrange("p (h f) -> p h f", h=8),
                    rep8(dtpT_all, c, g), ALU.mult),
                    reads=["X_nat", "dtpT"], writes=[f"xdt_g{gb}"], tag=f"xdt{g}.{c}")
                em.op("pool", lambda e, g=g, gb=gb, c=c: e.tensor_tensor(
                    xd2_g[gb][:, :].rearrange("p (h f) -> p h f", h=8),
                    xdt_g[gb][:, :].rearrange("p (h f) -> p h f", h=8),
                    rep8(edT_all, c, g), ALU.mult),
                    reads=[f"xdt_g{gb}", "edT_all"], writes=[f"xd2_g{gb}"], tag=f"xd2{g}.{c}")
                em.op("dve", lambda e, g=g, gb=gb, cbb=cbb: e.tensor_tensor(
                    gam_g[gb][:, :].rearrange("p (h t) -> p h t", h=8),
                    lamT[:, g * 8 * Q:(g + 1) * 8 * Q].rearrange("p (h t) -> p h t", h=8),
                    strided(p_cb[cbb][:, 0:128], [[512, 128], [0, 8], [1, 128]]),
                    ALU.mult),
                    reads=["lamT", cbt], writes=[f"gam_g{gb}", cbt], tag=f"gam{g}.{c}")
                for hh in range(8):
                    em.op("pe", lambda e, hh=hh, gb=gb: e.matmul(
                        p_y1[gb][:, hh * 64:(hh + 1) * 64],
                        gam_g[gb][:, hh * Q:(hh + 1) * Q],
                        xdt_g[gb][:, hh * 64:(hh + 1) * 64],
                        start=True, stop=True),
                        reads=[f"gam_g{gb}", f"xdt_g{gb}", f"p_y1{gb}"],
                        writes=[f"p_y1{gb}"] if hh in (0, 7) else [], tag=f"y1.{g}.{hh}.{c}")
                em.op("pe", lambda e, cs=cslice, g=g, gb=gb: e.matmul(
                    p_y2[gb][:, :], cs(), S_b16[:, g * 512:(g + 1) * 512],
                    start=True, stop=True),
                    reads=[xbtok, SB_GRP[g], f"p_y2{gb}"], writes=[f"p_y2{gb}"], tag=f"y2.{g}.{c}")
                em.op("pe", lambda e, gb=gb: e.matmul(
                    p_ds[:, :], bn_g[gb][:, :], xd2_g[gb][:, :], start=True, stop=True),
                    reads=[f"bn_g{gb}", f"xd2_g{gb}", "p_ds"], writes=["p_ds"],
                    tag=f"ds{g}.{c}")
                em.op("dve", lambda e, g=g, c=c, gb=gb: e.tensor_tensor(
                    ytmp[:, :].rearrange("p (h f) -> p h f", h=8),
                    p_y2[gb][:, :].rearrange("p (h f) -> p h f", h=8),
                    rep8(expS_all, c, g), ALU.mult),
                    reads=[f"p_y2{gb}", "expS_all"], writes=["ytmp", f"p_y2{gb}"],
                    tag=f"ytmp{g}.{c}")
                em.op("dve", lambda e, gb=gb: e.tensor_add(
                    ytmp[:, :], p_y1[gb][:, :], ytmp[:, :]),
                    reads=[f"p_y1{gb}", "ytmp"], writes=["ytmp", f"p_y1{gb}"],
                    tag=f"ycmb{g}.{c}")
                em.op("pool", lambda e, g=g, gb=gb: e.tensor_tensor(
                    xdterm[gb][:, :].rearrange("p (h f) -> p h f", h=8),
                    X_nat[:, g * 512:(g + 1) * 512].rearrange("p (h f) -> p h f", h=8),
                    strided(dbcp[:, g * 8:g * 8 + 8], [[NH, 128], [1, 8], [0, HP]]),
                    ALU.mult),
                    reads=["X_nat", "dbcp"], writes=[f"xdterm{gb}"], tag=f"xD{g}.{c}")
                em.op("dve", lambda e, gb=gb: e.tensor_add(
                    yg[gb][:, :], ytmp[:, :], xdterm[gb][:, :]),
                    reads=["ytmp", f"xdterm{gb}"], writes=[f"yg{gb}"], tag=f"ygv{g}.{c}")
                em.op("act", lambda e, gb=gb: e.activation(
                    trashB[:, :], yg[gb][:, :], AF.Square, accum_out=gcol[:, 0:1]),
                    reads=[f"yg{gb}"], writes=["trashB", "gcol"],
                    tag=f"gn1.{g}.{c}", force=True)
                em.op("act", lambda e: e.activation(
                    gcol[:, 1:2], gcol[:, 0:1], AF.Ln, bias=cst[:, 0:1],
                    scale=cst[:, 1:2]),
                    reads=["gcol", "cst"], writes=["gcol1"], tag=f"gn2.{g}.{c}", force=True)
                em.op("act", lambda e: e.activation(
                    gcol[:, 2:3], gcol[:, 1:2], AF.Exp, scale=cst[:, 2:3]),
                    reads=["gcol1", "cst"], writes=["gcol2"], tag=f"gn3.{g}.{c}", force=True)
                em.op("act", lambda e, gb=gb: e.activation(
                    yn16[gb][:, :], yg[gb][:, :], AF.Copy, scale=gcol[:, 2:3]),
                    reads=[f"yg{gb}", "gcol2"], writes=[f"yn16{gb}"], tag=f"yn{g}.{c}")
                flush_yn()
                pend_yn.append((g, gb, toff, c))
                em.op("pool", lambda e, g=g, gb=gb, c=c: e.tensor_tensor(
                    stmp[gb][:, :].rearrange("p (h f) -> p h f", h=8),
                    S_f32[:, g * 512:(g + 1) * 512].rearrange("p (h f) -> p h f", h=8),
                    rep8(edecS_all, c, g), ALU.mult),
                    reads=[S_GRP[g], "edecS_all"], writes=[f"stmpB{gb}"], tag=f"sdec{g}.{c}")
                em.op("dve", lambda e, g=g, gb=gb: e.tensor_add(
                    S_f32[:, g * 512:(g + 1) * 512], stmp[gb][:, :], p_ds[:, :]),
                    reads=[f"stmpB{gb}", "p_ds"], writes=[S_GRP[g], "p_ds"],
                    tag=f"supd{g}.{c}")
                em.op("act", lambda e, g=g: e.copy(
                    S_b16[:, g * 512:(g + 1) * 512], S_f32[:, g * 512:(g + 1) * 512]),
                    reads=[S_GRP[g]], writes=[SB_GRP[g]], tag=f"sb16{g}.{c}")
                if c + 1 < NCH:
                    if g == 0:
                        prefix(c + 1)
                    elif g == 2:
                        prefix2(c + 1)
                    elif g == 3:
                        prefix3(c + 1)
                    elif g == 7:
                        prefix4(c + 1)
            if c % 2 == 1:
                flush_yn()
                em.dma(ynT_scr[:, :, (c - 1) * Q:(c + 1) * Q].rearrange("a p t -> p a t"),
                       ynTb[:, :].rearrange("p (a t) -> p a t", a=32), q="qact",
                       reads=[f"ynTb{g2}" for g2 in range(NG)],
                       writes=[f"ynTb{g2}" for g2 in range(NG)],
                       tag=f"spyn{c}")
        close_scope()

    if STAGES[2]:
        ynr = [sbs(f"ynr{i}", [128, 32 * 512], BF16) for i in range(2)]
        wcs = [sbs("wcs0", [128, 32 * 256])]
        wcb = [sbs(f"wcb{i}", [128, 32 * 512], BF16) for i in range(2)]
        nwT = sbs("nwT", [128, 32])
        attb = [sbs(f"attb{i}", [128, 512]) for i in range(2)]
        p_c = [pss(f"p_c{i}", [128, 512]) for i in range(2)]

        em.dma(nwT[:, :], nw_in, writes=["nwT"], tag="ldnw")

        def rec_wo(dpj):
            wbj = dpj % 2
            for hq in range(2):
                em.dma(wcs[0][:, :].rearrange("p (a c) -> p a c", a=32),
                       woutT[:, dpj * 512 + hq * 256:dpj * 512 + (hq + 1) * 256]
                       .rearrange("(a p) c -> p a c", p=128),
                       reads=["wcs0"], writes=["wcs0w"], tag=f"ldWo{dpj}.{hq}")
                for ih2 in range(32):
                    if ih2 % 2 == 0:
                        em.op("act", lambda e, wbj=wbj, hq=hq, ih2=ih2: e.activation(
                            wcb[wbj][:, ih2 * 512 + hq * 256:ih2 * 512 + (hq + 1) * 256],
                            wcs[0][:, ih2 * 256:(ih2 + 1) * 256],
                            AF.Copy, scale=nwT[:, ih2:ih2 + 1]),
                            reads=["wcs0w", "nwT"],
                            writes=[f"wcb{wbj}"] if ih2 in (0, 30) else [],
                            tag=f"cvWo{dpj}.{hq}.{ih2}")
                    else:
                        em.op("dve", lambda e, wbj=wbj, hq=hq, ih2=ih2: e.tensor_scalar(
                            wcb[wbj][:, ih2 * 512 + hq * 256:ih2 * 512 + (hq + 1) * 256],
                            wcs[0][:, ih2 * 256:(ih2 + 1) * 256],
                            nwT[:, ih2:ih2 + 1], None, ALU.mult),
                            reads=["wcs0w", "nwT"],
                            writes=[f"wcb{wbj}"] if ih2 == 31 else [],
                            tag=f"cvWo{dpj}.{hq}.{ih2}")

        for dp in range(8):
            wb = dp % 2
            if dp == 0:
                rec_wo(0)
            for tc in range(NCH):
                if tc == 5 and dp + 1 < 8:
                    rec_wo(dp + 1)
                yb = (tc // 4) % 2
                if tc % 4 == 0:
                    em.dma(ynr[yb][:, :].rearrange("p (a t) -> p a t", a=32),
                           ynT_scr[:, :, tc * Q:(tc + 4) * Q].rearrange("a p t -> p a t"),
                           reads=[f"ynr{yb}"], writes=[f"ynr{yb}w"], tag=f"ldyn{dp}.{tc}")
                pc = tc % 2
                for i in range(32):
                    em.op("pe", lambda e, yb=yb, wb=wb, i=i, tc=tc, pc=pc: e.matmul(
                        p_c[pc][:, :],
                        ynr[yb][:, i * 512 + (tc % 4) * 128:i * 512 + (tc % 4) * 128 + 128],
                        wcb[wb][:, i * 512:(i + 1) * 512],
                        start=(i == 0), stop=(i == 31)),
                        reads=[f"ynr{yb}w", f"wcb{wb}", f"p_c{pc}"],
                        writes=[f"p_c{pc}"] if i in (0, 31) else [],
                        tag=f"mmC{dp}.{tc}.{i}")
                em.op("dve", lambda e, pc=pc: e.tensor_copy(attb[pc][:, :], p_c[pc][:, :]),
                      reads=[f"p_c{pc}"], writes=[f"attb{pc}", f"p_c{pc}"],
                      tag=f"evC{dp}.{tc}")
                em.dma(attn_scr[tc][:, dp * 512:(dp + 1) * 512], attb[pc][:, :],
                       q="qact", reads=[f"attb{pc}"], tag=f"spC{dp}.{tc}")
                if tc == NCH - 1:
                    pass
        close_scope()

        attc = sbs("attc", [128, D])
        xc = sbs("xc", [128, D])
        z0 = sbs("z0", [128, D])
        trash4 = sbs("trash4", [128, D])
        zb = sbs("zb", [128, D])
        zb16 = sbs("zb16", [128, D], BF16)
        zTp = sbs("zTp", [128, 32 * 256], BF16)
        lncol = sbs("lncol", [128, 8])
        p_tc = [pss(f"p_tc{i}", [128, 1024], BF16) for i in range(2)]

        for tc in range(NCH):
            em.dma(attc[:, :], attn_scr[tc], reads=["attc"], writes=["attcw"],
                   tag=f"ldat{tc}")
            em.dma(xc[:, :], x_in[tc * Q:(tc + 1) * Q, :], reads=["xc"],
                   writes=["xcw"], tag=f"ldx{tc}")
            em.op("dve", lambda e: e.scalar_tensor_tensor(
                z0[:, :], attc[:, :], 1.0, xc[:, :], ALU.mult, ALU.add,
                accum_out=lncol[:, 0:1]),
                reads=["attcw", "xcw"], writes=["z0", "lncol0", "attc", "xc"],
                tag=f"z0.{tc}", force=True)
            em.op("act", lambda e: e.activation(
                trash4[:, :], z0[:, :], AF.Square, accum_out=lncol[:, 1:2]),
                reads=["z0"], writes=["trash4", "lncol1"], tag=f"sq.{tc}", force=True)
            em.op("dve", lambda e: e.tensor_scalar_mul(lncol[:, 2:3], lncol[:, 0:1], 1.0 / D),
                  reads=["lncol0"], writes=["lncol2"], tag=f"mean.{tc}", force=True)
            em.op("dve", lambda e: e.tensor_tensor(lncol[:, 3:4], lncol[:, 2:3],
                                                   lncol[:, 2:3], ALU.mult),
                  reads=["lncol2"], writes=["lncol3"], tag=f"m2.{tc}", force=True)
            em.op("dve", lambda e: e.tensor_scalar(lncol[:, 4:5], lncol[:, 1:2],
                                                   1.0 / D, lncol[:, 3:4], ALU.mult,
                                                   ALU.subtract),
                  reads=["lncol1", "lncol3"], writes=["lncol4"], tag=f"var.{tc}", force=True)
            em.op("act", lambda e: e.activation(lncol[:, 5:6], lncol[:, 4:5], AF.Ln,
                                                bias=cst[:, 0:1]),
                  reads=["lncol4", "cst"], writes=["lncol5"], tag=f"lnv.{tc}", force=True)
            em.op("act", lambda e: e.activation(lncol[:, 6:7], lncol[:, 5:6], AF.Exp,
                                                scale=cst[:, 2:3]),
                  reads=["lncol5", "cst"], writes=["lncol6"], tag=f"rstd.{tc}", force=True)
            em.op("dve", lambda e: e.tensor_scalar(
                zb[:, :], z0[:, :], lncol[:, 2:3], lncol[:, 6:7],
                ALU.subtract, ALU.mult),
                reads=["z0", "lncol2", "lncol6"], writes=["zb"], tag=f"zn.{tc}")
            em.dma(z_scr[tc], zb[:, :], q="qact", reads=["zb"], tag=f"spz{tc}")
            em.op("act", lambda e: e.copy(zb16[:, :], zb[:, :]),
                  reads=["zb"], writes=["zb16"], tag=f"zb16.{tc}")
            for j4 in range(8):
                pb = j4 % 2
                for j in range(4):
                    jj = j4 * 4 + j
                    em.op("pe", lambda e, pb=pb, j=j, jj=jj: e.transpose(
                        p_tc[pb][:, j * 128:(j + 1) * 128],
                        zb16[:, jj * 128:(jj + 1) * 128], ident[:, :]),
                        reads=["zb16", "ident", f"p_tc{pb}"],
                        writes=[f"p_tc{pb}"] if j == 3 else [], tag=f"trz{jj}.{tc}")
                em.op("dve", lambda e, pb=pb, j4=j4, tc=tc: e.tensor_copy(
                    bass.AP(zTp[:, :].tensor,
                            zTp[:, :].offset + j4 * 4 * 256 + (tc % 2) * 128,
                            [[32 * 256, 128], [256, 4], [1, 128]]),
                    p_tc[pb][:, 0:512].rearrange("p (a t) -> p a t", a=4)),
                    reads=[f"p_tc{pb}", "zTp"], writes=["zTp", f"p_tc{pb}"],
                    tag=f"evz{j4}.{tc}")
            if tc % 2 == 1:
                em.dma(zT_scr[:, :, (tc - 1) * Q:(tc + 1) * Q].rearrange("a p t -> p a t"),
                       zTp[:, :].rearrange("p (a t) -> p a t", a=32), q="qact",
                       reads=["zTp"], writes=["zTp"], tag=f"spzT{tc}")
        close_scope()

    if STAGES[3]:
        zTb = sbs("zTb", [128, 32 * T], BF16)
        gw_sb = sbs("gw_sb", [128, 32 * NE])
        gw16 = sbs("gw16", [128, 32 * NE], BF16)
        fstg = [sbs("fstg0", [128, 32 * 128])]
        fb16 = [sbs(f"fb16{i}", [128, 32 * 128], BF16) for i in range(2)]
        lgb = sbs("lgb", [128, NE])
        lgm = sbs("lgm", [128, NE])
        rc = sbs("rc", [128, 4])
        xlb = sbs("xlb", [128, T], BF16)
        rbuf = sbs("rbuf", [128, 512])
        h1b = sbs("h1b", [128, T], BF16)
        wdstg = sbs("wdstg", [128, T])
        wdb16 = sbs("wdb16", [128, T], BF16)
        p_d = [pss(f"p_d{i}", [128, 512]) for i in range(2)]
        p_lg = pss("p_lg", [128, 512])

        em.dma(zTb[:, :].rearrange("p (a t) -> p a t", a=32),
               zT_scr.rearrange("a p t -> p a t"), writes=["zTb"], tag="ldzT")
        em.dma(gw_sb[:, :].rearrange("p (a k) -> p a k", a=32),
               w_gate.rearrange("(a p) k -> p a k", p=128), writes=["gwf"], tag="ldgw")
        em.op("act", lambda e: e.copy(gw16[:, :], gw_sb[:, :]),
              reads=["gwf"], writes=["gw"], tag="cvgw")
        for tc in range(NCH):
            for d in range(32):
                em.op("pe", lambda e, d=d, tc=tc: e.matmul(
                    p_lg[:, 0:NE], zTb[:, d * T + tc * Q:d * T + tc * Q + Q]
                    if False else zTb[:, d * T + tc * 128:d * T + tc * 128 + 128],
                    gw16[:, d * NE:(d + 1) * NE],
                    start=(d == 0), stop=(d == 31)),
                    reads=["zTb", "gw", "p_lg"],
                    writes=["p_lg"] if d in (0, 31) else [], tag=f"mmL{tc}.{d}")
            em.op("dve", lambda e: e.tensor_copy(lgb[:, :], p_lg[:, 0:NE]),
                  reads=["p_lg"], writes=["lgb", "p_lg"], tag=f"evL{tc}", force=True)
            em.op("dve", lambda e: e.tensor_reduce(rc[:, 0:1], lgb[:, :], AX.X, ALU.max),
                  reads=["lgb"], writes=["rc0"], tag=f"m1.{tc}", force=True)
            em.op("dve", lambda e: e.tensor_scalar(lgm[:, :], lgb[:, :], rc[:, 0:1],
                                                   None, ALU.is_lt),
                  reads=["lgb", "rc0"], writes=["lgm"], tag=f"msk.{tc}", force=True)
            em.op("dve", lambda e: e.scalar_tensor_tensor(
                lgm[:, :], lgm[:, :], 1.0e4, lgb[:, :], ALU.mult, ALU.add),
                reads=["lgm", "lgb"], writes=["lgm"], tag=f"mu.{tc}", force=True)
            em.op("dve", lambda e: e.tensor_reduce(rc[:, 1:2], lgm[:, :], AX.X, ALU.max),
                  reads=["lgm"], writes=["rc1"], tag=f"m2.{tc}", force=True)
            em.op("dve", lambda e: e.tensor_scalar_add(rc[:, 2:3], rc[:, 1:2], -1.0e4),
                  reads=["rc1"], writes=["rc2"], tag=f"m2b.{tc}", force=True)
            em.op("dve", lambda e, tc=tc: e.tensor_add(
                route_cols[:, tc:tc + 1], rc[:, 0:1], rc[:, 2:3]),
                reads=["rc0", "rc2"], writes=["route"], tag=f"rt.{tc}", force=True)
        em.dma(strided(r_scr, [[1, 128], [128, NCH]]), route_cols[:, :],
               reads=["route"], writes=["r_scr"], tag="spr",
               allow_slow_non_contiguous=True)
        em.dma(rbc[:, :], strided(r_scr, [[0, 128], [1, T]]),
               reads=["r_scr"], writes=["rbc"], tag="ldrbc")
        def rec_f(ltj, wsrc, tagc):
            bj = ltj % 2
            em.dma(fstg[0][:, :].rearrange("p (a c) -> p a c", a=32),
                   wsrc[:, ltj * 128:(ltj + 1) * 128].rearrange("(a p) c -> p a c", p=128),
                   writes=["fstg0"], tag=f"ld{tagc}{ltj}")
            em.op("act", lambda e, bj=bj: e.copy(fb16[bj][:, :], fstg[0][:, :]),
                  reads=["fstg0"], writes=[f"fb16{bj}"], tag=f"cv{tagc}{ltj}")

        for lt in range(16):
            bb = lt % 2
            if lt == 0:
                rec_f(0, w_fc1, "F")
            if lt + 1 < 16:
                rec_f(lt + 1, w_fc1, "F")
            for t4 in range(4):
                pp = t4 % 2
                for d in range(32):
                    em.op("pe", lambda e, bb=bb, t4=t4, d=d, pp=pp: e.matmul(
                        p_d[pp][:, :], fb16[bb][:, d * 128:(d + 1) * 128],
                        zTb[:, d * T + t4 * 512:d * T + (t4 + 1) * 512],
                        start=(d == 0), stop=(d == 31)),
                        reads=[f"fb16{bb}", "zTb", f"p_d{pp}"],
                        writes=[f"p_d{pp}"] if d in (0, 31) else [], tag=f"mmF{lt}.{t4}.{d}")
                em.op("act", lambda e, pp=pp, t4=t4: e.copy(
                    xlb[:, t4 * 512:(t4 + 1) * 512], p_d[pp][:, :]),
                    reads=[f"p_d{pp}"], writes=["xlb", f"p_d{pp}"], tag=f"evF{lt}.{t4}")
            em.dma(xlat_scr[lt], xlb[:, :], q="qact", reads=["xlb"], tag=f"spF{lt}")
        def rec_wd(k):
            ih, half = k // 2, k % 2
            em.dma(wdstg[:, :], w_down[ih * 128:(ih + 1) * 128,
                                       half * 2048:(half + 1) * 2048],
                   reads=["wdstg"], writes=["wdstgw"], tag=f"ldWd{k}")
            em.op("act", lambda e: e.copy(wdb16[:, :], wdstg[:, :]),
                  reads=["wdstgw"], writes=["wdb16", "wdstg"], tag=f"cvWd{k}")
            em.dma(wdb_scr[ih][:, half * 2048:(half + 1) * 2048], wdb16[:, :],
                   q="qact", reads=["wdb16"], tag=f"spWd{k}")

        for it in range(64):
            bb = it % 2
            if it == 0:
                rec_f(0, w_up, "U")
            if it + 1 < 64:
                rec_f(it + 1, w_up, "U")
            for k in range(it * 2, it * 2 + 2):
                rec_wd(k)
            for t4 in range(4):
                pp = t4 % 2
                for d in range(32):
                    em.op("pe", lambda e, bb=bb, t4=t4, d=d, pp=pp: e.matmul(
                        p_d[pp][:, :], fb16[bb][:, d * 128:(d + 1) * 128],
                        zTb[:, d * T + t4 * 512:d * T + (t4 + 1) * 512],
                        start=(d == 0), stop=(d == 31)),
                        reads=[f"fb16{bb}", "zTb", f"p_d{pp}"],
                        writes=[f"p_d{pp}"] if d in (0, 31) else [], tag=f"mmU{it}.{t4}.{d}")
                em.op("act", lambda e, pp=pp: e.activation(
                    rbuf[:, :], p_d[pp][:, :], AF.Relu),
                    reads=[f"p_d{pp}"], writes=["rbuf", f"p_d{pp}"], tag=f"rl{it}.{t4}")
                em.op("dve", lambda e, t4=t4: e.tensor_tensor(
                    h1b[:, t4 * 512:(t4 + 1) * 512], rbuf[:, :], rbuf[:, :], ALU.mult),
                    reads=["rbuf"], writes=["h1b"], tag=f"sq{it}.{t4}")
            em.dma(h1_scr[it], h1b[:, :], q="qact", reads=["h1b"], tag=f"spU{it}")
        close_scope()

        esT = sbs("esT", [128, 16 * T], BF16)
        xlT_h = nc.sbuf_tensor("xlT", [128, 16 * T], BF16)
        xlT = xlT_h.__enter__()
        estg_h = nc.sbuf_tensor("estg0", [128, 16 * 128], F32)
        estg = [estg_h.__enter__()]
        eb16_h = [nc.sbuf_tensor(f"eb16{i}", [128, 16 * 128], BF16) for i in range(2)]
        eb16 = [h.__enter__() for h in eb16_h]
        p_d2 = [pss(f"p_d2{i}", [128, 512]) for i in range(2)]

        em.dma(xlT[:, :].rearrange("p (a t) -> p a t", a=16),
               xlat_scr.rearrange("a p t -> p a t"), writes=["xlT"], tag="ldxl")
        def rec_e(mtj):
            bj = mtj % 2
            em.dma(estg[0][:, :].rearrange("p (a c) -> p a c", a=16),
                   w_exp[:, mtj * 128:(mtj + 1) * 128].rearrange("(a p) c -> p a c", p=128),
                   writes=["estg0"], tag=f"ldE{mtj}")
            em.op("act", lambda e, bj=bj: e.copy(eb16[bj][:, :], estg[0][:, :]),
                  reads=["estg0"], writes=[f"eb16{bj}"], tag=f"cvE{mtj}")

        for mt in range(16):
            bb = mt % 2
            if mt == 0:
                rec_e(0)
            if mt + 1 < 16:
                rec_e(mt + 1)
            for t4 in range(4):
                pp = t4 % 2
                for l in range(16):
                    em.op("pe", lambda e, bb=bb, t4=t4, l=l, pp=pp: e.matmul(
                        p_d2[pp][:, :], eb16[bb][:, l * 128:(l + 1) * 128],
                        xlT[:, l * T + t4 * 512:l * T + (t4 + 1) * 512],
                        start=(l == 0), stop=(l == 15)),
                        reads=[f"eb16{bb}", "xlT", f"p_d2{pp}"],
                        writes=[f"p_d2{pp}"] if l in (0, 15) else [], tag=f"mmE{mt}.{t4}.{l}")
                em.op("dve", lambda e, pp=pp, mt=mt, t4=t4: e.tensor_tensor(
                    esT[:, mt * T + t4 * 512:mt * T + (t4 + 1) * 512],
                    p_d2[pp][:, :], rbc[:, t4 * 512:(t4 + 1) * 512], ALU.mult),
                    reads=[f"p_d2{pp}", "rbc"], writes=["esT", f"p_d2{pp}"],
                    tag=f"evE{mt}.{t4}")
        em.fence()
        for h in [eb16_h[1], eb16_h[0], estg_h, xlT_h]:
            h.__exit__(None, None, None)
        f2stg = [sbs("f2stg0", [128, 16 * 256])]
        f2b16 = [sbs(f"f2b16{i}", [128, 16 * 512], BF16) for i in range(2)]
        rb = [sbs(f"rb{i}", [128, 512]) for i in range(2)]

        def rec_f2(dpj):
            fbj = dpj % 2
            for hq in range(2):
                em.dma(f2stg[0][:, :].rearrange("p (a c) -> p a c", a=16),
                       w_fc2[:, dpj * 512 + hq * 256:dpj * 512 + (hq + 1) * 256]
                       .rearrange("(a p) c -> p a c", p=128),
                       writes=["f2stg0"], tag=f"ldF2{dpj}.{hq}")
                em.op("act", lambda e, hq=hq, fbj=fbj: e.copy(
                    bass.AP(f2b16[fbj][:, :].tensor, f2b16[fbj][:, :].offset + hq * 256,
                            [[16 * 512, 128], [512, 16], [1, 256]]),
                    f2stg[0][:, :].rearrange("p (a c) -> p a c", a=16)),
                    reads=["f2stg0"], writes=[f"f2b16{fbj}"],
                    tag=f"cvF2{dpj}.{hq}")

        for dp2 in range(8):
            fb = dp2 % 2
            if dp2 == 0:
                rec_f2(0)
            if dp2 + 1 < 8:
                rec_f2(dp2 + 1)
            for tc in range(NCH):
                pp = tc % 2
                for m in range(16):
                    em.op("pe", lambda e, m=m, tc=tc, pp=pp, fb=fb: e.matmul(
                        p_d2[pp][:, :], esT[:, m * T + tc * 128:m * T + tc * 128 + 128],
                        f2b16[fb][:, m * 512:(m + 1) * 512],
                        start=(m == 0), stop=(m == 15)),
                        reads=["esT", f"f2b16{fb}", f"p_d2{pp}"],
                        writes=[f"p_d2{pp}"] if m in (0, 15) else [], tag=f"mmR{dp2}.{tc}.{m}")
                em.op("dve", lambda e, pp=pp: e.tensor_copy(rb[pp][:, :], p_d2[pp][:, :]),
                      reads=[f"p_d2{pp}"], writes=[f"rb{pp}", f"p_d2{pp}"],
                      tag=f"evR{dp2}.{tc}")
                em.dma(routed_scr[tc][:, dp2 * 512:(dp2 + 1) * 512], rb[pp][:, :],
                       q="qact", reads=[f"rb{pp}"], tag=f"spR{dp2}.{tc}")
        close_scope()

        wpan = sbs("wpan", [128, 64 * 512], BF16)
        h1r = [sbs(f"h1r{i}", [128, 64 * 256], BF16) for i in range(2)]
        rtb = [sbs(f"rtb{i}", [128, 512]) for i in range(2)]
        acb = [sbs(f"acb{i}", [128, 512]) for i in range(2)]
        p_d3 = [pss(f"p_d3{i}", [128, 512]) for i in range(2)]

        for dp in range(8):
            em.dma(wpan[:, :].rearrange("p (a c) -> p a c", a=64),
                   wdb_scr[:, :, dp * 512:(dp + 1) * 512].rearrange("a p c -> p a c"),
                   reads=["wpan"], writes=["wpanw"], tag=f"ldD{dp}")
            for tc in range(NCH):
                hb = (tc // 2) % 2
                if tc % 2 == 0:
                    em.dma(h1r[hb][:, :].rearrange("p (a t) -> p a t", a=64),
                           h1_scr[:, :, tc * Q:(tc + 2) * Q].rearrange("a p t -> p a t"),
                           reads=[f"h1r{hb}"], writes=[f"h1r{hb}w"], tag=f"ldh{dp}.{tc}")
                pp = tc % 2
                for i in range(64):
                    em.op("pe", lambda e, hb=hb, i=i, tc=tc, pp=pp: e.matmul(
                        p_d3[pp][:, :],
                        h1r[hb][:, i * 256 + (tc % 2) * 128:i * 256 + (tc % 2) * 128 + 128],
                        wpan[:, i * 512:(i + 1) * 512],
                        start=(i == 0), stop=(i == 63)),
                        reads=[f"h1r{hb}w", "wpanw", f"p_d3{pp}"],
                        writes=[f"p_d3{pp}"] if i in (0, 63) else [], tag=f"mmD{dp}.{tc}.{i}")
                em.dma(rtb[pp][:, :], routed_scr[tc][:, dp * 512:(dp + 1) * 512],
                       reads=[f"rtb{pp}"], writes=[f"rtb{pp}w"], tag=f"ldrt{dp}.{tc}")
                em.op("dve", lambda e, pp=pp: e.tensor_add(
                    acb[pp][:, :], p_d3[pp][:, :], rtb[pp][:, :]),
                    reads=[f"p_d3{pp}", f"rtb{pp}w"],
                    writes=[f"acb{pp}", f"p_d3{pp}", f"rtb{pp}"], tag=f"evD{dp}.{tc}")
                em.dma(acc_scr[tc][:, dp * 512:(dp + 1) * 512], acb[pp][:, :],
                       q="qact", reads=[f"acb{pp}"], tag=f"spD{dp}.{tc}")
        close_scope()

        gbc = sbs("gbc", [128, D])
        bbc = sbs("bbc", [128, D])
        ac4_ = [sbs(f"ac4_{i}", [128, D]) for i in range(2)]
        zc4_ = [sbs(f"zc4_{i}", [128, D]) for i in range(2)]
        f1_ = [sbs("f1_s", [128, D])] * 2
        f2_ = [sbs("f2_s", [128, D])] * 2
        trash5 = sbs("trash5", [128, D])
        obuf_ = [sbs(f"obuf_{i}", [128, D]) for i in range(2)]
        lc_ = [sbs(f"lc_{i}", [128, 10]) for i in range(2)]

        em.dma(gbc[:, :], strided(gam_in, [[0, 128], [1, D]]), writes=["gbc"], tag="ldg")
        em.dma(bbc[:, :], strided(bet_in, [[0, 128], [1, D]]), writes=["bbc"], tag="ldb")
        for tc in range(NCH):
            pp4 = tc % 2
            ac4, zc4, f1, f2, obuf, lc = (ac4_[pp4], zc4_[pp4], f1_[pp4], f2_[pp4], obuf_[pp4], lc_[pp4])
            em.dma(ac4[:, :], acc_scr[tc], reads=["ac4" + str(pp4)], writes=["ac4w" + str(pp4)], tag=f"lda4{tc}")
            em.dma(zc4[:, :], z_scr[tc], reads=["zc4" + str(pp4)], writes=["zc4w" + str(pp4)], tag=f"ldz4{tc}")
            em.op("dve", lambda e, ac4=ac4, zc4=zc4, f1=f1, f2=f2, obuf=obuf, lc=lc: e.tensor_reduce(lc[:, 0:1], ac4[:, :], AX.X, ALU.add),
                  reads=["ac4w" + str(pp4)], writes=["lc0" + str(pp4)], tag=f"s4.{tc}", force=True)
            em.op("act", lambda e, ac4=ac4, zc4=zc4, f1=f1, f2=f2, obuf=obuf, lc=lc: e.activation(trash5[:, :], ac4[:, :], AF.Square,
                                                accum_out=lc[:, 1:2]),
                  reads=["ac4w" + str(pp4)], writes=["trash5", "lc1" + str(pp4)], tag=f"q4.{tc}", force=True)
            em.op("dve", lambda e, ac4=ac4, zc4=zc4, f1=f1, f2=f2, obuf=obuf, lc=lc: e.tensor_scalar_mul(lc[:, 2:3], lc[:, 0:1], 1.0 / D),
                  reads=["lc0" + str(pp4)], writes=["lc2" + str(pp4)], tag=f"mn4.{tc}", force=True)
            em.op("dve", lambda e, ac4=ac4, zc4=zc4, f1=f1, f2=f2, obuf=obuf, lc=lc: e.tensor_tensor(lc[:, 3:4], lc[:, 2:3], lc[:, 2:3],
                                                   ALU.mult),
                  reads=["lc2" + str(pp4)], writes=["lc3" + str(pp4)], tag=f"m24.{tc}", force=True)
            em.op("dve", lambda e, ac4=ac4, zc4=zc4, f1=f1, f2=f2, obuf=obuf, lc=lc: e.tensor_scalar(lc[:, 4:5], lc[:, 1:2], 1.0 / D,
                                                   lc[:, 3:4], ALU.mult, ALU.subtract),
                  reads=["lc1" + str(pp4), "lc3" + str(pp4)], writes=["lc4" + str(pp4)], tag=f"v4.{tc}", force=True)
            em.op("act", lambda e, ac4=ac4, zc4=zc4, f1=f1, f2=f2, obuf=obuf, lc=lc: e.activation(lc[:, 5:6], lc[:, 4:5], AF.Ln,
                                                bias=cst[:, 0:1]),
                  reads=["lc4" + str(pp4), "cst"], writes=["lc5" + str(pp4)], tag=f"l4.{tc}", force=True)
            em.op("act", lambda e, ac4=ac4, zc4=zc4, f1=f1, f2=f2, obuf=obuf, lc=lc: e.activation(lc[:, 6:7], lc[:, 5:6], AF.Exp,
                                                scale=cst[:, 2:3]),
                  reads=["lc5" + str(pp4), "cst"], writes=["lc6" + str(pp4)], tag=f"r4.{tc}", force=True)
            em.op("dve", lambda e, ac4=ac4, zc4=zc4, f1=f1, f2=f2, obuf=obuf, lc=lc: e.tensor_scalar(
                f1[:, :], ac4[:, :], lc[:, 2:3], lc[:, 6:7], ALU.subtract, ALU.mult),
                reads=["ac4w" + str(pp4), "lc2" + str(pp4), "lc6" + str(pp4)], writes=["f1", "ac4" + str(pp4)], tag=f"n4.{tc}")
            em.op("pool", lambda e, f1=f1, f2=f2: e.tensor_mul(f2[:, :], f1[:, :], gbc[:, :]),
                  reads=["f1", "gbc"], writes=["f2"], tag=f"g4.{tc}")
            em.op("pool", lambda e, zc4=zc4, f1=f1: e.tensor_add(f1[:, :], zc4[:, :], bbc[:, :]),
                  reads=["zc4w" + str(pp4), "bbc"], writes=["f1", "zc4" + str(pp4)], tag=f"zb4.{tc}")
            em.op("dve", lambda e, ac4=ac4, zc4=zc4, f1=f1, f2=f2, obuf=obuf, lc=lc: e.scalar_tensor_tensor(
                f2[:, :], f2[:, :], 1.0, f1[:, :], ALU.mult, ALU.add,
                accum_out=lc[:, 7:8]),
                reads=["f1", "f2"], writes=["f2", "lc7" + str(pp4)], tag=f"z2.{tc}", force=True)
            em.op("act", lambda e, ac4=ac4, zc4=zc4, f1=f1, f2=f2, obuf=obuf, lc=lc: e.activation(trash5[:, :], f2[:, :], AF.Square,
                                                accum_out=lc[:, 8:9]),
                  reads=["f2"], writes=["trash5", "lc8" + str(pp4)], tag=f"q5.{tc}", force=True)
            em.op("dve", lambda e, ac4=ac4, zc4=zc4, f1=f1, f2=f2, obuf=obuf, lc=lc: e.tensor_scalar_mul(lc[:, 2:3], lc[:, 7:8], 1.0 / D),
                  reads=["lc7" + str(pp4)], writes=["lc2" + str(pp4)], tag=f"mn5.{tc}", force=True)
            em.op("dve", lambda e, ac4=ac4, zc4=zc4, f1=f1, f2=f2, obuf=obuf, lc=lc: e.tensor_tensor(lc[:, 3:4], lc[:, 2:3], lc[:, 2:3],
                                                   ALU.mult),
                  reads=["lc2" + str(pp4)], writes=["lc3" + str(pp4)], tag=f"m25.{tc}", force=True)
            em.op("dve", lambda e, ac4=ac4, zc4=zc4, f1=f1, f2=f2, obuf=obuf, lc=lc: e.tensor_scalar(lc[:, 4:5], lc[:, 8:9], 1.0 / D,
                                                   lc[:, 3:4], ALU.mult, ALU.subtract),
                  reads=["lc8" + str(pp4), "lc3" + str(pp4)], writes=["lc4" + str(pp4)], tag=f"v5.{tc}", force=True)
            em.op("act", lambda e, ac4=ac4, zc4=zc4, f1=f1, f2=f2, obuf=obuf, lc=lc: e.activation(lc[:, 5:6], lc[:, 4:5], AF.Ln,
                                                bias=cst[:, 0:1]),
                  reads=["lc4" + str(pp4), "cst"], writes=["lc5" + str(pp4)], tag=f"l5.{tc}", force=True)
            em.op("act", lambda e, ac4=ac4, zc4=zc4, f1=f1, f2=f2, obuf=obuf, lc=lc: e.activation(lc[:, 6:7], lc[:, 5:6], AF.Exp,
                                                scale=cst[:, 2:3]),
                  reads=["lc5" + str(pp4), "cst"], writes=["lc6" + str(pp4)], tag=f"r5.{tc}", force=True)
            em.op("dve", lambda e, ac4=ac4, zc4=zc4, f1=f1, f2=f2, obuf=obuf, lc=lc: e.tensor_scalar(
                obuf[:, :], f2[:, :], lc[:, 2:3], lc[:, 6:7], ALU.subtract, ALU.mult),
                reads=["f2", "lc2" + str(pp4), "lc6" + str(pp4)], writes=["obuf" + str(pp4)], tag=f"o.{tc}")
            em.dma(out[tc * Q:(tc + 1) * Q, :], obuf[:, :], q="qact",
                   reads=["obuf" + str(pp4)], writes=["obuf" + str(pp4)], tag=f"out{tc}")
        close_scope()

    em.fence()
    em.op("sp", lambda e: e.nop(), tag="final")
    em.emit(nc)
    for h in reversed(persist):
        h.__exit__(None, None, None)
    return nc


_NC_CACHE = None


def _get_nc():
    global _NC_CACHE
    if _NC_CACHE is None:
        nc = bass.Bass("TRN2", target_bir_lowering=False)
        build(nc)
        _NC_CACHE = nc
    return _NC_CACHE


def make_in_maps(inputs):
    x = np.asarray(inputs["L0_attn_hidden_states"], dtype=np.float32)
    B = x.shape[0]
    shared = {
        "winT": np.ascontiguousarray(np.asarray(inputs["L0_attn_in_proj_w"]).T),
        "woutT": np.ascontiguousarray(np.asarray(inputs["L0_attn_out_proj_w"]).T),
        "w_up": np.ascontiguousarray(np.asarray(inputs["L0_ffn_up_w"])),
        "w_down": np.ascontiguousarray(np.asarray(inputs["L0_ffn_down_w"])),
        "w_fc1": np.ascontiguousarray(np.asarray(inputs["L0_ffn_fc1_w"])),
        "w_fc2": np.ascontiguousarray(np.asarray(inputs["L0_ffn_fc2_w"])),
        "w_exp": np.ascontiguousarray(np.asarray(inputs["L0_ffn_expert_w"])),
        "w_gate": np.ascontiguousarray(np.asarray(inputs["L0_ffn_gate_w"])),
        "convw": np.ascontiguousarray(
            np.asarray(inputs["L0_attn_conv_weight"]).reshape(48, 128, 4).transpose(1, 0, 2)),
        "convb": np.ascontiguousarray(
            np.asarray(inputs["L0_attn_conv_bias"]).reshape(48, 128).T),
        "dtbias": np.ascontiguousarray(
            np.asarray(inputs["L0_attn_dt_bias"]).reshape(NH, 1)),
        "alog": np.ascontiguousarray(np.asarray(inputs["L0_attn_A_log"]).reshape(NH, 1)),
        "dvec": np.ascontiguousarray(np.asarray(inputs["L0_attn_D"])),
        "nw": np.ascontiguousarray(
            np.asarray(inputs["L0_attn_norm_weight"]).reshape(32, 128).T),
        "gam": np.ascontiguousarray(np.asarray(inputs["L0_ffn_ln_gamma"])),
        "bet": np.ascontiguousarray(np.asarray(inputs["L0_ffn_ln_beta"])),
    }
    in_maps = []
    for b in range(B):
        m = dict(shared)
        m["x"] = np.ascontiguousarray(x[b])
        m["xT"] = np.ascontiguousarray(x[b].T)
        in_maps.append(m)
    return in_maps


def kernel(**inputs):
    in_maps = make_in_maps(inputs)
    nc = _get_nc()
    res = run_bass_kernel_spmd(nc, in_maps, core_ids=list(range(len(in_maps))))
    return np.stack([np.asarray(r["out"], dtype=np.float32) for r in res.results])
